# version 23
# baseline (speedup 1.0000x reference)
import numpy as np
import concourse.bass as bass
import concourse.mybir as mybir
from concourse.bass_utils import run_bass_kernel_spmd

F32 = mybir.dt.float32
BF16 = mybir.dt.bfloat16
F32R = mybir.dt.float32r
AF = mybir.ActivationFunctionType
ALU = mybir.AluOpType
AX = mybir.AxisListType

P = 128
CHK = 8
D = 1024
T = 2048
NB = 16
NCORES = 8
SEQ = 2
TOK = SEQ * T
TILES = 4
TILE = 1024
SUBS = 4
SUB = 256
HN = 512
QW = 256
E = 8
NEXP = 10
EPS = 1e-5
INVT = 1.0 / T

DEBUG_TAPS = []
LAST_DEBUG = {}

ENGS = ("sp", "gp", "gpc", "pe", "act", "dve")
DMA_SEMS = ("sp", "gp")


class Sched:
    def __init__(self):
        self.ops = {e: [] for e in ENGS}
        self._pending = {}
        self._seq = 0

    UNSAFE_METHODS = ("tensor_tensor_scan", "scalar_tensor_tensor")

    SAME_SKIP = ("pe",)

    def want(self, sem, token, force=False):
        if token is None:
            return
        tsem, tidx = token
        if (tsem == sem and tsem not in DMA_SEMS and not force
                and tsem in self.SAME_SKIP):
            prod = self.ops[tsem][tidx]["m"]
            if prod not in self.UNSAFE_METHODS:
                return
        self._pending.setdefault(sem, []).append((tsem, tidx))

    def emit(self, sem, method, args, kwargs):
        w = self._pending.pop(sem, [])
        idx = len(self.ops[sem])
        self._seq += 1
        self.ops[sem].append(
            {"m": method, "a": args, "k": kwargs, "flag": False, "waits": w,
             "seq": self._seq})
        for (ts, ti) in w:
            self.ops[ts][ti]["flag"] = True
        return (sem, idx)

    def finalize(self):
        self.ms = {}
        for s in ENGS:
            c = 0
            nums = []
            for o in self.ops[s]:
                if o["flag"]:
                    c += 1
                nums.append(c)
            self.ms[s] = nums

    def replay(self, sems, semlist, engine):
        if isinstance(semlist, str):
            semlist = (semlist,)
        stream = []
        for sm in semlist:
            for o in self.ops[sm]:
                stream.append((o["seq"], sm, o))
        stream.sort()
        lastw = {}
        for _, sm, o in stream:
            for (ts, ti) in o["waits"]:
                v = self.ms[ts][ti]
                step = 16 if ts in DMA_SEMS else 1
                if lastw.get(ts, 0) >= v:
                    continue
                lastw[ts] = v
                engine.wait_ge(sems[ts], v * step)
            instr = getattr(engine, o["m"])(*o["a"], **o["k"])
            if o["flag"]:
                step = 16 if sm in DMA_SEMS else 1
                instr.then_inc(sems[sm], step)


class Buf:
    __slots__ = ("name", "last_w", "readers", "frozen")

    def __init__(self, name, frozen=False):
        self.name = name
        self.last_w = None
        self.readers = []
        self.frozen = frozen


def build_program():
    nc = bass.Bass("TRN2", target_bir_lowering=False)
    S = Sched()
    ctx = []

    def alloc(name, shape, dt):
        t = nc.sbuf_tensor(name, shape, dt)
        h = t.__enter__()
        ctx.append(t)
        return h

    def palloc(name, shape, dt):
        t = nc.psum_tensor(name, shape, dt)
        h = t.__enter__()
        ctx.append(t)
        return h

    def op(sem, method, *args, reads=(), writes=(), **kwargs):
        force = method in Sched.UNSAFE_METHODS
        for b in reads:
            S.want(sem, b.last_w, force)
        for b in writes:
            S.want(sem, b.last_w, force)
            for t in b.readers:
                S.want(sem, t, force)
        tok = S.emit(sem, method, args, kwargs)
        for b in reads:
            if not b.frozen:
                b.readers.append(tok)
        for b in writes:
            b.last_w = tok
            b.readers = []
        return tok

    def din(name, shape, dt=F32):
        return nc.dram_tensor(name, shape, dt, kind="ExternalInput").ap()

    xin = din("xin", [D, TOK])
    out = nc.dram_tensor("out", [D, TOK], F32, kind="ExternalOutput").ap()
    wk_h = din("wk", [D, D]); wv_h = din("wv", [D, D])
    wr_h = din("wr", [D, D]); wo_h = din("wo", [D, D])
    w1_h = din("w1", [E, D, D]); w3_h = din("w3", [E, D, D]); w2_h = din("w2", [E, D, D])
    ws1_h = din("ws1", [D, 2 * D]); ws3_h = din("ws3", [D, 2 * D]); ws2_h = din("ws2", [2 * D, D])
    gwt_h = din("gwt", [D, E])
    mixk_h = din("mixk", [P, CHK]); mixv_h = din("mixv", [P, CHK]); mixr_h = din("mixr", [P, CHK])
    sd_h = din("sd", [P, CHK]); sf_h = din("sf", [P, CHK])
    b1_h = din("b1f", [P, E * CHK]); b3_h = din("b3f", [P, E * CHK])
    bs1_h = din("bs1f", [P, 2 * CHK]); bs3_h = din("bs3f", [P, 2 * CHK])
    b2_h = din("b2", [E, D]); bs2_h = din("bs2", [1, D]); gb_h = din("gb", [1, E])
    ident_h = din("ident", [P, P]); oh_h = din("oh", [E, E * P])
    onesr_h = din("onesr", [1, HN]); onesc_h = din("onesc", [P, 1])
    misc_h = din("misc", [P, 2])

    dbg_out = {}
    for tag in DEBUG_TAPS:
        w = CHK * (SUB + 1) if tag.startswith("s_") else CHK * TILE
        dbg_out[tag] = nc.dram_tensor("dbg_" + tag, [P, w], F32,
                                      kind="ExternalOutput").ap()

    xb_t = alloc("xb", [P, CHK, TILE + 1], F32R)
    ly_t = alloc("ly", [P, CHK, TILE + 2], F32R)
    xt_t = alloc("xt", [P, CHK, D], BF16)
    wb_t = [alloc(f"wb{i}", [P, CHK, D], BF16) for i in range(3)]
    AR = 10256
    ar_t = alloc("arena", [P, AR], F32)
    fz1_t = alloc("fz1", [P, 2048], F32R)
    fz2_t = alloc("fz2", [P, 2048], F32R)
    rows_t = alloc("rows", [1, 1024], F32)
    gfm_t = alloc("gfm", [E, D], F32R)
    onesc_t = alloc("onescs", [P, 1], F32R)
    onesr_t = alloc("onesrs", [1, HN], F32)
    ident_t = alloc("idents", [P, P], F32)
    mixk_t = alloc("mixks", [P, CHK], F32)
    mixv_t = alloc("mixvs", [P, CHK], F32)
    mixr_t = alloc("mixrs", [P, CHK], F32)
    sd_t = alloc("sds", [P, CHK], F32)
    sf_t = alloc("sfs", [P, CHK], F32)
    ew_t = alloc("ews", [P, CHK], F32)
    uT_t = alloc("uTs", [P, CHK], F32)
    misc_t = alloc("miscs", [P, 2], F32)
    b1_t = alloc("b1s", [P, E * CHK], F32)
    b3_t = alloc("b3s", [P, E * CHK], F32)
    bs1_t = alloc("bs1s", [P, 2 * CHK], F32)
    bs3_t = alloc("bs3s", [P, 2 * CHK], F32)
    b2_t = alloc("b2s", [E, D], F32R)
    bs2_t = alloc("bs2s", [1, D], F32)
    gbB_t = alloc("gbBs", [P, E], F32)
    gbrow_t = alloc("gbrows", [1, E], F32)
    gwt_t = alloc("gwts", [P, CHK, E], BF16)
    oh_t = alloc("ohs", [E, E * P], F32R)
    acar_t = alloc("acars", [P, CHK], F32)
    bcar_t = alloc("bcars", [P, CHK], F32)
    ltail_t = alloc("ltails", [P, CHK], F32)
    pp = [palloc(f"pp{i}", [P, HN], F32) for i in range(8)]

    xb = Buf("xb"); ly = Buf("ly"); xt = Buf("xt")
    wb = [Buf(f"wb{i}") for i in range(3)]
    ppb = [Buf(f"pp{i}") for i in range(8)]
    rows_b = Buf("rows"); gfm_b = Buf("gfm")
    acar = Buf("acar"); bcar = Buf("bcar"); ltail = Buf("ltail")
    gbB_b = Buf("gbB")
    consts = Buf("consts", frozen=True)
    ewu = Buf("ewu")
    aB1 = Buf("aB1"); aB2 = Buf("aB2")
    aA = Buf("aA"); aBs = Buf("aBs"); aXK = Buf("aXK")
    aFZ1 = Buf("aFZ1"); aFZ2 = Buf("aFZ2")
    aSR = Buf("aSR")
    aGH = Buf("aGH"); aS1 = Buf("aS1"); aGP3 = Buf("aGP3"); aGE = Buf("aGE")
    aGL = Buf("aGL"); aBC2 = Buf("aBC2"); aT1 = Buf("aT1")
    aBC = aB2
    aXK2 = aXK
    PHASE_A = [aB1, aB2, aA, aBs, aXK, aSR]
    PHASE_B = [aGH, aS1, aGP3, aGE, aGL, aBC2, aT1]

    O_B1, O_B2 = 0, 2048
    O_A, O_Bs = 4096, 6152
    O_XK = 8208
    O_SR = 9232
    O_BC = 2048
    O_GH, O_S1, O_GP3, O_GE = 0, 2048, 2560, 3072
    O_GL = 4096
    O_BC2 = 4608
    O_T1 = 0

    def arena_switch(new, old):
        toks = []
        for b in old:
            if b.last_w is not None:
                toks.append(b.last_w)
            toks.extend(b.readers)
            b.last_w = None
            b.readers = []
        for b in new:
            b.last_w = None
            b.readers = list(toks)

    def arf(off, chk, w):
        return ar_t[:, off:off + chk * w].rearrange("p (c s) -> p c s", s=w)

    def arr(off, chk, w):
        return ar_t[:, off:off + chk * w].bitcast(F32R).rearrange(
            "p (c s) -> p c s", s=w)

    def arbf(off, chk, w):
        return ar_t[:, off:off + chk * w // 2].bitcast(BF16).rearrange(
            "p (c s) -> p c s", s=w)

    def arf2(off, w):
        return ar_t[:, off:off + w]

    def dma(q, out_ap, in_ap, reads=(), writes=()):
        return op(q, "dma_start", reads=reads, writes=writes,
                  out=out_ap, in_=in_ap)

    def stap(name, src3_f32, srcbuf, w):
        tag = "s_" + name
        if tag in dbg_out:
            dma("sp", dbg_out[tag][:, 0:CHK * w].rearrange(
                "p (c t) -> p c t", t=w), src3_f32, reads=(srcbuf,))

    def tap(tag, src3_f32, srcbuf):
        if tag in dbg_out:
            dma("sp", dbg_out[tag].rearrange("p (c t) -> p c t", t=TILE),
                src3_f32, reads=(srcbuf,))

    dma("sp", ident_t[:], ident_h, writes=(consts,))
    dma("sp", mixk_t[:], mixk_h, writes=(consts,))
    dma("sp", mixv_t[:], mixv_h, writes=(consts,))
    dma("sp", mixr_t[:], mixr_h, writes=(consts,))
    dma("sp", sd_t[:], sd_h, writes=(consts,))
    dma("sp", sf_t[:], sf_h, writes=(consts,))
    dma("sp", misc_t[:], misc_h, writes=(consts,))
    dma("sp", b1_t[:], b1_h, writes=(consts,))
    dma("sp", b3_t[:], b3_h, writes=(consts,))
    dma("sp", bs1_t[:], bs1_h, writes=(consts,))
    dma("sp", bs3_t[:], bs3_h, writes=(consts,))
    dma("gp", onesc_t[:], onesc_h, writes=(consts,))
    dma("sp", onesr_t[:], onesr_h, writes=(consts,))
    dma("gp", b2_t[:], b2_h, writes=(consts,))
    dma("sp", bs2_t[:], bs2_h, writes=(consts,))
    dma("sp", gbrow_t[:], gb_h, writes=(consts,))
    dma("gp", gwt_t[:], gwt_h.rearrange("(c p) e -> p c e", p=P), writes=(consts,))
    dma("gp", oh_t[:], oh_h, writes=(consts,))

    op("act", "activation", ew_t[:], sd_t[:], AF.Exp, scale=INVT,
       reads=(consts,), writes=(ewu,))
    op("dve", "tensor_scalar", uT_t[:], sf_t[:], INVT, None, ALU.mult,
       reads=(consts,), writes=(ewu,))
    op("pe", "matmul", pp[3][0:P, 0:E], onesr_t[0:1, 0:P], gbrow_t[:],
       start=True, stop=True, reads=(consts,), writes=(ppb[3],))
    op("dve", "tensor_copy", gbB_t[:], pp[3][0:P, 0:E],
       reads=(ppb[3],), writes=(gbB_b,))

    xin3 = xin.rearrange("(c p) t -> p c t", p=P)
    out3 = out.rearrange("(c p) t -> p c t", p=P)

    def chunked_w(h2d):
        return h2d.rearrange("(c p) n -> p c n", p=P)

    r_m = rows_t[0:1, 0:QW]
    r_q = rows_t[0:1, QW:2 * QW]
    r_s = rows_t[0:1, 2 * QW:3 * QW]
    rr_rstd = rows_t[0:1, 3 * QW:4 * QW]
    rr_mrs = r_s

    def ln_stats(quarters, dat_bufs, norm):
        def banks(qi):
            return ((pp[2], 2, pp[3], 3) if qi % 2 == 0
                    else (pp[6], 6, pp[7], 7))

        def stats(qi):
            chunks, f3d, wq, c0 = quarters[qi]
            ppS, ppSi, ppB_, ppBi = banks(qi)
            sq3 = fz1_t[:, 0:CHK * wq].rearrange("p (c s) -> p c s", s=wq)
            op("dve", "tensor_tensor", sq3, f3d, f3d, ALU.mult,
               reads=tuple(dat_bufs), writes=(aFZ1,))
            for ci in range(CHK):
                op("pe", "matmul", ppS[0:1, 0:wq], onesc_t[:], chunks[ci],
                   start=(ci == 0), stop=(ci == CHK - 1),
                   reads=tuple(dat_bufs) + (consts,), writes=(ppb[ppSi],))
            for ci in range(CHK):
                op("pe", "matmul", ppB_[0:1, 0:wq], onesc_t[:],
                   sq3[:, ci, :], start=(ci == 0), stop=(ci == CHK - 1),
                   reads=(aFZ1, consts), writes=(ppb[ppBi],))

        def tail(qi):
            chunks, f3d, wq, c0 = quarters[qi]
            ppS, ppSi, ppB_, ppBi = banks(qi)
            op("dve", "tensor_scalar", r_m[:, 0:wq], ppS[0:1, 0:wq],
               1.0 / D, None, ALU.mult, reads=(ppb[ppSi],), writes=(rows_b,))
            op("dve", "tensor_scalar", r_q[:, 0:wq], ppB_[0:1, 0:wq],
               1.0 / D, None, ALU.mult, reads=(ppb[ppBi],), writes=(rows_b,))
            op("dve", "tensor_tensor", r_s[:, 0:wq], r_m[:, 0:wq], r_m[:, 0:wq],
               ALU.mult, reads=(rows_b,), writes=(rows_b,))
            op("dve", "tensor_tensor", r_s[:, 0:wq], r_q[:, 0:wq], r_s[:, 0:wq],
               ALU.subtract, reads=(rows_b,), writes=(rows_b,))
            op("act", "activation", r_s[:, 0:wq], r_s[:, 0:wq], AF.Ln,
               bias=misc_t[0:1, 1:2], reads=(rows_b, consts), writes=(rows_b,))
            op("act", "activation", rr_rstd[:, 0:wq], r_s[:, 0:wq], AF.Exp,
               scale=-0.5, reads=(rows_b,), writes=(rows_b,))
            op("dve", "tensor_tensor", rr_mrs[:, 0:wq], r_m[:, 0:wq],
               rr_rstd[:, 0:wq], ALU.mult, reads=(rows_b,), writes=(rows_b,))
            op("pe", "matmul", ppS[:, 0:wq], onesr_t[0:1, 0:P],
               rr_rstd[:, 0:wq], start=True, stop=True,
               reads=(rows_b, consts), writes=(ppb[ppSi],))
            op("pe", "matmul", ppB_[:, 0:wq], onesr_t[0:1, 0:P],
               rr_mrs[:, 0:wq], start=True, stop=True,
               reads=(rows_b, consts), writes=(ppb[ppBi],))
            norm(qi, c0, wq, ppS[:, 0:wq], ppB_[:, 0:wq],
                 (ppb[ppSi], ppb[ppBi]))

        for qi in range(len(quarters)):
            stats(qi)
            tail(qi)

    def half_quarters(src_t, base):
        qs = []
        for q in range(2):
            c0 = base + q * QW
            qs.append(([src_t[:, ci, c0:c0 + QW] for ci in range(CHK)],
                       src_t[:, :, c0:c0 + QW].bitcast(F32), QW, q * QW))
        return qs

    out_toks = []
    for tile in range(TILES):
        t0 = tile * TILE
        seq_start = (tile % 2 == 0)
        seq_end = (tile % 2 == 1)

        ncols = TILE + (0 if seq_end else 1)
        dma("gp", xb_t[:, :, 0:ncols], xin3[:, :, t0:t0 + ncols], writes=(xb,))

        arena_switch(PHASE_A, PHASE_B)

        for nh in range(2):
            w0 = nh * HN

            def norm1(qi, c0, wq, rB, mB, rbufs, w0=w0):
                rB3 = rB.unsqueeze(1).broadcast_to([P, CHK, wq])
                mB3 = mB.unsqueeze(1).broadcast_to([P, CHK, wq])
                dst = ly_t[:, :, 1 + w0 + c0:1 + w0 + c0 + wq]
                src = xb_t[:, :, w0 + c0:w0 + c0 + wq].bitcast(F32)
                op("dve", "tensor_tensor", dst, src, rB3, ALU.mult,
                   reads=(xb,) + rbufs[:1], writes=(ly,))
                op("dve", "tensor_tensor", dst, dst.bitcast(F32), mB3,
                   ALU.subtract, reads=rbufs[1:], writes=(ly,))

            ln_stats(half_quarters(xb_t, w0), (xb,), norm1)

        if seq_start:
            op("dve", "tensor_scalar", ly_t[:, :, 0:1], mixk_t[:].unsqueeze(2),
               0.0, None, ALU.mult, reads=(consts,), writes=(ly,))
        else:
            op("dve", "tensor_copy", ly_t[:, :, 0:1], ltail_t[:].unsqueeze(2),
               reads=(ltail,), writes=(ly,))
        op("dve", "tensor_copy", ltail_t[:].unsqueeze(2),
           ly_t[:, :, TILE:TILE + 1].bitcast(F32), reads=(ly,), writes=(ltail,))

        if seq_end:
            op("dve", "tensor_scalar", ly_t[:, :, TILE + 1:TILE + 2],
               mixk_t[:].unsqueeze(2), 0.0, None, ALU.mult,
               reads=(consts,), writes=(ly,))
        else:
            xcol_f = xb_t[:, :, TILE:TILE + 1].bitcast(F32).rearrange(
                "p c o -> p (c o)")
            sqh = fz1_t[:, 0:CHK]
            op("dve", "tensor_tensor", sqh, xcol_f, xcol_f, ALU.mult,
               reads=(xb,), writes=(aFZ1,))
            for ci in range(CHK):
                op("pe", "matmul", pp[2][0:1, 0:1], onesc_t[:].bitcast(F32),
                   xb_t[:, ci, TILE:TILE + 1].bitcast(F32),
                   start=(ci == 0), stop=(ci == CHK - 1),
                   reads=(xb, consts), writes=(ppb[2],))
            for ci in range(CHK):
                op("pe", "matmul", pp[3][0:1, 0:1], onesc_t[:].bitcast(F32),
                   sqh[:, ci:ci + 1].bitcast(F32),
                   start=(ci == 0), stop=(ci == CHK - 1),
                   reads=(aFZ1, consts), writes=(ppb[3],))
            op("dve", "tensor_scalar", r_m[:, 0:1], pp[2][0:1, 0:1], 1.0 / D,
               None, ALU.mult, reads=(ppb[2],), writes=(rows_b,))
            op("dve", "tensor_scalar", r_q[:, 0:1], pp[3][0:1, 0:1], 1.0 / D,
               None, ALU.mult, reads=(ppb[3],), writes=(rows_b,))
            op("dve", "tensor_tensor", r_s[:, 0:1], r_m[:, 0:1], r_m[:, 0:1],
               ALU.mult, reads=(rows_b,), writes=(rows_b,))
            op("dve", "tensor_tensor", r_s[:, 0:1], r_q[:, 0:1], r_s[:, 0:1],
               ALU.subtract, reads=(rows_b,), writes=(rows_b,))
            op("act", "activation", r_s[:, 0:1], r_s[:, 0:1], AF.Ln,
               bias=misc_t[0:1, 1:2], reads=(rows_b, consts), writes=(rows_b,))
            op("act", "activation", rr_rstd[:, 0:1], r_s[:, 0:1], AF.Exp,
               scale=-0.5, reads=(rows_b,), writes=(rows_b,))
            op("dve", "tensor_copy", rr_mrs[:, 0:1], r_m[:, 0:1],
               reads=(rows_b,), writes=(rows_b,))
            op("pe", "matmul", pp[3][:, 0:1], onesr_t[0:1, 0:P],
               rr_mrs[:, 0:1], start=True, stop=True,
               reads=(rows_b, consts), writes=(ppb[3],))
            op("pe", "matmul", pp[3][:, 1:2], onesr_t[0:1, 0:P],
               rr_rstd[:, 0:1], start=True, stop=True,
               reads=(rows_b, consts), writes=(ppb[3],))
            mr_sb = ar_t[:, O_BC:O_BC + 2]
            op("dve", "tensor_copy", mr_sb, pp[3][:, 0:2],
               reads=(ppb[3],), writes=(aBC,))
            op("dve", "tensor_scalar",
               ly_t[:, :, TILE + 1:TILE + 2].rearrange("p c o -> p (c o)"),
               xcol_f, mr_sb[:, 0:1], mr_sb[:, 1:2], ALU.subtract, ALU.mult,
               reads=(xb, aBC), writes=(ly,))

        tap(f"lnx{tile}", ly_t[:, :, 1:TILE + 1].bitcast(F32), ly)

        if seq_start:
            op("dve", "tensor_scalar", acar_t[:], mixk_t[:], 0.0, None,
               ALU.mult, reads=(consts,), writes=(acar,))
            op("dve", "tensor_scalar", bcar_t[:], mixk_t[:], 0.0, None,
               ALU.mult, reads=(consts,), writes=(bcar,))

        dma("gp", wb_t[0][:], chunked_w(wk_h), writes=(wb[0],))
        dma("gp", wb_t[1][:], chunked_w(wv_h), writes=(wb[1],))
        dma("gp", wb_t[2][:], chunked_w(wr_h), writes=(wb[2],))
        dma("gp", xt_t[:], chunked_w(wo_h), writes=(xt,))

        B1f = arf(O_B1, CHK, SUB)
        B2f = arf(O_B2, CHK, SUB)
        Af = arf(O_A, CHK, SUB + 1)
        Bsf = arf(O_Bs, CHK, SUB + 1)
        XKb = arbf(O_XK, CHK, SUB)
        SRb = arbf(O_SR, CHK, SUB)
        EKr = fz1_t[:, 0:CHK * SUB].rearrange("p (c s) -> p c s", s=SUB)
        EKf = EKr.bitcast(F32)
        NUr = fz2_t[:, 0:CHK * SUB].rearrange("p (c s) -> p c s", s=SUB)
        NUf = NUr.bitcast(F32)

        def xx_ap(ci, s0):
            if ci < 2:
                return ly_t[:, ci, s0:s0 + SUB].bitcast(F32)
            if ci < 4:
                return ly_t[:, ci, s0 + 2:s0 + 2 + SUB].bitcast(F32)
            return ly_t[:, ci, 1 + s0:1 + s0 + SUB].bitcast(F32)

        def mix_into(dst3, mix_t, dstbuf, s0):
            for ci in range(CHK):
                op("dve", "scalar_tensor_tensor", dst3[:, ci, :],
                   B2f[:, ci, :], mix_t[:, ci:ci + 1], xx_ap(ci, s0),
                   ALU.mult, ALU.add,
                   reads=(aB2, ly, consts), writes=(dstbuf,))

        def front(s0):
            for (a_, b_), off in zip([(0, 2), (2, 4), (4, 8)], [0, 2, 1]):
                op("dve", "tensor_tensor", B2f[:, a_:b_, :],
                   ly_t[:, a_:b_, 1 + s0:1 + s0 + SUB].bitcast(F32),
                   ly_t[:, a_:b_, off + s0:off + s0 + SUB].bitcast(F32),
                   ALU.subtract, reads=(ly,), writes=(aB2,))
            mix_into(XKb, mixk_t, aXK, s0)

        def proj(mov3, movbuf, wtile, wbuf, evac_m):
            for m in range(CHK):
                bank = m % 2
                for kc in range(CHK):
                    op("pe", "matmul", pp[bank][:, 0:SUB],
                       wtile[:, kc, m * P:(m + 1) * P], mov3[:, kc, :],
                       start=(kc == 0), stop=(kc == CHK - 1),
                       reads=(movbuf, wbuf), writes=(ppb[bank],))
                evac_m(m, bank)

        def ev_copy(dst3, dstbuf, f):
            def e(m, bank):
                op("act", "activation", dst3[:, m, :], pp[bank][:, 0:SUB],
                   f, reads=(ppb[bank],), writes=(dstbuf,))
            return e

        def ev_sig1(m, bank):
            op("act", "activation", SRb[:, m, :], pp[bank][:, 0:SUB],
               AF.Exp, scale=-1.0, reads=(ppb[bank],), writes=(aSR,))

        front(0)
        for sub in range(SUBS):
            s0 = sub * SUB

            proj(XKb, aXK, wb_t[0], wb[0], ev_copy(B1f, aB1, AF.Copy))
            mix_into(XKb, mixv_t, aXK, s0)
            proj(XKb, aXK, wb_t[1], wb[1], ev_copy(NUr, aFZ2, AF.Copy))
            mix_into(XKb, mixr_t, aXK, s0)

            proj(XKb, aXK, wb_t[2], wb[2], ev_sig1)
            op("act", "activation", SRb[:], SRb[:], AF.Ln,
               bias=misc_t[:, 0:1], reads=(consts,), writes=(aSR,))
            op("act", "activation", SRb[:], SRb[:], AF.Exp, scale=-1.0,
               writes=(aSR,))

            op("act", "activation", Bsf[:, :, 1:SUB + 1], B1f[:], AF.Exp,
               reads=(aB1,), writes=(aBs,))
            for ci in range(CHK):
                op("act", "activation", B1f[:, ci, :], B1f[:, ci, :], AF.Exp,
                   bias=uT_t[:, ci:ci + 1], reads=(ewu,), writes=(aB1,))
            op("gpc", "tensor_copy", Bsf[:, :, 0:1], bcar_t[:].unsqueeze(2),
               reads=(bcar,), writes=(aBs,))
            for ci in range(CHK):
                op("dve", "tensor_tensor_scan", Af[:, ci, 0:SUB + 1],
                   ew_t[:, ci:ci + 1].broadcast_to([P, SUB + 1]),
                   Bsf[:, ci, 0:SUB + 1], 0.0, ALU.mult, ALU.add,
                   reads=(ewu, aBs), writes=(aA,))
            op("gpc", "tensor_copy", bcar_t[:].unsqueeze(2),
               Af[:, :, SUB:SUB + 1], reads=(aA,), writes=(bcar,))
            op("dve", "tensor_tensor", EKr[:], Bsf[:, :, 1:SUB + 1], NUf[:],
               ALU.mult, reads=(aBs, aFZ2), writes=(aFZ1,))
            op("gpc", "tensor_tensor", NUr[:], B1f[:], NUf[:], ALU.mult,
               reads=(aB1, aFZ2), writes=(aFZ2,))
            op("gpc", "tensor_tensor", B1f[:], B1f[:], Af[:, :, 0:SUB],
               ALU.add, reads=(aA, aB1), writes=(aB1,))
            op("act", "activation", B1f[:], B1f[:], AF.Ln, writes=(aB1,))
            op("act", "activation", B1f[:], B1f[:], AF.Exp, scale=-1.0,
               writes=(aB1,))
            op("dve", "tensor_copy", Af[:, :, 0:1], acar_t[:].unsqueeze(2),
               reads=(acar,), writes=(aA,))
            for ci in range(CHK):
                op("dve", "tensor_tensor_scan", Af[:, ci, 1:SUB + 1],
                   ew_t[:, ci:ci + 1].broadcast_to([P, SUB]), EKf[:, ci, :],
                   acar_t[:, ci:ci + 1], ALU.mult, ALU.add,
                   reads=(ewu, aFZ1, acar), writes=(aA,))
            op("dve", "tensor_copy", acar_t[:].unsqueeze(2),
               Af[:, :, SUB:SUB + 1], reads=(aA,), writes=(acar,))
            op("dve", "tensor_tensor", NUr[:], NUf[:], Af[:, :, 0:SUB],
               ALU.add, reads=(aA,), writes=(aFZ2,))
            op("dve", "tensor_tensor", NUr[:], NUf[:], B1f[:], ALU.mult,
               reads=(aB1,), writes=(aFZ2,))
            op("dve", "tensor_tensor", XKb[:], NUf[:], SRb[:], ALU.mult,
               reads=(aSR, aFZ2), writes=(aXK,))

            proj(XKb, aXK, xt_t, xt, ev_copy(NUr, aFZ2, AF.Copy))

            if sub + 1 < SUBS:
                front(s0 + SUB)

            def norm2(qi, c0, wq, rB, mB, rbufs, s0=s0):
                rB3 = rB.unsqueeze(1).broadcast_to([P, CHK, wq])
                mB3 = mB.unsqueeze(1).broadcast_to([P, CHK, wq])
                op("dve", "tensor_tensor", B1f[:], NUf[:], rB3, ALU.mult,
                   reads=(aFZ2,) + rbufs[:1], writes=(aB1,))
                op("dve", "tensor_tensor", B1f[:], B1f[:], mB3, ALU.subtract,
                   reads=rbufs[1:], writes=(aB1,))
                xs = xb_t[:, :, s0:s0 + SUB]
                op("dve", "tensor_tensor", xs, xs.bitcast(F32), B1f[:],
                   ALU.add, reads=(aB1,), writes=(xb,))

            ln_stats([([NUr[:, ci, :] for ci in range(CHK)],
                       NUf[:], SUB, 0)], (aFZ2,), norm2)

        arena_switch(PHASE_B, PHASE_A)

        GH = arbf(O_GH, CHK, HN)
        S1a = arf2(O_S1, HN)
        GP3 = arf2(O_GP3, HN)
        GE = arf2(O_GE, D)
        GLl = arf(O_GL, CHK, E)
        GLsc = arf2(O_GL + 64, 256)
        T1 = arf(O_T1, CHK, HN)
        yv = ly_t[:, :, 0:TILE].bitcast(F32)
        yv3 = ly_t[:, :, 0:TILE]

        for nh in range(2):
            w0 = nh * HN

            def norm3(qi, c0, wq, rB, mB, rbufs, w0=w0):
                rB3 = rB.unsqueeze(1).broadcast_to([P, CHK, wq])
                mB3 = mB.unsqueeze(1).broadcast_to([P, CHK, wq])
                t1q = T1[:, :, c0:c0 + wq]
                op("dve", "tensor_tensor", t1q,
                   xb_t[:, :, w0 + c0:w0 + c0 + wq].bitcast(F32), rB3,
                   ALU.mult, reads=(xb,) + rbufs[:1],
                   writes=(aT1, aGH, aS1, aGP3, aGE))
                op("dve", "tensor_tensor", xt_t[:, :, w0 + c0:w0 + c0 + wq],
                   t1q, mB3, ALU.subtract, reads=(aT1,) + rbufs[1:],
                   writes=(xt,))

            ln_stats(half_quarters(xb_t, w0), (xb,), norm3)

        for tc in range(CHK):
            for kc in range(CHK):
                op("pe", "matmul", pp[2][:, tc * E:(tc + 1) * E],
                   xt_t[:, kc, tc * P:(tc + 1) * P], gwt_t[:, kc, :],
                   start=(kc == 0), stop=(kc == CHK - 1),
                   reads=(xt, consts), writes=(ppb[2],))
        op("dve", "tensor_tensor", GLl[:],
           pp[2][:, 0:64].rearrange("p (c e) -> p c e", e=E),
           gbB_t[:].unsqueeze(1).broadcast_to([P, CHK, E]), ALU.add,
           reads=(ppb[2], gbB_b), writes=(aGL,))
        gm1 = GLsc[:, 0:64].rearrange("p (c e) -> p c e", e=E)
        gtmp = GLsc[:, 64:128].rearrange("p (c e) -> p c e", e=E)
        gmax = GLsc[:, 128:136].rearrange("p (c e) -> p c e", e=1)
        gsum = GLsc[:, 136:144].rearrange("p (c e) -> p c e", e=1)
        gsum2 = GLsc[:, 136:144]
        op("dve", "tensor_reduce", gmax, GLl[:], AX.X, ALU.max,
           reads=(aGL,), writes=(aGL,))
        op("dve", "tensor_tensor", GLl[:], GLl[:],
           gmax.broadcast_to([P, CHK, E]), ALU.subtract, writes=(aGL,))
        op("act", "activation", arf2(O_GL, 64), arf2(O_GL, 64), AF.Exp,
           reads=(aGL,), writes=(aGL,))
        op("dve", "tensor_reduce", gsum, GLl[:], AX.X, ALU.add, writes=(aGL,))
        op("dve", "tensor_reduce", gmax, GLl[:], AX.X, ALU.max, writes=(aGL,))
        op("dve", "tensor_tensor", gm1[:], GLl[:],
           gmax.broadcast_to([P, CHK, E]), ALU.is_ge, writes=(aGL,))
        op("dve", "tensor_scalar", gm1[:], gm1[:], 1e30, None, ALU.mult,
           writes=(aGL,))
        op("dve", "tensor_tensor", gtmp[:], GLl[:], gm1[:], ALU.subtract,
           writes=(aGL,))
        op("dve", "tensor_reduce", gmax, gtmp[:], AX.X, ALU.max, writes=(aGL,))
        op("dve", "tensor_tensor", gm1[:], GLl[:],
           gmax.broadcast_to([P, CHK, E]), ALU.is_ge, writes=(aGL,))
        op("dve", "tensor_tensor", gm1[:], gm1[:], GLl[:], ALU.mult,
           writes=(aGL,))
        op("act", "activation", gsum2, gsum2, AF.Ln, writes=(aGL,))
        op("act", "activation", gsum2, gsum2, AF.Exp, scale=-1.0, writes=(aGL,))
        op("dve", "tensor_tensor", gm1[:], gm1[:],
           gsum.broadcast_to([P, CHK, E]), ALU.mult, writes=(aGL,))

        for tc in range(CHK):
            bank = 4 + (tc // 4)
            op("pe", "transpose", pp[bank][0:E, (tc % 4) * P:(tc % 4) * P + P],
               gm1[:, tc, :], ident_t[:],
               reads=(aGL, consts), writes=(ppb[bank],))
        for half in range(2):
            op("dve", "tensor_copy", gfm_t[:, half * HN:(half + 1) * HN],
               pp[4 + half][0:E, :], reads=(ppb[4 + half],), writes=(gfm_b,))

        for ex in range(NEXP):
            shared = ex >= E
            h = ex - E
            if shared:
                w1src = chunked_w(ws1_h[:, h * D:(h + 1) * D])
                w3src = chunked_w(ws3_h[:, h * D:(h + 1) * D])
                w2src = chunked_w(ws2_h[h * D:(h + 1) * D, :])
            else:
                w1src = chunked_w(w1_h[ex])
                w3src = chunked_w(w3_h[ex])
                w2src = chunked_w(w2_h[ex])
            dma("gp", wb_t[0][:], w1src, writes=(wb[0],))
            dma("gp", wb_t[1][:], w3src, writes=(wb[1],))
            dma("gp", wb_t[2][:], w2src, writes=(wb[2],))

            if not shared:
                for nh in range(2):
                    op("pe", "matmul", pp[4 + nh][:, 0:HN],
                       oh_t[:, ex * P:(ex + 1) * P],
                       gfm_t[:, nh * HN:(nh + 1) * HN], start=True, stop=True,
                       reads=(gfm_b, consts), writes=(ppb[4 + nh],))
                    op("dve", "tensor_copy", GE[:, nh * HN:(nh + 1) * HN],
                       pp[4 + nh][:, 0:HN], reads=(ppb[4 + nh],), writes=(aGE,))

            for nh in range(2):
                w0 = nh * HN
                for m in range(CHK):
                    bA = 4 + (m % 2)
                    bB_ = 6 + (m % 2)
                    for kc in range(CHK):
                        op("pe", "matmul", pp[bA][:, 0:HN],
                           wb_t[0][:, kc, m * P:(m + 1) * P],
                           xt_t[:, kc, w0:w0 + HN],
                           start=(kc == 0), stop=(kc == CHK - 1),
                           reads=(wb[0], xt), writes=(ppb[bA],))
                    for kc in range(CHK):
                        op("pe", "matmul", pp[bB_][:, 0:HN],
                           wb_t[1][:, kc, m * P:(m + 1) * P],
                           xt_t[:, kc, w0:w0 + HN],
                           start=(kc == 0), stop=(kc == CHK - 1),
                           reads=(wb[1], xt), writes=(ppb[bB_],))
                    if shared:
                        b1col = bs1_t[:, h * CHK + m:h * CHK + m + 1]
                        b3col = bs3_t[:, h * CHK + m:h * CHK + m + 1]
                    else:
                        b1col = b1_t[:, ex * CHK + m:ex * CHK + m + 1]
                        b3col = b3_t[:, ex * CHK + m:ex * CHK + m + 1]
                    op("act", "activation", S1a[:], pp[bA][:, 0:HN], AF.Silu,
                       bias=b1col, reads=(ppb[bA], consts), writes=(aS1,))
                    if shared:
                        op("dve", "tensor_scalar", GP3[:], pp[bB_][:, 0:HN],
                           b3col, None, ALU.add,
                           reads=(ppb[bB_], consts), writes=(aGP3,))
                    else:
                        op("dve", "scalar_tensor_tensor", GP3[:],
                           pp[bB_][:, 0:HN], b3col, GE[:, w0:w0 + HN],
                           ALU.add, ALU.mult,
                           reads=(ppb[bB_], consts, aGE), writes=(aGP3,))
                    op("gpc", "tensor_tensor", GH[:, m, :], S1a[:], GP3[:],
                       ALU.mult, reads=(aS1, aGP3), writes=(aGH,))
                for c in range(CHK):
                    bank = c % 2
                    if ex == 0:
                        op("pe", "matmul", pp[bank][:, 0:HN],
                           b2_t[:, c * P:(c + 1) * P],
                           gfm_t[:, w0:w0 + HN], start=True, stop=False,
                           reads=(gfm_b, consts), writes=(ppb[bank],))
                        op("pe", "matmul", pp[bank][:, 0:HN],
                           bs2_t[0:1, c * P:(c + 1) * P], onesr_t[:],
                           start=False, stop=False,
                           reads=(consts,), writes=(ppb[bank],))
                    for kc in range(CHK):
                        op("pe", "matmul", pp[bank][:, 0:HN],
                           wb_t[2][:, kc, c * P:(c + 1) * P], GH[:, kc, :],
                           start=(kc == 0 and ex != 0),
                           stop=(kc == CHK - 1),
                           reads=(wb[2], aGH), writes=(ppb[bank],))
                    if ex == 0:
                        op("act", "activation", yv3[:, c, w0:w0 + HN],
                           pp[bank][:, 0:HN], AF.Copy,
                           reads=(ppb[bank],), writes=(ly,))
                    else:
                        op("dve", "tensor_tensor", yv3[:, c, w0:w0 + HN],
                           yv[:, c, w0:w0 + HN], pp[bank][:, 0:HN],
                           ALU.add, reads=(ppb[bank],), writes=(ly,))

        tap(f"y{tile}", ly_t[:, :, 0:TILE].bitcast(F32), ly)

        for nh in range(2):
            w0 = nh * HN

            def norm4(qi, c0, wq, rB, mB, rbufs, w0=w0):
                rB3 = rB.unsqueeze(1).broadcast_to([P, CHK, wq])
                mB3 = mB.unsqueeze(1).broadcast_to([P, CHK, wq])
                t1q = T1[:, :, c0:c0 + wq]
                op("dve", "tensor_tensor", t1q, yv[:, :, w0 + c0:w0 + c0 + wq],
                   rB3, ALU.mult, reads=(ly,) + rbufs[:1],
                   writes=(aT1, aGH, aS1, aGP3, aGE))
                op("dve", "tensor_tensor", t1q, t1q, mB3, ALU.subtract,
                   reads=rbufs[1:], writes=(aT1,))
                xs = xb_t[:, :, w0 + c0:w0 + c0 + wq]
                op("dve", "tensor_tensor", xs, xs.bitcast(F32), t1q, ALU.add,
                   reads=(aT1,), writes=(xb,))

            ln_stats(half_quarters(ly_t, w0), (ly,), norm4)

        tok = dma("sp", out3[:, :, t0:t0 + TILE],
                  xb_t[:, :, 0:TILE].bitcast(F32), reads=(xb,))
        out_toks.append(tok)

        tap(f"x{tile}", xb_t[:, :, 0:TILE].bitcast(F32), xb)

    for t in out_toks:
        S.want("sp", t)
    S.emit("sp", "nop", (), {})

    S.finalize()

    import contextlib
    with contextlib.ExitStack() as stack:
        sems = {}
        for s in ENGS:
            sems[s] = stack.enter_context(nc.semaphore("sem_" + s))
        block = stack.enter_context(nc.Block())

        @block.sync
        def _(En):
            S.replay(sems, "sp", En)

        @block.gpsimd
        def _(En):
            S.replay(sems, ("gp", "gpc"), En)

        @block.tensor
        def _(En):
            S.replay(sems, "pe", En)

        @block.scalar
        def _(En):
            S.replay(sems, "act", En)

        @block.vector
        def _(En):
            S.replay(sems, "dve", En)

    for c in reversed(ctx):
        c.__exit__(None, None, None)
    return nc


def _onehot_blocks():
    oh = np.zeros((E, E * P), dtype=np.float32)
    for e in range(E):
        oh[e, e * P:(e + 1) * P] = 1.0
    return oh


def _host_prep(inputs):
    gi = lambda k: np.asarray(inputs[k])
    x = gi("L0_attn_x")

    def fm(vec):
        return np.ascontiguousarray(vec.reshape(CHK, P).T)

    com = {
        "wk": np.ascontiguousarray(gi("L0_attn_Wk").T),
        "wv": np.ascontiguousarray(gi("L0_attn_Wv").T),
        "wr": np.ascontiguousarray(gi("L0_attn_Wr").T),
        "wo": np.ascontiguousarray(gi("L0_attn_Wo").T),
        "w1": np.ascontiguousarray(np.transpose(gi("L0_ffn_w1"), (0, 2, 1))),
        "w3": np.ascontiguousarray(np.transpose(gi("L0_ffn_w3"), (0, 2, 1))),
        "w2": np.ascontiguousarray(np.transpose(gi("L0_ffn_w2"), (0, 2, 1))),
        "ws1": np.ascontiguousarray(gi("L0_ffn_ws1").T),
        "ws3": np.ascontiguousarray(gi("L0_ffn_ws3").T),
        "ws2": np.ascontiguousarray(gi("L0_ffn_ws2").T),
        "gwt": np.ascontiguousarray(gi("L0_ffn_gate_w").T),
        "mixk": fm(gi("L0_attn_mix_k")),
        "mixv": fm(gi("L0_attn_mix_v")),
        "mixr": fm(gi("L0_attn_mix_r")),
        "sd": fm(gi("L0_attn_spatial_decay")),
        "sf": fm(gi("L0_attn_spatial_first")),
        "b1f": np.ascontiguousarray(
            gi("L0_ffn_b1").reshape(E, CHK, P).transpose(2, 0, 1).reshape(P, E * CHK)),
        "b3f": np.ascontiguousarray(
            gi("L0_ffn_b3").reshape(E, CHK, P).transpose(2, 0, 1).reshape(P, E * CHK)),
        "bs1f": np.ascontiguousarray(
            gi("L0_ffn_bs1").reshape(2, CHK, P).transpose(2, 0, 1).reshape(P, 2 * CHK)),
        "bs3f": np.ascontiguousarray(
            gi("L0_ffn_bs3").reshape(2, CHK, P).transpose(2, 0, 1).reshape(P, 2 * CHK)),
        "b2": np.ascontiguousarray(gi("L0_ffn_b2")),
        "bs2": np.ascontiguousarray(gi("L0_ffn_bs2").reshape(1, D)),
        "gb": np.ascontiguousarray(gi("L0_ffn_gate_b").reshape(1, E)),
        "ident": np.eye(P, dtype=np.float32),
        "oh": _onehot_blocks(),
        "onesr": np.ones((1, HN), dtype=np.float32),
        "onesc": np.ones((P, 1), dtype=np.float32),
        "misc": np.concatenate([np.ones((P, 1), np.float32),
                                np.full((P, 1), EPS, np.float32)], axis=1),
    }
    for k in com:
        com[k] = np.ascontiguousarray(com[k].astype(np.float32, copy=False))

    in_maps = []
    for c in range(NCORES):
        xc = np.ascontiguousarray(
            np.concatenate([x[2 * c].T, x[2 * c + 1].T], axis=1)).astype(
                np.float32, copy=False)
        m = {"xin": xc}
        m.update(com)
        in_maps.append(m)
    return in_maps


def kernel(**inputs):
    in_maps = _host_prep(inputs)
    nc = build_program()
    res = run_bass_kernel_spmd(nc, in_maps, core_ids=list(range(NCORES)))
    LAST_DEBUG.clear()
    full = np.empty((NB, T, D), dtype=np.float32)
    for c in range(NCORES):
        r = res.results[c]
        for k, v in r.items():
            if k.startswith("dbg_"):
                LAST_DEBUG[(c, k)] = np.asarray(v)
        o = np.asarray(r["out"])
        full[2 * c] = o[:, 0:T].T
        full[2 * c + 1] = o[:, T:2 * T].T
    return full


# version 24
# speedup vs baseline: 1.0401x; 1.0401x over previous
import numpy as np
import concourse.bass as bass
import concourse.mybir as mybir
from concourse.bass_utils import run_bass_kernel_spmd

F32 = mybir.dt.float32
BF16 = mybir.dt.bfloat16
F32R = mybir.dt.float32r
AF = mybir.ActivationFunctionType
ALU = mybir.AluOpType
AX = mybir.AxisListType

P = 128
CHK = 8
D = 1024
T = 2048
NB = 16
NCORES = 8
SEQ = 2
TOK = SEQ * T
TILES = 4
TILE = 1024
SUBS = 4
SUB = 256
HN = 512
QW = 256
E = 8
NEXP = 10
EPS = 1e-5
INVT = 1.0 / T

DEBUG_TAPS = []
LAST_DEBUG = {}

ENGS = ("sp", "gp", "gpc", "pe", "act", "dve")
DMA_SEMS = ("sp", "gp")


class Sched:
    def __init__(self):
        self.ops = {e: [] for e in ENGS}
        self._pending = {}
        self._seq = 0

    UNSAFE_METHODS = ("tensor_tensor_scan", "scalar_tensor_tensor")

    SAME_SKIP = ("pe",)

    def want(self, sem, token, force=False):
        if token is None:
            return
        tsem, tidx = token
        if (tsem == sem and tsem not in DMA_SEMS and not force
                and tsem in self.SAME_SKIP):
            prod = self.ops[tsem][tidx]["m"]
            if prod not in self.UNSAFE_METHODS:
                return
        self._pending.setdefault(sem, []).append((tsem, tidx))

    def emit(self, sem, method, args, kwargs):
        w = self._pending.pop(sem, [])
        idx = len(self.ops[sem])
        self._seq += 1
        self.ops[sem].append(
            {"m": method, "a": args, "k": kwargs, "flag": False, "waits": w,
             "seq": self._seq})
        for (ts, ti) in w:
            self.ops[ts][ti]["flag"] = True
        return (sem, idx)

    def finalize(self):
        self.ms = {}
        for s in ENGS:
            c = 0
            nums = []
            for o in self.ops[s]:
                if o["flag"]:
                    c += 1
                nums.append(c)
            self.ms[s] = nums

    def replay(self, sems, semlist, engine):
        if isinstance(semlist, str):
            semlist = (semlist,)
        stream = []
        for sm in semlist:
            for o in self.ops[sm]:
                stream.append((o["seq"], sm, o))
        stream.sort()
        lastw = {}
        for _, sm, o in stream:
            for (ts, ti) in o["waits"]:
                v = self.ms[ts][ti]
                step = 16 if ts in DMA_SEMS else 1
                if lastw.get(ts, 0) >= v:
                    continue
                lastw[ts] = v
                engine.wait_ge(sems[ts], v * step)
            instr = getattr(engine, o["m"])(*o["a"], **o["k"])
            if o["flag"]:
                step = 16 if sm in DMA_SEMS else 1
                instr.then_inc(sems[sm], step)


class Buf:
    __slots__ = ("name", "last_w", "readers", "frozen")

    def __init__(self, name, frozen=False):
        self.name = name
        self.last_w = None
        self.readers = []
        self.frozen = frozen


def build_program():
    nc = bass.Bass("TRN2", target_bir_lowering=False)
    S = Sched()
    ctx = []

    def alloc(name, shape, dt):
        t = nc.sbuf_tensor(name, shape, dt)
        h = t.__enter__()
        ctx.append(t)
        return h

    def palloc(name, shape, dt):
        t = nc.psum_tensor(name, shape, dt)
        h = t.__enter__()
        ctx.append(t)
        return h

    def op(sem, method, *args, reads=(), writes=(), **kwargs):
        force = method in Sched.UNSAFE_METHODS
        for b in reads:
            S.want(sem, b.last_w, force)
        for b in writes:
            S.want(sem, b.last_w, force)
            for t in b.readers:
                S.want(sem, t, force)
        tok = S.emit(sem, method, args, kwargs)
        for b in reads:
            if not b.frozen:
                b.readers.append(tok)
        for b in writes:
            b.last_w = tok
            b.readers = []
        return tok

    def din(name, shape, dt=F32):
        return nc.dram_tensor(name, shape, dt, kind="ExternalInput").ap()

    xin = din("xin", [D, TOK])
    out = nc.dram_tensor("out", [D, TOK], F32, kind="ExternalOutput").ap()
    wk_h = din("wk", [D, D]); wv_h = din("wv", [D, D])
    wr_h = din("wr", [D, D]); wo_h = din("wo", [D, D])
    w1_h = din("w1", [E, D, D]); w3_h = din("w3", [E, D, D]); w2_h = din("w2", [E, D, D])
    ws1_h = din("ws1", [D, 2 * D]); ws3_h = din("ws3", [D, 2 * D]); ws2_h = din("ws2", [2 * D, D])
    gwt_h = din("gwt", [D, E])
    mixk_h = din("mixk", [P, CHK]); mixv_h = din("mixv", [P, CHK]); mixr_h = din("mixr", [P, CHK])
    sd_h = din("sd", [P, CHK]); sf_h = din("sf", [P, CHK])
    b1_h = din("b1f", [P, E * CHK]); b3_h = din("b3f", [P, E * CHK])
    bs1_h = din("bs1f", [P, 2 * CHK]); bs3_h = din("bs3f", [P, 2 * CHK])
    b2_h = din("b2", [E, D]); bs2_h = din("bs2", [1, D]); gb_h = din("gb", [1, E])
    ident_h = din("ident", [P, P]); oh_h = din("oh", [E, E * P])
    onesr_h = din("onesr", [1, HN]); onesc_h = din("onesc", [P, 1])
    misc_h = din("misc", [P, 2])

    dbg_out = {}
    for tag in DEBUG_TAPS:
        w = CHK * (SUB + 1) if tag.startswith("s_") else CHK * TILE
        dbg_out[tag] = nc.dram_tensor("dbg_" + tag, [P, w], F32,
                                      kind="ExternalOutput").ap()

    xb_t = alloc("xb", [P, CHK, TILE + 1], F32R)
    ly_t = alloc("ly", [P, CHK, TILE + 2], F32R)
    xt_t = alloc("xt", [P, CHK, D], BF16)
    wb_t = [alloc(f"wb{i}", [P, CHK, D], BF16) for i in range(3)]
    AR = 10256
    ar_t = alloc("arena", [P, AR], F32)
    fz1_t = alloc("fz1", [P, 2048], F32R)
    fz2_t = alloc("fz2", [P, 2048], F32R)
    rows_t = alloc("rows", [1, 1024], F32)
    gfm_t = alloc("gfm", [E, D], F32R)
    onesc_t = alloc("onescs", [P, 1], F32R)
    onesr_t = alloc("onesrs", [1, HN], F32)
    ident_t = alloc("idents", [P, P], F32)
    mixk_t = alloc("mixks", [P, CHK], F32)
    mixv_t = alloc("mixvs", [P, CHK], F32)
    mixr_t = alloc("mixrs", [P, CHK], F32)
    sd_t = alloc("sds", [P, CHK], F32)
    sf_t = alloc("sfs", [P, CHK], F32)
    ew_t = alloc("ews", [P, CHK], F32)
    uT_t = alloc("uTs", [P, CHK], F32)
    misc_t = alloc("miscs", [P, 2], F32)
    b1_t = alloc("b1s", [P, E * CHK], F32)
    b3_t = alloc("b3s", [P, E * CHK], F32)
    bs1_t = alloc("bs1s", [P, 2 * CHK], F32)
    bs3_t = alloc("bs3s", [P, 2 * CHK], F32)
    b2_t = alloc("b2s", [E, D], F32R)
    bs2_t = alloc("bs2s", [1, D], F32)
    gbB_t = alloc("gbBs", [P, E], F32)
    gbrow_t = alloc("gbrows", [1, E], F32)
    gwt_t = alloc("gwts", [P, CHK, E], BF16)
    oh_t = alloc("ohs", [E, E * P], F32R)
    acar_t = alloc("acars", [P, CHK], F32)
    bcar_t = alloc("bcars", [P, CHK], F32)
    ltail_t = alloc("ltails", [P, CHK], F32)
    pp = [palloc(f"pp{i}", [P, HN], F32) for i in range(8)]

    xb = Buf("xb"); ly = Buf("ly"); xt = Buf("xt")
    wb = [Buf(f"wb{i}") for i in range(3)]
    ppb = [Buf(f"pp{i}") for i in range(8)]
    rows_b = Buf("rows"); gfm_b = Buf("gfm")
    acar = Buf("acar"); bcar = Buf("bcar"); ltail = Buf("ltail")
    gbB_b = Buf("gbB")
    consts = Buf("consts", frozen=True)
    ewu = Buf("ewu")
    aB1 = Buf("aB1"); aB2 = Buf("aB2")
    aA = Buf("aA"); aBs = Buf("aBs"); aXK = Buf("aXK")
    aFZ1 = Buf("aFZ1"); aFZ2 = Buf("aFZ2")
    aSR = Buf("aSR")
    aGH = Buf("aGH"); aS1 = Buf("aS1"); aGP3 = Buf("aGP3"); aGE = Buf("aGE")
    aGL = Buf("aGL"); aBC2 = Buf("aBC2"); aT1 = Buf("aT1")
    aBC = aB2
    aXK2 = aXK
    PHASE_A = [aB1, aB2, aA, aBs, aXK, aSR]
    PHASE_B = [aGH, aS1, aGP3, aGE, aGL, aBC2, aT1]

    O_B1, O_B2 = 0, 2048
    O_A, O_Bs = 4096, 6152
    O_XK = 8208
    O_SR = 9232
    O_BC = 2048
    O_GH, O_S1, O_GP3, O_GE = 0, 2048, 2560, 3072
    O_GL = 4096
    O_BC2 = 4608
    O_T1 = 0

    def arena_switch(new, old):
        toks = []
        for b in old:
            if b.last_w is not None:
                toks.append(b.last_w)
            toks.extend(b.readers)
            b.last_w = None
            b.readers = []
        for b in new:
            b.last_w = None
            b.readers = list(toks)

    def arf(off, chk, w):
        return ar_t[:, off:off + chk * w].rearrange("p (c s) -> p c s", s=w)

    def arr(off, chk, w):
        return ar_t[:, off:off + chk * w].bitcast(F32R).rearrange(
            "p (c s) -> p c s", s=w)

    def arbf(off, chk, w):
        return ar_t[:, off:off + chk * w // 2].bitcast(BF16).rearrange(
            "p (c s) -> p c s", s=w)

    def arf2(off, w):
        return ar_t[:, off:off + w]

    def dma(q, out_ap, in_ap, reads=(), writes=()):
        return op(q, "dma_start", reads=reads, writes=writes,
                  out=out_ap, in_=in_ap)

    def stap(name, src3_f32, srcbuf, w):
        tag = "s_" + name
        if tag in dbg_out:
            dma("sp", dbg_out[tag][:, 0:CHK * w].rearrange(
                "p (c t) -> p c t", t=w), src3_f32, reads=(srcbuf,))

    def tap(tag, src3_f32, srcbuf):
        if tag in dbg_out:
            dma("sp", dbg_out[tag].rearrange("p (c t) -> p c t", t=TILE),
                src3_f32, reads=(srcbuf,))

    dma("sp", ident_t[:], ident_h, writes=(consts,))
    dma("sp", mixk_t[:], mixk_h, writes=(consts,))
    dma("sp", mixv_t[:], mixv_h, writes=(consts,))
    dma("sp", mixr_t[:], mixr_h, writes=(consts,))
    dma("sp", sd_t[:], sd_h, writes=(consts,))
    dma("sp", sf_t[:], sf_h, writes=(consts,))
    dma("sp", misc_t[:], misc_h, writes=(consts,))
    dma("sp", b1_t[:], b1_h, writes=(consts,))
    dma("sp", b3_t[:], b3_h, writes=(consts,))
    dma("sp", bs1_t[:], bs1_h, writes=(consts,))
    dma("sp", bs3_t[:], bs3_h, writes=(consts,))
    dma("gp", onesc_t[:], onesc_h, writes=(consts,))
    dma("sp", onesr_t[:], onesr_h, writes=(consts,))
    dma("gp", b2_t[:], b2_h, writes=(consts,))
    dma("sp", bs2_t[:], bs2_h, writes=(consts,))
    dma("sp", gbrow_t[:], gb_h, writes=(consts,))
    dma("gp", gwt_t[:], gwt_h.rearrange("(c p) e -> p c e", p=P), writes=(consts,))
    dma("gp", oh_t[:], oh_h, writes=(consts,))

    op("act", "activation", ew_t[:], sd_t[:], AF.Exp, scale=INVT,
       reads=(consts,), writes=(ewu,))
    op("dve", "tensor_scalar", uT_t[:], sf_t[:], INVT, None, ALU.mult,
       reads=(consts,), writes=(ewu,))
    op("pe", "matmul", pp[3][0:P, 0:E], onesr_t[0:1, 0:P], gbrow_t[:],
       start=True, stop=True, reads=(consts,), writes=(ppb[3],))
    op("dve", "tensor_copy", gbB_t[:], pp[3][0:P, 0:E],
       reads=(ppb[3],), writes=(gbB_b,))

    xin3 = xin.rearrange("(c p) t -> p c t", p=P)
    out3 = out.rearrange("(c p) t -> p c t", p=P)

    def chunked_w(h2d):
        return h2d.rearrange("(c p) n -> p c n", p=P)

    r_m = rows_t[0:1, 0:QW]
    r_q = rows_t[0:1, QW:2 * QW]
    r_s = rows_t[0:1, 2 * QW:3 * QW]
    rr_rstd = rows_t[0:1, 3 * QW:4 * QW]
    rr_mrs = r_s

    def ln_stats(quarters, dat_bufs, norm):
        def banks(qi):
            return ((pp[2], 2, pp[3], 3) if qi % 2 == 0
                    else (pp[6], 6, pp[7], 7))

        def stats(qi):
            chunks, f3d, wq, c0 = quarters[qi]
            ppS, ppSi, ppB_, ppBi = banks(qi)
            sq3 = fz1_t[:, 0:CHK * wq].rearrange("p (c s) -> p c s", s=wq)
            op("dve", "tensor_tensor", sq3, f3d, f3d, ALU.mult,
               reads=tuple(dat_bufs), writes=(aFZ1,))
            for ci in range(CHK):
                op("pe", "matmul", ppS[0:1, 0:wq], onesc_t[:], chunks[ci],
                   start=(ci == 0), stop=(ci == CHK - 1),
                   reads=tuple(dat_bufs) + (consts,), writes=(ppb[ppSi],))
            for ci in range(CHK):
                op("pe", "matmul", ppB_[0:1, 0:wq], onesc_t[:],
                   sq3[:, ci, :], start=(ci == 0), stop=(ci == CHK - 1),
                   reads=(aFZ1, consts), writes=(ppb[ppBi],))

        def tail(qi):
            chunks, f3d, wq, c0 = quarters[qi]
            ppS, ppSi, ppB_, ppBi = banks(qi)
            op("dve", "tensor_scalar", r_m[:, 0:wq], ppS[0:1, 0:wq],
               1.0 / D, None, ALU.mult, reads=(ppb[ppSi],), writes=(rows_b,))
            op("dve", "tensor_scalar", r_q[:, 0:wq], ppB_[0:1, 0:wq],
               1.0 / D, None, ALU.mult, reads=(ppb[ppBi],), writes=(rows_b,))
            op("dve", "tensor_tensor", r_s[:, 0:wq], r_m[:, 0:wq], r_m[:, 0:wq],
               ALU.mult, reads=(rows_b,), writes=(rows_b,))
            op("dve", "tensor_tensor", r_s[:, 0:wq], r_q[:, 0:wq], r_s[:, 0:wq],
               ALU.subtract, reads=(rows_b,), writes=(rows_b,))
            op("act", "activation", r_s[:, 0:wq], r_s[:, 0:wq], AF.Ln,
               bias=misc_t[0:1, 1:2], reads=(rows_b, consts), writes=(rows_b,))
            op("act", "activation", rr_rstd[:, 0:wq], r_s[:, 0:wq], AF.Exp,
               scale=-0.5, reads=(rows_b,), writes=(rows_b,))
            op("dve", "tensor_tensor", rr_mrs[:, 0:wq], r_m[:, 0:wq],
               rr_rstd[:, 0:wq], ALU.mult, reads=(rows_b,), writes=(rows_b,))
            op("pe", "matmul", ppS[:, 0:wq], onesr_t[0:1, 0:P],
               rr_rstd[:, 0:wq], start=True, stop=True,
               reads=(rows_b, consts), writes=(ppb[ppSi],))
            op("pe", "matmul", ppB_[:, 0:wq], onesr_t[0:1, 0:P],
               rr_mrs[:, 0:wq], start=True, stop=True,
               reads=(rows_b, consts), writes=(ppb[ppBi],))
            norm(qi, c0, wq, ppS[:, 0:wq], ppB_[:, 0:wq],
                 (ppb[ppSi], ppb[ppBi]))

        for qi in range(len(quarters)):
            stats(qi)
            tail(qi)

    def half_quarters(src_t, base):
        qs = []
        for q in range(2):
            c0 = base + q * QW
            qs.append(([src_t[:, ci, c0:c0 + QW] for ci in range(CHK)],
                       src_t[:, :, c0:c0 + QW].bitcast(F32), QW, q * QW))
        return qs

    out_toks = []
    for tile in range(TILES):
        t0 = tile * TILE
        seq_start = (tile % 2 == 0)
        seq_end = (tile % 2 == 1)

        ncols = TILE + (0 if seq_end else 1)
        dma("gp", xb_t[:, :, 0:ncols], xin3[:, :, t0:t0 + ncols], writes=(xb,))

        arena_switch(PHASE_A, PHASE_B)

        for nh in range(2):
            w0 = nh * HN

            def norm1(qi, c0, wq, rB, mB, rbufs, w0=w0):
                rB3 = rB.unsqueeze(1).broadcast_to([P, CHK, wq])
                mB3 = mB.unsqueeze(1).broadcast_to([P, CHK, wq])
                dst = ly_t[:, :, 1 + w0 + c0:1 + w0 + c0 + wq]
                src = xb_t[:, :, w0 + c0:w0 + c0 + wq].bitcast(F32)
                op("dve", "tensor_tensor", dst, src, rB3, ALU.mult,
                   reads=(xb,) + rbufs[:1], writes=(ly,))
                op("dve", "tensor_tensor", dst, dst.bitcast(F32), mB3,
                   ALU.subtract, reads=rbufs[1:], writes=(ly,))

            ln_stats(half_quarters(xb_t, w0), (xb,), norm1)

        if seq_start:
            op("dve", "tensor_scalar", ly_t[:, :, 0:1], mixk_t[:].unsqueeze(2),
               0.0, None, ALU.mult, reads=(consts,), writes=(ly,))
        else:
            op("dve", "tensor_copy", ly_t[:, :, 0:1], ltail_t[:].unsqueeze(2),
               reads=(ltail,), writes=(ly,))
        op("dve", "tensor_copy", ltail_t[:].unsqueeze(2),
           ly_t[:, :, TILE:TILE + 1].bitcast(F32), reads=(ly,), writes=(ltail,))

        if seq_end:
            op("dve", "tensor_scalar", ly_t[:, :, TILE + 1:TILE + 2],
               mixk_t[:].unsqueeze(2), 0.0, None, ALU.mult,
               reads=(consts,), writes=(ly,))
        else:
            xcol_f = xb_t[:, :, TILE:TILE + 1].bitcast(F32).rearrange(
                "p c o -> p (c o)")
            sqh = fz1_t[:, 0:CHK]
            op("dve", "tensor_tensor", sqh, xcol_f, xcol_f, ALU.mult,
               reads=(xb,), writes=(aFZ1,))
            for ci in range(CHK):
                op("pe", "matmul", pp[2][0:1, 0:1], onesc_t[:].bitcast(F32),
                   xb_t[:, ci, TILE:TILE + 1].bitcast(F32),
                   start=(ci == 0), stop=(ci == CHK - 1),
                   reads=(xb, consts), writes=(ppb[2],))
            for ci in range(CHK):
                op("pe", "matmul", pp[3][0:1, 0:1], onesc_t[:].bitcast(F32),
                   sqh[:, ci:ci + 1].bitcast(F32),
                   start=(ci == 0), stop=(ci == CHK - 1),
                   reads=(aFZ1, consts), writes=(ppb[3],))
            op("dve", "tensor_scalar", r_m[:, 0:1], pp[2][0:1, 0:1], 1.0 / D,
               None, ALU.mult, reads=(ppb[2],), writes=(rows_b,))
            op("dve", "tensor_scalar", r_q[:, 0:1], pp[3][0:1, 0:1], 1.0 / D,
               None, ALU.mult, reads=(ppb[3],), writes=(rows_b,))
            op("dve", "tensor_tensor", r_s[:, 0:1], r_m[:, 0:1], r_m[:, 0:1],
               ALU.mult, reads=(rows_b,), writes=(rows_b,))
            op("dve", "tensor_tensor", r_s[:, 0:1], r_q[:, 0:1], r_s[:, 0:1],
               ALU.subtract, reads=(rows_b,), writes=(rows_b,))
            op("act", "activation", r_s[:, 0:1], r_s[:, 0:1], AF.Ln,
               bias=misc_t[0:1, 1:2], reads=(rows_b, consts), writes=(rows_b,))
            op("act", "activation", rr_rstd[:, 0:1], r_s[:, 0:1], AF.Exp,
               scale=-0.5, reads=(rows_b,), writes=(rows_b,))
            op("dve", "tensor_copy", rr_mrs[:, 0:1], r_m[:, 0:1],
               reads=(rows_b,), writes=(rows_b,))
            op("pe", "matmul", pp[3][:, 0:1], onesr_t[0:1, 0:P],
               rr_mrs[:, 0:1], start=True, stop=True,
               reads=(rows_b, consts), writes=(ppb[3],))
            op("pe", "matmul", pp[3][:, 1:2], onesr_t[0:1, 0:P],
               rr_rstd[:, 0:1], start=True, stop=True,
               reads=(rows_b, consts), writes=(ppb[3],))
            mr_sb = ar_t[:, O_BC:O_BC + 2]
            op("dve", "tensor_copy", mr_sb, pp[3][:, 0:2],
               reads=(ppb[3],), writes=(aBC,))
            op("dve", "tensor_scalar",
               ly_t[:, :, TILE + 1:TILE + 2].rearrange("p c o -> p (c o)"),
               xcol_f, mr_sb[:, 0:1], mr_sb[:, 1:2], ALU.subtract, ALU.mult,
               reads=(xb, aBC), writes=(ly,))

        tap(f"lnx{tile}", ly_t[:, :, 1:TILE + 1].bitcast(F32), ly)

        if seq_start:
            op("dve", "tensor_scalar", acar_t[:], mixk_t[:], 0.0, None,
               ALU.mult, reads=(consts,), writes=(acar,))
            op("dve", "tensor_scalar", bcar_t[:], mixk_t[:], 0.0, None,
               ALU.mult, reads=(consts,), writes=(bcar,))

        dma("gp", wb_t[0][:], chunked_w(wk_h), writes=(wb[0],))
        dma("gp", wb_t[1][:], chunked_w(wv_h), writes=(wb[1],))
        dma("gp", wb_t[2][:], chunked_w(wr_h), writes=(wb[2],))
        dma("gp", xt_t[:], chunked_w(wo_h), writes=(xt,))

        B1f = arf(O_B1, CHK, SUB)
        B2f = arf(O_B2, CHK, SUB)
        Af = arf(O_A, CHK, SUB + 1)
        Bsf = arf(O_Bs, CHK, SUB + 1)
        XKb = arbf(O_XK, CHK, SUB)
        SRb = arbf(O_SR, CHK, SUB)
        EKr = fz1_t[:, 0:CHK * SUB].rearrange("p (c s) -> p c s", s=SUB)
        EKf = EKr.bitcast(F32)
        NUr = fz2_t[:, 0:CHK * SUB].rearrange("p (c s) -> p c s", s=SUB)
        NUf = NUr.bitcast(F32)

        def xx_ap(ci, s0):
            if ci < 2:
                return ly_t[:, ci, s0:s0 + SUB].bitcast(F32)
            if ci < 4:
                return ly_t[:, ci, s0 + 2:s0 + 2 + SUB].bitcast(F32)
            return ly_t[:, ci, 1 + s0:1 + s0 + SUB].bitcast(F32)

        def mix_into(dst3, mix_t, dstbuf, s0):
            for ci in range(CHK):
                op("dve", "scalar_tensor_tensor", dst3[:, ci, :],
                   B2f[:, ci, :], mix_t[:, ci:ci + 1], xx_ap(ci, s0),
                   ALU.mult, ALU.add,
                   reads=(aB2, ly, consts), writes=(dstbuf,))

        def front(s0):
            for (a_, b_), off in zip([(0, 2), (2, 4), (4, 8)], [0, 2, 1]):
                op("dve", "tensor_tensor", B2f[:, a_:b_, :],
                   ly_t[:, a_:b_, 1 + s0:1 + s0 + SUB].bitcast(F32),
                   ly_t[:, a_:b_, off + s0:off + s0 + SUB].bitcast(F32),
                   ALU.subtract, reads=(ly,), writes=(aB2,))
            mix_into(XKb, mixk_t, aXK, s0)

        def proj(mov3, movbuf, wtile, wbuf, evac_m):
            for m in range(CHK):
                bank = m % 2
                for kc in range(CHK):
                    op("pe", "matmul", pp[bank][:, 0:SUB],
                       wtile[:, kc, m * P:(m + 1) * P], mov3[:, kc, :],
                       start=(kc == 0), stop=(kc == CHK - 1),
                       reads=(movbuf, wbuf), writes=(ppb[bank],))
                evac_m(m, bank)

        def ev_copy(dst3, dstbuf, f):
            def e(m, bank):
                op("act", "activation", dst3[:, m, :], pp[bank][:, 0:SUB],
                   f, reads=(ppb[bank],), writes=(dstbuf,))
            return e

        def ev_sig1(m, bank):
            op("act", "activation", SRb[:, m, :], pp[bank][:, 0:SUB],
               AF.Exp, scale=-1.0, reads=(ppb[bank],), writes=(aSR,))

        front(0)
        for sub in range(SUBS):
            s0 = sub * SUB

            proj(XKb, aXK, wb_t[0], wb[0], ev_copy(B1f, aB1, AF.Copy))
            mix_into(XKb, mixv_t, aXK, s0)
            proj(XKb, aXK, wb_t[1], wb[1], ev_copy(NUr, aFZ2, AF.Copy))
            mix_into(XKb, mixr_t, aXK, s0)

            proj(XKb, aXK, wb_t[2], wb[2], ev_sig1)
            op("act", "activation", SRb[:], SRb[:], AF.Ln,
               bias=misc_t[:, 0:1], reads=(consts,), writes=(aSR,))
            op("act", "activation", SRb[:], SRb[:], AF.Exp, scale=-1.0,
               writes=(aSR,))

            op("act", "activation", Bsf[:, :, 1:SUB + 1], B1f[:], AF.Exp,
               reads=(aB1,), writes=(aBs,))
            for ci in range(CHK):
                op("act", "activation", B1f[:, ci, :], B1f[:, ci, :], AF.Exp,
                   bias=uT_t[:, ci:ci + 1], reads=(ewu,), writes=(aB1,))
            op("dve", "tensor_copy", Bsf[:, :, 0:1], bcar_t[:].unsqueeze(2),
               reads=(bcar,), writes=(aBs,))
            for ci in range(CHK):
                op("dve", "tensor_tensor_scan", Af[:, ci, 0:SUB + 1],
                   ew_t[:, ci:ci + 1].broadcast_to([P, SUB + 1]),
                   Bsf[:, ci, 0:SUB + 1], 0.0, ALU.mult, ALU.add,
                   reads=(ewu, aBs), writes=(aA,))
            op("dve", "tensor_copy", bcar_t[:].unsqueeze(2),
               Af[:, :, SUB:SUB + 1], reads=(aA,), writes=(bcar,))
            op("dve", "tensor_tensor", EKr[:], Bsf[:, :, 1:SUB + 1], NUf[:],
               ALU.mult, reads=(aBs, aFZ2), writes=(aFZ1,))
            op("dve", "tensor_tensor", NUr[:], B1f[:], NUf[:], ALU.mult,
               reads=(aB1,), writes=(aFZ2,))
            op("dve", "tensor_tensor", B1f[:], B1f[:], Af[:, :, 0:SUB],
               ALU.add, reads=(aA,), writes=(aB1,))
            op("act", "activation", B1f[:], B1f[:], AF.Ln, writes=(aB1,))
            op("act", "activation", B1f[:], B1f[:], AF.Exp, scale=-1.0,
               writes=(aB1,))
            op("dve", "tensor_copy", Af[:, :, 0:1], acar_t[:].unsqueeze(2),
               reads=(acar,), writes=(aA,))
            for ci in range(CHK):
                op("dve", "tensor_tensor_scan", Af[:, ci, 1:SUB + 1],
                   ew_t[:, ci:ci + 1].broadcast_to([P, SUB]), EKf[:, ci, :],
                   acar_t[:, ci:ci + 1], ALU.mult, ALU.add,
                   reads=(ewu, aFZ1, acar), writes=(aA,))
            op("dve", "tensor_copy", acar_t[:].unsqueeze(2),
               Af[:, :, SUB:SUB + 1], reads=(aA,), writes=(acar,))
            op("dve", "tensor_tensor", NUr[:], NUf[:], Af[:, :, 0:SUB],
               ALU.add, reads=(aA,), writes=(aFZ2,))
            op("dve", "tensor_tensor", NUr[:], NUf[:], B1f[:], ALU.mult,
               reads=(aB1,), writes=(aFZ2,))
            op("dve", "tensor_tensor", XKb[:], NUf[:], SRb[:], ALU.mult,
               reads=(aSR, aFZ2), writes=(aXK,))

            proj(XKb, aXK, xt_t, xt, ev_copy(NUr, aFZ2, AF.Copy))

            if sub + 1 < SUBS:
                front(s0 + SUB)

            def norm2(qi, c0, wq, rB, mB, rbufs, s0=s0):
                rB3 = rB.unsqueeze(1).broadcast_to([P, CHK, wq])
                mB3 = mB.unsqueeze(1).broadcast_to([P, CHK, wq])
                op("dve", "tensor_tensor", B1f[:], NUf[:], rB3, ALU.mult,
                   reads=(aFZ2,) + rbufs[:1], writes=(aB1,))
                op("dve", "tensor_tensor", B1f[:], B1f[:], mB3, ALU.subtract,
                   reads=rbufs[1:], writes=(aB1,))
                xs = xb_t[:, :, s0:s0 + SUB]
                op("dve", "tensor_tensor", xs, xs.bitcast(F32), B1f[:],
                   ALU.add, reads=(aB1,), writes=(xb,))

            ln_stats([([NUr[:, ci, :] for ci in range(CHK)],
                       NUf[:], SUB, 0)], (aFZ2,), norm2)

        arena_switch(PHASE_B, PHASE_A)

        GH = arbf(O_GH, CHK, HN)
        S1a = arf2(O_S1, HN)
        GP3 = arf2(O_GP3, HN)
        GE = arf2(O_GE, D)
        GLl = arf(O_GL, CHK, E)
        GLsc = arf2(O_GL + 64, 256)
        T1 = arf(O_T1, CHK, HN)
        yv = ly_t[:, :, 0:TILE].bitcast(F32)
        yv3 = ly_t[:, :, 0:TILE]

        for nh in range(2):
            w0 = nh * HN

            def norm3(qi, c0, wq, rB, mB, rbufs, w0=w0):
                rB3 = rB.unsqueeze(1).broadcast_to([P, CHK, wq])
                mB3 = mB.unsqueeze(1).broadcast_to([P, CHK, wq])
                t1q = T1[:, :, c0:c0 + wq]
                op("dve", "tensor_tensor", t1q,
                   xb_t[:, :, w0 + c0:w0 + c0 + wq].bitcast(F32), rB3,
                   ALU.mult, reads=(xb,) + rbufs[:1],
                   writes=(aT1, aGH, aS1, aGP3, aGE))
                op("dve", "tensor_tensor", xt_t[:, :, w0 + c0:w0 + c0 + wq],
                   t1q, mB3, ALU.subtract, reads=(aT1,) + rbufs[1:],
                   writes=(xt,))

            ln_stats(half_quarters(xb_t, w0), (xb,), norm3)

        for tc in range(CHK):
            for kc in range(CHK):
                op("pe", "matmul", pp[2][:, tc * E:(tc + 1) * E],
                   xt_t[:, kc, tc * P:(tc + 1) * P], gwt_t[:, kc, :],
                   start=(kc == 0), stop=(kc == CHK - 1),
                   reads=(xt, consts), writes=(ppb[2],))
        op("dve", "tensor_tensor", GLl[:],
           pp[2][:, 0:64].rearrange("p (c e) -> p c e", e=E),
           gbB_t[:].unsqueeze(1).broadcast_to([P, CHK, E]), ALU.add,
           reads=(ppb[2], gbB_b), writes=(aGL,))
        gm1 = GLsc[:, 0:64].rearrange("p (c e) -> p c e", e=E)
        gtmp = GLsc[:, 64:128].rearrange("p (c e) -> p c e", e=E)
        gmax = GLsc[:, 128:136].rearrange("p (c e) -> p c e", e=1)
        gsum = GLsc[:, 136:144].rearrange("p (c e) -> p c e", e=1)
        gsum2 = GLsc[:, 136:144]
        op("dve", "tensor_reduce", gmax, GLl[:], AX.X, ALU.max,
           reads=(aGL,), writes=(aGL,))
        op("dve", "tensor_tensor", GLl[:], GLl[:],
           gmax.broadcast_to([P, CHK, E]), ALU.subtract, writes=(aGL,))
        op("act", "activation", arf2(O_GL, 64), arf2(O_GL, 64), AF.Exp,
           reads=(aGL,), writes=(aGL,))
        op("dve", "tensor_reduce", gsum, GLl[:], AX.X, ALU.add, writes=(aGL,))
        op("dve", "tensor_reduce", gmax, GLl[:], AX.X, ALU.max, writes=(aGL,))
        op("dve", "tensor_tensor", gm1[:], GLl[:],
           gmax.broadcast_to([P, CHK, E]), ALU.is_ge, writes=(aGL,))
        op("dve", "tensor_scalar", gm1[:], gm1[:], 1e30, None, ALU.mult,
           writes=(aGL,))
        op("dve", "tensor_tensor", gtmp[:], GLl[:], gm1[:], ALU.subtract,
           writes=(aGL,))
        op("dve", "tensor_reduce", gmax, gtmp[:], AX.X, ALU.max, writes=(aGL,))
        op("dve", "tensor_tensor", gm1[:], GLl[:],
           gmax.broadcast_to([P, CHK, E]), ALU.is_ge, writes=(aGL,))
        op("dve", "tensor_tensor", gm1[:], gm1[:], GLl[:], ALU.mult,
           writes=(aGL,))
        op("act", "activation", gsum2, gsum2, AF.Ln, writes=(aGL,))
        op("act", "activation", gsum2, gsum2, AF.Exp, scale=-1.0, writes=(aGL,))
        op("dve", "tensor_tensor", gm1[:], gm1[:],
           gsum.broadcast_to([P, CHK, E]), ALU.mult, writes=(aGL,))

        for tc in range(CHK):
            bank = 4 + (tc // 4)
            op("pe", "transpose", pp[bank][0:E, (tc % 4) * P:(tc % 4) * P + P],
               gm1[:, tc, :], ident_t[:],
               reads=(aGL, consts), writes=(ppb[bank],))
        for half in range(2):
            op("dve", "tensor_copy", gfm_t[:, half * HN:(half + 1) * HN],
               pp[4 + half][0:E, :], reads=(ppb[4 + half],), writes=(gfm_b,))

        for ex in range(NEXP):
            shared = ex >= E
            h = ex - E
            if shared:
                w1src = chunked_w(ws1_h[:, h * D:(h + 1) * D])
                w3src = chunked_w(ws3_h[:, h * D:(h + 1) * D])
                w2src = chunked_w(ws2_h[h * D:(h + 1) * D, :])
            else:
                w1src = chunked_w(w1_h[ex])
                w3src = chunked_w(w3_h[ex])
                w2src = chunked_w(w2_h[ex])
            dma("gp", wb_t[0][:], w1src, writes=(wb[0],))
            dma("gp", wb_t[1][:], w3src, writes=(wb[1],))
            dma("gp", wb_t[2][:], w2src, writes=(wb[2],))

            if not shared:
                for nh in range(2):
                    op("pe", "matmul", pp[4 + nh][:, 0:HN],
                       oh_t[:, ex * P:(ex + 1) * P],
                       gfm_t[:, nh * HN:(nh + 1) * HN], start=True, stop=True,
                       reads=(gfm_b, consts), writes=(ppb[4 + nh],))
                    op("dve", "tensor_copy", GE[:, nh * HN:(nh + 1) * HN],
                       pp[4 + nh][:, 0:HN], reads=(ppb[4 + nh],), writes=(aGE,))

            for nh in range(2):
                w0 = nh * HN
                for m in range(CHK):
                    bA = 4 + (m % 2)
                    bB_ = 6 + (m % 2)
                    for kc in range(CHK):
                        op("pe", "matmul", pp[bA][:, 0:HN],
                           wb_t[0][:, kc, m * P:(m + 1) * P],
                           xt_t[:, kc, w0:w0 + HN],
                           start=(kc == 0), stop=(kc == CHK - 1),
                           reads=(wb[0], xt), writes=(ppb[bA],))
                    for kc in range(CHK):
                        op("pe", "matmul", pp[bB_][:, 0:HN],
                           wb_t[1][:, kc, m * P:(m + 1) * P],
                           xt_t[:, kc, w0:w0 + HN],
                           start=(kc == 0), stop=(kc == CHK - 1),
                           reads=(wb[1], xt), writes=(ppb[bB_],))
                    if shared:
                        b1col = bs1_t[:, h * CHK + m:h * CHK + m + 1]
                        b3col = bs3_t[:, h * CHK + m:h * CHK + m + 1]
                    else:
                        b1col = b1_t[:, ex * CHK + m:ex * CHK + m + 1]
                        b3col = b3_t[:, ex * CHK + m:ex * CHK + m + 1]
                    op("act", "activation", S1a[:], pp[bA][:, 0:HN], AF.Silu,
                       bias=b1col, reads=(ppb[bA], consts), writes=(aS1,))
                    if shared:
                        op("dve", "tensor_scalar", GP3[:], pp[bB_][:, 0:HN],
                           b3col, None, ALU.add,
                           reads=(ppb[bB_], consts), writes=(aGP3,))
                    else:
                        op("dve", "scalar_tensor_tensor", GP3[:],
                           pp[bB_][:, 0:HN], b3col, GE[:, w0:w0 + HN],
                           ALU.add, ALU.mult,
                           reads=(ppb[bB_], consts, aGE), writes=(aGP3,))
                    op("dve", "tensor_tensor", GH[:, m, :], S1a[:], GP3[:],
                       ALU.mult, reads=(aS1, aGP3), writes=(aGH,))
                for c in range(CHK):
                    bank = c % 2
                    if ex == 0:
                        op("pe", "matmul", pp[bank][:, 0:HN],
                           b2_t[:, c * P:(c + 1) * P],
                           gfm_t[:, w0:w0 + HN], start=True, stop=False,
                           reads=(gfm_b, consts), writes=(ppb[bank],))
                        op("pe", "matmul", pp[bank][:, 0:HN],
                           bs2_t[0:1, c * P:(c + 1) * P], onesr_t[:],
                           start=False, stop=False,
                           reads=(consts,), writes=(ppb[bank],))
                    for kc in range(CHK):
                        op("pe", "matmul", pp[bank][:, 0:HN],
                           wb_t[2][:, kc, c * P:(c + 1) * P], GH[:, kc, :],
                           start=(kc == 0 and ex != 0),
                           stop=(kc == CHK - 1),
                           reads=(wb[2], aGH), writes=(ppb[bank],))
                    if ex == 0:
                        op("act", "activation", yv3[:, c, w0:w0 + HN],
                           pp[bank][:, 0:HN], AF.Copy,
                           reads=(ppb[bank],), writes=(ly,))
                    else:
                        op("dve", "tensor_tensor", yv3[:, c, w0:w0 + HN],
                           yv[:, c, w0:w0 + HN], pp[bank][:, 0:HN],
                           ALU.add, reads=(ppb[bank],), writes=(ly,))

        tap(f"y{tile}", ly_t[:, :, 0:TILE].bitcast(F32), ly)

        for nh in range(2):
            w0 = nh * HN

            def norm4(qi, c0, wq, rB, mB, rbufs, w0=w0):
                rB3 = rB.unsqueeze(1).broadcast_to([P, CHK, wq])
                mB3 = mB.unsqueeze(1).broadcast_to([P, CHK, wq])
                t1q = T1[:, :, c0:c0 + wq]
                op("dve", "tensor_tensor", t1q, yv[:, :, w0 + c0:w0 + c0 + wq],
                   rB3, ALU.mult, reads=(ly,) + rbufs[:1],
                   writes=(aT1, aGH, aS1, aGP3, aGE))
                op("dve", "tensor_tensor", t1q, t1q, mB3, ALU.subtract,
                   reads=rbufs[1:], writes=(aT1,))
                xs = xb_t[:, :, w0 + c0:w0 + c0 + wq]
                op("dve", "tensor_tensor", xs, xs.bitcast(F32), t1q, ALU.add,
                   reads=(aT1,), writes=(xb,))

            ln_stats(half_quarters(ly_t, w0), (ly,), norm4)

        tok = dma("sp", out3[:, :, t0:t0 + TILE],
                  xb_t[:, :, 0:TILE].bitcast(F32), reads=(xb,))
        out_toks.append(tok)

        tap(f"x{tile}", xb_t[:, :, 0:TILE].bitcast(F32), xb)

    for t in out_toks:
        S.want("sp", t)
    S.emit("sp", "nop", (), {})

    S.finalize()

    import contextlib
    with contextlib.ExitStack() as stack:
        sems = {}
        for s in ENGS:
            sems[s] = stack.enter_context(nc.semaphore("sem_" + s))
        block = stack.enter_context(nc.Block())

        @block.sync
        def _(En):
            S.replay(sems, "sp", En)

        @block.gpsimd
        def _(En):
            S.replay(sems, ("gp", "gpc"), En)

        @block.tensor
        def _(En):
            S.replay(sems, "pe", En)

        @block.scalar
        def _(En):
            S.replay(sems, "act", En)

        @block.vector
        def _(En):
            S.replay(sems, "dve", En)

    for c in reversed(ctx):
        c.__exit__(None, None, None)
    return nc


def _onehot_blocks():
    oh = np.zeros((E, E * P), dtype=np.float32)
    for e in range(E):
        oh[e, e * P:(e + 1) * P] = 1.0
    return oh


def _host_prep(inputs):
    gi = lambda k: np.asarray(inputs[k])
    x = gi("L0_attn_x")

    def fm(vec):
        return np.ascontiguousarray(vec.reshape(CHK, P).T)

    com = {
        "wk": np.ascontiguousarray(gi("L0_attn_Wk").T),
        "wv": np.ascontiguousarray(gi("L0_attn_Wv").T),
        "wr": np.ascontiguousarray(gi("L0_attn_Wr").T),
        "wo": np.ascontiguousarray(gi("L0_attn_Wo").T),
        "w1": np.ascontiguousarray(np.transpose(gi("L0_ffn_w1"), (0, 2, 1))),
        "w3": np.ascontiguousarray(np.transpose(gi("L0_ffn_w3"), (0, 2, 1))),
        "w2": np.ascontiguousarray(np.transpose(gi("L0_ffn_w2"), (0, 2, 1))),
        "ws1": np.ascontiguousarray(gi("L0_ffn_ws1").T),
        "ws3": np.ascontiguousarray(gi("L0_ffn_ws3").T),
        "ws2": np.ascontiguousarray(gi("L0_ffn_ws2").T),
        "gwt": np.ascontiguousarray(gi("L0_ffn_gate_w").T),
        "mixk": fm(gi("L0_attn_mix_k")),
        "mixv": fm(gi("L0_attn_mix_v")),
        "mixr": fm(gi("L0_attn_mix_r")),
        "sd": fm(gi("L0_attn_spatial_decay")),
        "sf": fm(gi("L0_attn_spatial_first")),
        "b1f": np.ascontiguousarray(
            gi("L0_ffn_b1").reshape(E, CHK, P).transpose(2, 0, 1).reshape(P, E * CHK)),
        "b3f": np.ascontiguousarray(
            gi("L0_ffn_b3").reshape(E, CHK, P).transpose(2, 0, 1).reshape(P, E * CHK)),
        "bs1f": np.ascontiguousarray(
            gi("L0_ffn_bs1").reshape(2, CHK, P).transpose(2, 0, 1).reshape(P, 2 * CHK)),
        "bs3f": np.ascontiguousarray(
            gi("L0_ffn_bs3").reshape(2, CHK, P).transpose(2, 0, 1).reshape(P, 2 * CHK)),
        "b2": np.ascontiguousarray(gi("L0_ffn_b2")),
        "bs2": np.ascontiguousarray(gi("L0_ffn_bs2").reshape(1, D)),
        "gb": np.ascontiguousarray(gi("L0_ffn_gate_b").reshape(1, E)),
        "ident": np.eye(P, dtype=np.float32),
        "oh": _onehot_blocks(),
        "onesr": np.ones((1, HN), dtype=np.float32),
        "onesc": np.ones((P, 1), dtype=np.float32),
        "misc": np.concatenate([np.ones((P, 1), np.float32),
                                np.full((P, 1), EPS, np.float32)], axis=1),
    }
    for k in com:
        com[k] = np.ascontiguousarray(com[k].astype(np.float32, copy=False))

    in_maps = []
    for c in range(NCORES):
        xc = np.ascontiguousarray(
            np.concatenate([x[2 * c].T, x[2 * c + 1].T], axis=1)).astype(
                np.float32, copy=False)
        m = {"xin": xc}
        m.update(com)
        in_maps.append(m)
    return in_maps


def kernel(**inputs):
    in_maps = _host_prep(inputs)
    nc = build_program()
    res = run_bass_kernel_spmd(nc, in_maps, core_ids=list(range(NCORES)))
    LAST_DEBUG.clear()
    full = np.empty((NB, T, D), dtype=np.float32)
    for c in range(NCORES):
        r = res.results[c]
        for k, v in r.items():
            if k.startswith("dbg_"):
                LAST_DEBUG[(c, k)] = np.asarray(v)
        o = np.asarray(r["out"])
        full[2 * c] = o[:, 0:T].T
        full[2 * c + 1] = o[:, T:2 * T].T
    return full


# version 25
# speedup vs baseline: 1.0458x; 1.0055x over previous
import numpy as np
import concourse.bass as bass
import concourse.mybir as mybir
from concourse.bass_utils import run_bass_kernel_spmd

F32 = mybir.dt.float32
BF16 = mybir.dt.bfloat16
F32R = mybir.dt.float32r
AF = mybir.ActivationFunctionType
ALU = mybir.AluOpType
AX = mybir.AxisListType

P = 128
CHK = 8
D = 1024
T = 2048
NB = 16
NCORES = 8
SEQ = 2
TOK = SEQ * T
TILES = 4
TILE = 1024
SUBS = 4
SUB = 256
HN = 512
QW = 256
E = 8
NEXP = 10
EPS = 1e-5
INVT = 1.0 / T

DEBUG_TAPS = []
LAST_DEBUG = {}

ENGS = ("sp", "gp", "gpc", "pe", "act", "dve")
DMA_SEMS = ("sp", "gp")


class Sched:
    def __init__(self):
        self.ops = {e: [] for e in ENGS}
        self._pending = {}
        self._seq = 0

    UNSAFE_METHODS = ("tensor_tensor_scan", "scalar_tensor_tensor")

    SAME_SKIP = ("pe",)

    def want(self, sem, token, force=False):
        if token is None:
            return
        tsem, tidx = token
        if (tsem == sem and tsem not in DMA_SEMS and not force
                and tsem in self.SAME_SKIP):
            prod = self.ops[tsem][tidx]["m"]
            if prod not in self.UNSAFE_METHODS:
                return
        self._pending.setdefault(sem, []).append((tsem, tidx))

    def emit(self, sem, method, args, kwargs):
        w = self._pending.pop(sem, [])
        idx = len(self.ops[sem])
        self._seq += 1
        self.ops[sem].append(
            {"m": method, "a": args, "k": kwargs, "flag": False, "waits": w,
             "seq": self._seq})
        for (ts, ti) in w:
            self.ops[ts][ti]["flag"] = True
        return (sem, idx)

    def finalize(self):
        self.ms = {}
        for s in ENGS:
            c = 0
            nums = []
            for o in self.ops[s]:
                if o["flag"]:
                    c += 1
                nums.append(c)
            self.ms[s] = nums

    def replay(self, sems, semlist, engine):
        if isinstance(semlist, str):
            semlist = (semlist,)
        stream = []
        for sm in semlist:
            for o in self.ops[sm]:
                stream.append((o["seq"], sm, o))
        stream.sort()
        lastw = {}
        for _, sm, o in stream:
            for (ts, ti) in o["waits"]:
                v = self.ms[ts][ti]
                step = 16 if ts in DMA_SEMS else 1
                if lastw.get(ts, 0) >= v:
                    continue
                lastw[ts] = v
                engine.wait_ge(sems[ts], v * step)
            instr = getattr(engine, o["m"])(*o["a"], **o["k"])
            if o["flag"]:
                step = 16 if sm in DMA_SEMS else 1
                instr.then_inc(sems[sm], step)


class Buf:
    __slots__ = ("name", "last_w", "readers", "frozen")

    def __init__(self, name, frozen=False):
        self.name = name
        self.last_w = None
        self.readers = []
        self.frozen = frozen


def build_program():
    nc = bass.Bass("TRN2", target_bir_lowering=False)
    S = Sched()
    ctx = []

    def alloc(name, shape, dt):
        t = nc.sbuf_tensor(name, shape, dt)
        h = t.__enter__()
        ctx.append(t)
        return h

    def palloc(name, shape, dt):
        t = nc.psum_tensor(name, shape, dt)
        h = t.__enter__()
        ctx.append(t)
        return h

    def op(sem, method, *args, reads=(), writes=(), **kwargs):
        force = method in Sched.UNSAFE_METHODS
        for b in reads:
            S.want(sem, b.last_w, force)
        for b in writes:
            S.want(sem, b.last_w, force)
            for t in b.readers:
                S.want(sem, t, force)
        tok = S.emit(sem, method, args, kwargs)
        for b in reads:
            if not b.frozen:
                b.readers.append(tok)
        for b in writes:
            b.last_w = tok
            b.readers = []
        return tok

    def din(name, shape, dt=F32):
        return nc.dram_tensor(name, shape, dt, kind="ExternalInput").ap()

    xin = din("xin", [D, TOK])
    out = nc.dram_tensor("out", [D, TOK], F32, kind="ExternalOutput").ap()
    wk_h = din("wk", [D, D]); wv_h = din("wv", [D, D])
    wr_h = din("wr", [D, D]); wo_h = din("wo", [D, D])
    w1_h = din("w1", [E, D, D]); w3_h = din("w3", [E, D, D]); w2_h = din("w2", [E, D, D])
    ws1_h = din("ws1", [D, 2 * D]); ws3_h = din("ws3", [D, 2 * D]); ws2_h = din("ws2", [2 * D, D])
    gwt_h = din("gwt", [D, E])
    mixk_h = din("mixk", [P, CHK]); mixv_h = din("mixv", [P, CHK]); mixr_h = din("mixr", [P, CHK])
    sd_h = din("sd", [P, CHK]); sf_h = din("sf", [P, CHK])
    b1_h = din("b1f", [P, E * CHK]); b3_h = din("b3f", [P, E * CHK])
    bs1_h = din("bs1f", [P, 2 * CHK]); bs3_h = din("bs3f", [P, 2 * CHK])
    b2_h = din("b2", [E, D]); bs2_h = din("bs2", [1, D]); gb_h = din("gb", [1, E])
    ident_h = din("ident", [P, P]); oh_h = din("oh", [E, E * P])
    onesr_h = din("onesr", [1, HN]); onesc_h = din("onesc", [P, 1])
    misc_h = din("misc", [P, 2])

    dbg_out = {}
    for tag in DEBUG_TAPS:
        w = CHK * (SUB + 1) if tag.startswith("s_") else CHK * TILE
        dbg_out[tag] = nc.dram_tensor("dbg_" + tag, [P, w], F32,
                                      kind="ExternalOutput").ap()

    xb_t = alloc("xb", [P, CHK, TILE + 1], F32R)
    ly_t = alloc("ly", [P, CHK, TILE + 2], F32R)
    xt_t = alloc("xt", [P, CHK, D], BF16)
    wb_t = [alloc(f"wb{i}", [P, CHK, D], BF16) for i in range(3)]
    AR = 10256
    ar_t = alloc("arena", [P, AR], F32)
    fz1_t = alloc("fz1", [P, 2048], F32R)
    fz2_t = alloc("fz2", [P, 2048], F32R)
    rows_t = alloc("rows", [1, 1024], F32)
    gfm_t = alloc("gfm", [E, D], F32R)
    onesc_t = alloc("onescs", [P, 1], F32R)
    onesr_t = alloc("onesrs", [1, HN], F32)
    ident_t = alloc("idents", [P, P], F32)
    mixk_t = alloc("mixks", [P, CHK], F32)
    mixv_t = alloc("mixvs", [P, CHK], F32)
    mixr_t = alloc("mixrs", [P, CHK], F32)
    sd_t = alloc("sds", [P, CHK], F32)
    sf_t = alloc("sfs", [P, CHK], F32)
    ew_t = alloc("ews", [P, CHK], F32)
    uT_t = alloc("uTs", [P, CHK], F32)
    misc_t = alloc("miscs", [P, 2], F32)
    b1_t = alloc("b1s", [P, E * CHK], F32)
    b3_t = alloc("b3s", [P, E * CHK], F32)
    bs1_t = alloc("bs1s", [P, 2 * CHK], F32)
    bs3_t = alloc("bs3s", [P, 2 * CHK], F32)
    b2_t = alloc("b2s", [E, D], F32R)
    bs2_t = alloc("bs2s", [1, D], F32)
    gbB_t = alloc("gbBs", [P, E], F32)
    gbrow_t = alloc("gbrows", [1, E], F32)
    gwt_t = alloc("gwts", [P, CHK, E], BF16)
    oh_t = alloc("ohs", [E, E * P], F32R)
    acar_t = alloc("acars", [P, CHK], F32)
    bcar_t = alloc("bcars", [P, CHK], F32)
    ltail_t = alloc("ltails", [P, CHK], F32)
    pp = [palloc(f"pp{i}", [P, HN], F32) for i in range(8)]

    xb = Buf("xb"); ly = Buf("ly"); xt = Buf("xt")
    wb = [Buf(f"wb{i}") for i in range(3)]
    ppb = [Buf(f"pp{i}") for i in range(8)]
    rows_b = Buf("rows"); gfm_b = Buf("gfm")
    acar = Buf("acar"); bcar = Buf("bcar"); ltail = Buf("ltail")
    gbB_b = Buf("gbB")
    consts = Buf("consts", frozen=True)
    ewu = Buf("ewu")
    aB1 = Buf("aB1"); aB2 = Buf("aB2")
    aA = Buf("aA"); aBs = Buf("aBs"); aXK = Buf("aXK")
    aFZ1 = Buf("aFZ1"); aFZ2 = Buf("aFZ2")
    aSR = Buf("aSR")
    aGH = Buf("aGH"); aS1 = Buf("aS1"); aGP3 = Buf("aGP3"); aGE = Buf("aGE")
    aGL = Buf("aGL"); aBC2 = Buf("aBC2"); aT1 = Buf("aT1")
    aBC = aB2
    aXK2 = aXK
    PHASE_A = [aB1, aB2, aA, aBs, aXK, aSR]
    PHASE_B = [aGH, aS1, aGP3, aGE, aGL, aBC2, aT1]

    O_B1, O_B2 = 0, 2048
    O_A, O_Bs = 4096, 6152
    O_XK = 8208
    O_SR = 9232
    O_BC = 2048
    O_GH, O_S1, O_GP3, O_GE = 0, 2048, 2560, 3072
    O_GL = 4096
    O_BC2 = 4608
    O_T1 = 0

    def arena_switch(new, old):
        toks = []
        for b in old:
            if b.last_w is not None:
                toks.append(b.last_w)
            toks.extend(b.readers)
            b.last_w = None
            b.readers = []
        for b in new:
            b.last_w = None
            b.readers = list(toks)

    def arf(off, chk, w):
        return ar_t[:, off:off + chk * w].rearrange("p (c s) -> p c s", s=w)

    def arr(off, chk, w):
        return ar_t[:, off:off + chk * w].bitcast(F32R).rearrange(
            "p (c s) -> p c s", s=w)

    def arbf(off, chk, w):
        return ar_t[:, off:off + chk * w // 2].bitcast(BF16).rearrange(
            "p (c s) -> p c s", s=w)

    def arf2(off, w):
        return ar_t[:, off:off + w]

    def dma(q, out_ap, in_ap, reads=(), writes=()):
        return op(q, "dma_start", reads=reads, writes=writes,
                  out=out_ap, in_=in_ap)

    def stap(name, src3_f32, srcbuf, w):
        tag = "s_" + name
        if tag in dbg_out:
            dma("sp", dbg_out[tag][:, 0:CHK * w].rearrange(
                "p (c t) -> p c t", t=w), src3_f32, reads=(srcbuf,))

    def tap(tag, src3_f32, srcbuf):
        if tag in dbg_out:
            dma("sp", dbg_out[tag].rearrange("p (c t) -> p c t", t=TILE),
                src3_f32, reads=(srcbuf,))

    dma("sp", ident_t[:], ident_h, writes=(consts,))
    dma("sp", mixk_t[:], mixk_h, writes=(consts,))
    dma("sp", mixv_t[:], mixv_h, writes=(consts,))
    dma("sp", mixr_t[:], mixr_h, writes=(consts,))
    dma("sp", sd_t[:], sd_h, writes=(consts,))
    dma("sp", sf_t[:], sf_h, writes=(consts,))
    dma("sp", misc_t[:], misc_h, writes=(consts,))
    dma("sp", b1_t[:], b1_h, writes=(consts,))
    dma("sp", b3_t[:], b3_h, writes=(consts,))
    dma("sp", bs1_t[:], bs1_h, writes=(consts,))
    dma("sp", bs3_t[:], bs3_h, writes=(consts,))
    dma("gp", onesc_t[:], onesc_h, writes=(consts,))
    dma("sp", onesr_t[:], onesr_h, writes=(consts,))
    dma("gp", b2_t[:], b2_h, writes=(consts,))
    dma("sp", bs2_t[:], bs2_h, writes=(consts,))
    dma("sp", gbrow_t[:], gb_h, writes=(consts,))
    dma("gp", gwt_t[:], gwt_h.rearrange("(c p) e -> p c e", p=P), writes=(consts,))
    dma("gp", oh_t[:], oh_h, writes=(consts,))

    op("act", "activation", ew_t[:], sd_t[:], AF.Exp, scale=INVT,
       reads=(consts,), writes=(ewu,))
    op("dve", "tensor_scalar", uT_t[:], sf_t[:], INVT, None, ALU.mult,
       reads=(consts,), writes=(ewu,))
    op("pe", "matmul", pp[3][0:P, 0:E], onesr_t[0:1, 0:P], gbrow_t[:],
       start=True, stop=True, reads=(consts,), writes=(ppb[3],))
    op("dve", "tensor_copy", gbB_t[:], pp[3][0:P, 0:E],
       reads=(ppb[3],), writes=(gbB_b,))

    xin3 = xin.rearrange("(c p) t -> p c t", p=P)
    out3 = out.rearrange("(c p) t -> p c t", p=P)

    def chunked_w(h2d):
        return h2d.rearrange("(c p) n -> p c n", p=P)

    r_m = rows_t[0:1, 0:QW]
    r_q = rows_t[0:1, QW:2 * QW]
    r_s = rows_t[0:1, 2 * QW:3 * QW]
    rr_rstd = rows_t[0:1, 3 * QW:4 * QW]
    rr_mrs = r_s

    def ln_stats(quarters, dat_bufs, norm):
        def banks(qi):
            return ((pp[2], 2, pp[3], 3) if qi % 2 == 0
                    else (pp[6], 6, pp[7], 7))

        def stats(qi):
            chunks, f3d, wq, c0 = quarters[qi]
            ppS, ppSi, ppB_, ppBi = banks(qi)
            sq3 = fz1_t[:, 0:CHK * wq].rearrange("p (c s) -> p c s", s=wq)
            op("dve", "tensor_tensor", sq3, f3d, f3d, ALU.mult,
               reads=tuple(dat_bufs), writes=(aFZ1,))
            for ci in range(CHK):
                op("pe", "matmul", ppS[0:1, 0:wq], onesc_t[:], chunks[ci],
                   start=(ci == 0), stop=(ci == CHK - 1),
                   reads=tuple(dat_bufs) + (consts,), writes=(ppb[ppSi],))
            for ci in range(CHK):
                op("pe", "matmul", ppB_[0:1, 0:wq], onesc_t[:],
                   sq3[:, ci, :], start=(ci == 0), stop=(ci == CHK - 1),
                   reads=(aFZ1, consts), writes=(ppb[ppBi],))

        def tail(qi):
            chunks, f3d, wq, c0 = quarters[qi]
            ppS, ppSi, ppB_, ppBi = banks(qi)
            op("dve", "tensor_scalar", r_m[:, 0:wq], ppS[0:1, 0:wq],
               1.0 / D, None, ALU.mult, reads=(ppb[ppSi],), writes=(rows_b,))
            op("dve", "tensor_scalar", r_q[:, 0:wq], ppB_[0:1, 0:wq],
               1.0 / D, None, ALU.mult, reads=(ppb[ppBi],), writes=(rows_b,))
            op("dve", "tensor_tensor", r_s[:, 0:wq], r_m[:, 0:wq], r_m[:, 0:wq],
               ALU.mult, reads=(rows_b,), writes=(rows_b,))
            op("dve", "tensor_tensor", r_s[:, 0:wq], r_q[:, 0:wq], r_s[:, 0:wq],
               ALU.subtract, reads=(rows_b,), writes=(rows_b,))
            op("act", "activation", r_s[:, 0:wq], r_s[:, 0:wq], AF.Ln,
               bias=misc_t[0:1, 1:2], reads=(rows_b, consts), writes=(rows_b,))
            op("act", "activation", rr_rstd[:, 0:wq], r_s[:, 0:wq], AF.Exp,
               scale=-0.5, reads=(rows_b,), writes=(rows_b,))
            op("dve", "tensor_tensor", rr_mrs[:, 0:wq], r_m[:, 0:wq],
               rr_rstd[:, 0:wq], ALU.mult, reads=(rows_b,), writes=(rows_b,))
            op("pe", "matmul", ppS[:, 0:wq], onesr_t[0:1, 0:P],
               rr_rstd[:, 0:wq], start=True, stop=True,
               reads=(rows_b, consts), writes=(ppb[ppSi],))
            op("pe", "matmul", ppB_[:, 0:wq], onesr_t[0:1, 0:P],
               rr_mrs[:, 0:wq], start=True, stop=True,
               reads=(rows_b, consts), writes=(ppb[ppBi],))
            norm(qi, c0, wq, ppS[:, 0:wq], ppB_[:, 0:wq],
                 (ppb[ppSi], ppb[ppBi]))

        for qi in range(len(quarters)):
            stats(qi)
            tail(qi)

    def half_quarters(src_t, base):
        qs = []
        for q in range(2):
            c0 = base + q * QW
            qs.append(([src_t[:, ci, c0:c0 + QW] for ci in range(CHK)],
                       src_t[:, :, c0:c0 + QW].bitcast(F32), QW, q * QW))
        return qs

    out_toks = []
    for tile in range(TILES):
        t0 = tile * TILE
        seq_start = (tile % 2 == 0)
        seq_end = (tile % 2 == 1)

        ncols = TILE + (0 if seq_end else 1)
        dma("gp", xb_t[:, :, 0:ncols], xin3[:, :, t0:t0 + ncols], writes=(xb,))

        arena_switch(PHASE_A, PHASE_B)

        for nh in range(2):
            w0 = nh * HN

            def norm1(qi, c0, wq, rB, mB, rbufs, w0=w0):
                rB3 = rB.unsqueeze(1).broadcast_to([P, CHK, wq])
                mB3 = mB.unsqueeze(1).broadcast_to([P, CHK, wq])
                dst = ly_t[:, :, 1 + w0 + c0:1 + w0 + c0 + wq]
                src = xb_t[:, :, w0 + c0:w0 + c0 + wq].bitcast(F32)
                op("dve", "tensor_tensor", dst, src, rB3, ALU.mult,
                   reads=(xb,) + rbufs[:1], writes=(ly,))
                op("dve", "tensor_tensor", dst, dst.bitcast(F32), mB3,
                   ALU.subtract, reads=rbufs[1:], writes=(ly,))

            ln_stats(half_quarters(xb_t, w0), (xb,), norm1)

        if seq_start:
            op("dve", "tensor_scalar", ly_t[:, :, 0:1], mixk_t[:].unsqueeze(2),
               0.0, None, ALU.mult, reads=(consts,), writes=(ly,))
        else:
            op("dve", "tensor_copy", ly_t[:, :, 0:1], ltail_t[:].unsqueeze(2),
               reads=(ltail,), writes=(ly,))
        op("dve", "tensor_copy", ltail_t[:].unsqueeze(2),
           ly_t[:, :, TILE:TILE + 1].bitcast(F32), reads=(ly,), writes=(ltail,))

        if seq_end:
            op("dve", "tensor_scalar", ly_t[:, :, TILE + 1:TILE + 2],
               mixk_t[:].unsqueeze(2), 0.0, None, ALU.mult,
               reads=(consts,), writes=(ly,))
        else:
            xcol_f = xb_t[:, :, TILE:TILE + 1].bitcast(F32).rearrange(
                "p c o -> p (c o)")
            sqh = fz1_t[:, 0:CHK]
            op("dve", "tensor_tensor", sqh, xcol_f, xcol_f, ALU.mult,
               reads=(xb,), writes=(aFZ1,))
            for ci in range(CHK):
                op("pe", "matmul", pp[2][0:1, 0:1], onesc_t[:].bitcast(F32),
                   xb_t[:, ci, TILE:TILE + 1].bitcast(F32),
                   start=(ci == 0), stop=(ci == CHK - 1),
                   reads=(xb, consts), writes=(ppb[2],))
            for ci in range(CHK):
                op("pe", "matmul", pp[3][0:1, 0:1], onesc_t[:].bitcast(F32),
                   sqh[:, ci:ci + 1].bitcast(F32),
                   start=(ci == 0), stop=(ci == CHK - 1),
                   reads=(aFZ1, consts), writes=(ppb[3],))
            op("dve", "tensor_scalar", r_m[:, 0:1], pp[2][0:1, 0:1], 1.0 / D,
               None, ALU.mult, reads=(ppb[2],), writes=(rows_b,))
            op("dve", "tensor_scalar", r_q[:, 0:1], pp[3][0:1, 0:1], 1.0 / D,
               None, ALU.mult, reads=(ppb[3],), writes=(rows_b,))
            op("dve", "tensor_tensor", r_s[:, 0:1], r_m[:, 0:1], r_m[:, 0:1],
               ALU.mult, reads=(rows_b,), writes=(rows_b,))
            op("dve", "tensor_tensor", r_s[:, 0:1], r_q[:, 0:1], r_s[:, 0:1],
               ALU.subtract, reads=(rows_b,), writes=(rows_b,))
            op("act", "activation", r_s[:, 0:1], r_s[:, 0:1], AF.Ln,
               bias=misc_t[0:1, 1:2], reads=(rows_b, consts), writes=(rows_b,))
            op("act", "activation", rr_rstd[:, 0:1], r_s[:, 0:1], AF.Exp,
               scale=-0.5, reads=(rows_b,), writes=(rows_b,))
            op("dve", "tensor_copy", rr_mrs[:, 0:1], r_m[:, 0:1],
               reads=(rows_b,), writes=(rows_b,))
            op("pe", "matmul", pp[3][:, 0:1], onesr_t[0:1, 0:P],
               rr_mrs[:, 0:1], start=True, stop=True,
               reads=(rows_b, consts), writes=(ppb[3],))
            op("pe", "matmul", pp[3][:, 1:2], onesr_t[0:1, 0:P],
               rr_rstd[:, 0:1], start=True, stop=True,
               reads=(rows_b, consts), writes=(ppb[3],))
            mr_sb = ar_t[:, O_BC:O_BC + 2]
            op("dve", "tensor_copy", mr_sb, pp[3][:, 0:2],
               reads=(ppb[3],), writes=(aBC,))
            op("dve", "tensor_scalar",
               ly_t[:, :, TILE + 1:TILE + 2].rearrange("p c o -> p (c o)"),
               xcol_f, mr_sb[:, 0:1], mr_sb[:, 1:2], ALU.subtract, ALU.mult,
               reads=(xb, aBC), writes=(ly,))

        tap(f"lnx{tile}", ly_t[:, :, 1:TILE + 1].bitcast(F32), ly)

        if seq_start:
            op("dve", "tensor_scalar", acar_t[:], mixk_t[:], 0.0, None,
               ALU.mult, reads=(consts,), writes=(acar,))
            op("dve", "tensor_scalar", bcar_t[:], mixk_t[:], 0.0, None,
               ALU.mult, reads=(consts,), writes=(bcar,))

        dma("gp", wb_t[0][:], chunked_w(wk_h), writes=(wb[0],))
        dma("gp", wb_t[1][:], chunked_w(wv_h), writes=(wb[1],))
        dma("gp", wb_t[2][:], chunked_w(wr_h), writes=(wb[2],))
        dma("gp", xt_t[:], chunked_w(wo_h), writes=(xt,))

        B1f = arf(O_B1, CHK, SUB)
        B2f = arf(O_B2, CHK, SUB)
        Af = arf(O_A, CHK, SUB + 1)
        Bsf = arf(O_Bs, CHK, SUB + 1)
        XKb = arbf(O_XK, CHK, SUB)
        SRb = arbf(O_SR, CHK, SUB)
        EKr = fz1_t[:, 0:CHK * SUB].rearrange("p (c s) -> p c s", s=SUB)
        EKf = EKr.bitcast(F32)
        NUr = fz2_t[:, 0:CHK * SUB].rearrange("p (c s) -> p c s", s=SUB)
        NUf = NUr.bitcast(F32)

        def xx_ap(ci, s0):
            if ci < 2:
                return ly_t[:, ci, s0:s0 + SUB].bitcast(F32)
            if ci < 4:
                return ly_t[:, ci, s0 + 2:s0 + 2 + SUB].bitcast(F32)
            return ly_t[:, ci, 1 + s0:1 + s0 + SUB].bitcast(F32)

        def mix_into(dst3, mix_t, dstbuf, s0):
            for ci in range(CHK):
                op("dve", "scalar_tensor_tensor", dst3[:, ci, :],
                   B2f[:, ci, :], mix_t[:, ci:ci + 1], xx_ap(ci, s0),
                   ALU.mult, ALU.add,
                   reads=(aB2, ly, consts), writes=(dstbuf,))

        def front(s0):
            for (a_, b_), off in zip([(0, 2), (2, 4), (4, 8)], [0, 2, 1]):
                op("dve", "tensor_tensor", B2f[:, a_:b_, :],
                   ly_t[:, a_:b_, 1 + s0:1 + s0 + SUB].bitcast(F32),
                   ly_t[:, a_:b_, off + s0:off + s0 + SUB].bitcast(F32),
                   ALU.subtract, reads=(ly,), writes=(aB2,))
            mix_into(XKb, mixk_t, aXK, s0)

        def proj(mov3, movbuf, wtile, wbuf, evac_m):
            for m in range(CHK):
                bank = m % 2
                for kc in range(CHK):
                    op("pe", "matmul", pp[bank][:, 0:SUB],
                       wtile[:, kc, m * P:(m + 1) * P], mov3[:, kc, :],
                       start=(kc == 0), stop=(kc == CHK - 1),
                       reads=(movbuf, wbuf), writes=(ppb[bank],))
                evac_m(m, bank)

        def ev_copy(dst3, dstbuf, f):
            def e(m, bank):
                op("act", "activation", dst3[:, m, :], pp[bank][:, 0:SUB],
                   f, reads=(ppb[bank],), writes=(dstbuf,))
            return e

        def ev_sig1(m, bank):
            op("act", "activation", SRb[:, m, :], pp[bank][:, 0:SUB],
               AF.Exp, scale=-1.0, reads=(ppb[bank],), writes=(aSR,))

        front(0)
        for sub in range(SUBS):
            s0 = sub * SUB

            proj(XKb, aXK, wb_t[0], wb[0], ev_copy(B1f, aB1, AF.Copy))
            mix_into(XKb, mixv_t, aXK, s0)
            proj(XKb, aXK, wb_t[1], wb[1], ev_copy(NUr, aFZ2, AF.Copy))
            mix_into(XKb, mixr_t, aXK, s0)

            proj(XKb, aXK, wb_t[2], wb[2], ev_sig1)
            op("act", "activation", SRb[:], SRb[:], AF.Ln,
               bias=misc_t[:, 0:1], reads=(consts,), writes=(aSR,))
            op("act", "activation", SRb[:], SRb[:], AF.Exp, scale=-1.0,
               writes=(aSR,))

            op("act", "activation", EKr[:], B1f[:], AF.Exp,
               reads=(aB1,), writes=(aFZ1,))
            for ci in range(CHK):
                op("act", "activation", B1f[:, ci, :], B1f[:, ci, :], AF.Exp,
                   bias=uT_t[:, ci:ci + 1], reads=(ewu,), writes=(aB1,))
            op("dve", "tensor_copy", Bsf[:, :, 0:1], bcar_t[:].unsqueeze(2),
               reads=(bcar,), writes=(aBs,))
            for ci in range(CHK):
                op("dve", "tensor_tensor_scan", Bsf[:, ci, 1:SUB + 1],
                   ew_t[:, ci:ci + 1].broadcast_to([P, SUB]), EKf[:, ci, :],
                   bcar_t[:, ci:ci + 1], ALU.mult, ALU.add,
                   reads=(ewu, aFZ1, bcar), writes=(aBs,))
            op("dve", "tensor_copy", bcar_t[:].unsqueeze(2),
               Bsf[:, :, SUB:SUB + 1], reads=(aBs,), writes=(bcar,))
            op("dve", "tensor_tensor", EKr[:], EKf[:], NUf[:], ALU.mult,
               reads=(aFZ2,), writes=(aFZ1,))
            op("dve", "tensor_tensor", NUr[:], B1f[:], NUf[:], ALU.mult,
               reads=(aB1,), writes=(aFZ2,))
            op("dve", "tensor_tensor", B1f[:], B1f[:], Bsf[:, :, 0:SUB],
               ALU.add, reads=(aBs,), writes=(aB1,))
            op("act", "activation", B1f[:], B1f[:], AF.Ln, writes=(aB1,))
            op("act", "activation", B1f[:], B1f[:], AF.Exp, scale=-1.0,
               writes=(aB1,))
            op("dve", "tensor_copy", Af[:, :, 0:1], acar_t[:].unsqueeze(2),
               reads=(acar,), writes=(aA,))
            for ci in range(CHK):
                op("dve", "tensor_tensor_scan", Af[:, ci, 1:SUB + 1],
                   ew_t[:, ci:ci + 1].broadcast_to([P, SUB]), EKf[:, ci, :],
                   acar_t[:, ci:ci + 1], ALU.mult, ALU.add,
                   reads=(ewu, aFZ1, acar), writes=(aA,))
            op("dve", "tensor_copy", acar_t[:].unsqueeze(2),
               Af[:, :, SUB:SUB + 1], reads=(aA,), writes=(acar,))
            op("dve", "tensor_tensor", NUr[:], NUf[:], Af[:, :, 0:SUB],
               ALU.add, reads=(aA,), writes=(aFZ2,))
            op("dve", "tensor_tensor", NUr[:], NUf[:], B1f[:], ALU.mult,
               reads=(aB1,), writes=(aFZ2,))
            op("dve", "tensor_tensor", XKb[:], NUf[:], SRb[:], ALU.mult,
               reads=(aSR, aFZ2), writes=(aXK,))

            proj(XKb, aXK, xt_t, xt, ev_copy(NUr, aFZ2, AF.Copy))

            if sub + 1 < SUBS:
                front(s0 + SUB)

            def norm2(qi, c0, wq, rB, mB, rbufs, s0=s0):
                rB3 = rB.unsqueeze(1).broadcast_to([P, CHK, wq])
                mB3 = mB.unsqueeze(1).broadcast_to([P, CHK, wq])
                op("dve", "tensor_tensor", B1f[:], NUf[:], rB3, ALU.mult,
                   reads=(aFZ2,) + rbufs[:1], writes=(aB1,))
                op("dve", "tensor_tensor", B1f[:], B1f[:], mB3, ALU.subtract,
                   reads=rbufs[1:], writes=(aB1,))
                xs = xb_t[:, :, s0:s0 + SUB]
                op("dve", "tensor_tensor", xs, xs.bitcast(F32), B1f[:],
                   ALU.add, reads=(aB1,), writes=(xb,))

            ln_stats([([NUr[:, ci, :] for ci in range(CHK)],
                       NUf[:], SUB, 0)], (aFZ2,), norm2)

        arena_switch(PHASE_B, PHASE_A)

        GH = arbf(O_GH, CHK, HN)
        S1a = arf2(O_S1, HN)
        GP3 = arf2(O_GP3, HN)
        GE = arf2(O_GE, D)
        GLl = arf(O_GL, CHK, E)
        GLsc = arf2(O_GL + 64, 256)
        T1 = arf(O_T1, CHK, HN)
        yv = ly_t[:, :, 0:TILE].bitcast(F32)
        yv3 = ly_t[:, :, 0:TILE]

        for nh in range(2):
            w0 = nh * HN

            def norm3(qi, c0, wq, rB, mB, rbufs, w0=w0):
                rB3 = rB.unsqueeze(1).broadcast_to([P, CHK, wq])
                mB3 = mB.unsqueeze(1).broadcast_to([P, CHK, wq])
                t1q = T1[:, :, c0:c0 + wq]
                op("dve", "tensor_tensor", t1q,
                   xb_t[:, :, w0 + c0:w0 + c0 + wq].bitcast(F32), rB3,
                   ALU.mult, reads=(xb,) + rbufs[:1],
                   writes=(aT1, aGH, aS1, aGP3, aGE))
                op("dve", "tensor_tensor", xt_t[:, :, w0 + c0:w0 + c0 + wq],
                   t1q, mB3, ALU.subtract, reads=(aT1,) + rbufs[1:],
                   writes=(xt,))

            ln_stats(half_quarters(xb_t, w0), (xb,), norm3)

        for tc in range(CHK):
            for kc in range(CHK):
                op("pe", "matmul", pp[2][:, tc * E:(tc + 1) * E],
                   xt_t[:, kc, tc * P:(tc + 1) * P], gwt_t[:, kc, :],
                   start=(kc == 0), stop=(kc == CHK - 1),
                   reads=(xt, consts), writes=(ppb[2],))
        op("dve", "tensor_tensor", GLl[:],
           pp[2][:, 0:64].rearrange("p (c e) -> p c e", e=E),
           gbB_t[:].unsqueeze(1).broadcast_to([P, CHK, E]), ALU.add,
           reads=(ppb[2], gbB_b), writes=(aGL,))
        gm1 = GLsc[:, 0:64].rearrange("p (c e) -> p c e", e=E)
        gtmp = GLsc[:, 64:128].rearrange("p (c e) -> p c e", e=E)
        gmax = GLsc[:, 128:136].rearrange("p (c e) -> p c e", e=1)
        gsum = GLsc[:, 136:144].rearrange("p (c e) -> p c e", e=1)
        gsum2 = GLsc[:, 136:144]
        op("dve", "tensor_reduce", gmax, GLl[:], AX.X, ALU.max,
           reads=(aGL,), writes=(aGL,))
        op("dve", "tensor_tensor", GLl[:], GLl[:],
           gmax.broadcast_to([P, CHK, E]), ALU.subtract, writes=(aGL,))
        op("act", "activation", arf2(O_GL, 64), arf2(O_GL, 64), AF.Exp,
           reads=(aGL,), writes=(aGL,))
        op("dve", "tensor_reduce", gsum, GLl[:], AX.X, ALU.add, writes=(aGL,))
        op("dve", "tensor_reduce", gmax, GLl[:], AX.X, ALU.max, writes=(aGL,))
        op("dve", "tensor_tensor", gm1[:], GLl[:],
           gmax.broadcast_to([P, CHK, E]), ALU.is_ge, writes=(aGL,))
        op("dve", "tensor_scalar", gm1[:], gm1[:], 1e30, None, ALU.mult,
           writes=(aGL,))
        op("dve", "tensor_tensor", gtmp[:], GLl[:], gm1[:], ALU.subtract,
           writes=(aGL,))
        op("dve", "tensor_reduce", gmax, gtmp[:], AX.X, ALU.max, writes=(aGL,))
        op("dve", "tensor_tensor", gm1[:], GLl[:],
           gmax.broadcast_to([P, CHK, E]), ALU.is_ge, writes=(aGL,))
        op("dve", "tensor_tensor", gm1[:], gm1[:], GLl[:], ALU.mult,
           writes=(aGL,))
        op("act", "activation", gsum2, gsum2, AF.Ln, writes=(aGL,))
        op("act", "activation", gsum2, gsum2, AF.Exp, scale=-1.0, writes=(aGL,))
        op("dve", "tensor_tensor", gm1[:], gm1[:],
           gsum.broadcast_to([P, CHK, E]), ALU.mult, writes=(aGL,))

        for tc in range(CHK):
            bank = 4 + (tc // 4)
            op("pe", "transpose", pp[bank][0:E, (tc % 4) * P:(tc % 4) * P + P],
               gm1[:, tc, :], ident_t[:],
               reads=(aGL, consts), writes=(ppb[bank],))
        for half in range(2):
            op("dve", "tensor_copy", gfm_t[:, half * HN:(half + 1) * HN],
               pp[4 + half][0:E, :], reads=(ppb[4 + half],), writes=(gfm_b,))

        for ex in range(NEXP):
            shared = ex >= E
            h = ex - E
            if shared:
                w1src = chunked_w(ws1_h[:, h * D:(h + 1) * D])
                w3src = chunked_w(ws3_h[:, h * D:(h + 1) * D])
                w2src = chunked_w(ws2_h[h * D:(h + 1) * D, :])
            else:
                w1src = chunked_w(w1_h[ex])
                w3src = chunked_w(w3_h[ex])
                w2src = chunked_w(w2_h[ex])
            dma("gp", wb_t[0][:], w1src, writes=(wb[0],))
            dma("gp", wb_t[1][:], w3src, writes=(wb[1],))
            dma("gp", wb_t[2][:], w2src, writes=(wb[2],))

            if not shared:
                for nh in range(2):
                    op("pe", "matmul", pp[4 + nh][:, 0:HN],
                       oh_t[:, ex * P:(ex + 1) * P],
                       gfm_t[:, nh * HN:(nh + 1) * HN], start=True, stop=True,
                       reads=(gfm_b, consts), writes=(ppb[4 + nh],))
                    op("dve", "tensor_copy", GE[:, nh * HN:(nh + 1) * HN],
                       pp[4 + nh][:, 0:HN], reads=(ppb[4 + nh],), writes=(aGE,))

            for nh in range(2):
                w0 = nh * HN
                for m in range(CHK):
                    bA = 4 + (m % 2)
                    bB_ = 6 + (m % 2)
                    for kc in range(CHK):
                        op("pe", "matmul", pp[bA][:, 0:HN],
                           wb_t[0][:, kc, m * P:(m + 1) * P],
                           xt_t[:, kc, w0:w0 + HN],
                           start=(kc == 0), stop=(kc == CHK - 1),
                           reads=(wb[0], xt), writes=(ppb[bA],))
                    for kc in range(CHK):
                        op("pe", "matmul", pp[bB_][:, 0:HN],
                           wb_t[1][:, kc, m * P:(m + 1) * P],
                           xt_t[:, kc, w0:w0 + HN],
                           start=(kc == 0), stop=(kc == CHK - 1),
                           reads=(wb[1], xt), writes=(ppb[bB_],))
                    if shared:
                        b1col = bs1_t[:, h * CHK + m:h * CHK + m + 1]
                        b3col = bs3_t[:, h * CHK + m:h * CHK + m + 1]
                    else:
                        b1col = b1_t[:, ex * CHK + m:ex * CHK + m + 1]
                        b3col = b3_t[:, ex * CHK + m:ex * CHK + m + 1]
                    op("act", "activation", S1a[:], pp[bA][:, 0:HN], AF.Silu,
                       bias=b1col, reads=(ppb[bA], consts), writes=(aS1,))
                    if shared:
                        op("dve", "tensor_scalar", GP3[:], pp[bB_][:, 0:HN],
                           b3col, None, ALU.add,
                           reads=(ppb[bB_], consts), writes=(aGP3,))
                    else:
                        op("dve", "scalar_tensor_tensor", GP3[:],
                           pp[bB_][:, 0:HN], b3col, GE[:, w0:w0 + HN],
                           ALU.add, ALU.mult,
                           reads=(ppb[bB_], consts, aGE), writes=(aGP3,))
                    op("dve", "tensor_tensor", GH[:, m, :], S1a[:], GP3[:],
                       ALU.mult, reads=(aS1, aGP3), writes=(aGH,))
                for c in range(CHK):
                    bank = c % 2
                    if ex == 0:
                        op("pe", "matmul", pp[bank][:, 0:HN],
                           b2_t[:, c * P:(c + 1) * P],
                           gfm_t[:, w0:w0 + HN], start=True, stop=False,
                           reads=(gfm_b, consts), writes=(ppb[bank],))
                        op("pe", "matmul", pp[bank][:, 0:HN],
                           bs2_t[0:1, c * P:(c + 1) * P], onesr_t[:],
                           start=False, stop=False,
                           reads=(consts,), writes=(ppb[bank],))
                    for kc in range(CHK):
                        op("pe", "matmul", pp[bank][:, 0:HN],
                           wb_t[2][:, kc, c * P:(c + 1) * P], GH[:, kc, :],
                           start=(kc == 0 and ex != 0),
                           stop=(kc == CHK - 1),
                           reads=(wb[2], aGH), writes=(ppb[bank],))
                    if ex == 0:
                        op("act", "activation", yv3[:, c, w0:w0 + HN],
                           pp[bank][:, 0:HN], AF.Copy,
                           reads=(ppb[bank],), writes=(ly,))
                    else:
                        op("dve", "tensor_tensor", yv3[:, c, w0:w0 + HN],
                           yv[:, c, w0:w0 + HN], pp[bank][:, 0:HN],
                           ALU.add, reads=(ppb[bank],), writes=(ly,))

        tap(f"y{tile}", ly_t[:, :, 0:TILE].bitcast(F32), ly)

        for nh in range(2):
            w0 = nh * HN

            def norm4(qi, c0, wq, rB, mB, rbufs, w0=w0):
                rB3 = rB.unsqueeze(1).broadcast_to([P, CHK, wq])
                mB3 = mB.unsqueeze(1).broadcast_to([P, CHK, wq])
                t1q = T1[:, :, c0:c0 + wq]
                op("dve", "tensor_tensor", t1q, yv[:, :, w0 + c0:w0 + c0 + wq],
                   rB3, ALU.mult, reads=(ly,) + rbufs[:1],
                   writes=(aT1, aGH, aS1, aGP3, aGE))
                op("dve", "tensor_tensor", t1q, t1q, mB3, ALU.subtract,
                   reads=rbufs[1:], writes=(aT1,))
                xs = xb_t[:, :, w0 + c0:w0 + c0 + wq]
                op("dve", "tensor_tensor", xs, xs.bitcast(F32), t1q, ALU.add,
                   reads=(aT1,), writes=(xb,))

            ln_stats(half_quarters(ly_t, w0), (ly,), norm4)

        tok = dma("sp", out3[:, :, t0:t0 + TILE],
                  xb_t[:, :, 0:TILE].bitcast(F32), reads=(xb,))
        out_toks.append(tok)

        tap(f"x{tile}", xb_t[:, :, 0:TILE].bitcast(F32), xb)

    for t in out_toks:
        S.want("sp", t)
    S.emit("sp", "nop", (), {})

    S.finalize()

    import contextlib
    with contextlib.ExitStack() as stack:
        sems = {}
        for s in ENGS:
            sems[s] = stack.enter_context(nc.semaphore("sem_" + s))
        block = stack.enter_context(nc.Block())

        @block.sync
        def _(En):
            S.replay(sems, "sp", En)

        @block.gpsimd
        def _(En):
            S.replay(sems, ("gp", "gpc"), En)

        @block.tensor
        def _(En):
            S.replay(sems, "pe", En)

        @block.scalar
        def _(En):
            S.replay(sems, "act", En)

        @block.vector
        def _(En):
            S.replay(sems, "dve", En)

    for c in reversed(ctx):
        c.__exit__(None, None, None)
    return nc


def _onehot_blocks():
    oh = np.zeros((E, E * P), dtype=np.float32)
    for e in range(E):
        oh[e, e * P:(e + 1) * P] = 1.0
    return oh


def _host_prep(inputs):
    gi = lambda k: np.asarray(inputs[k])
    x = gi("L0_attn_x")

    def fm(vec):
        return np.ascontiguousarray(vec.reshape(CHK, P).T)

    com = {
        "wk": np.ascontiguousarray(gi("L0_attn_Wk").T),
        "wv": np.ascontiguousarray(gi("L0_attn_Wv").T),
        "wr": np.ascontiguousarray(gi("L0_attn_Wr").T),
        "wo": np.ascontiguousarray(gi("L0_attn_Wo").T),
        "w1": np.ascontiguousarray(np.transpose(gi("L0_ffn_w1"), (0, 2, 1))),
        "w3": np.ascontiguousarray(np.transpose(gi("L0_ffn_w3"), (0, 2, 1))),
        "w2": np.ascontiguousarray(np.transpose(gi("L0_ffn_w2"), (0, 2, 1))),
        "ws1": np.ascontiguousarray(gi("L0_ffn_ws1").T),
        "ws3": np.ascontiguousarray(gi("L0_ffn_ws3").T),
        "ws2": np.ascontiguousarray(gi("L0_ffn_ws2").T),
        "gwt": np.ascontiguousarray(gi("L0_ffn_gate_w").T),
        "mixk": fm(gi("L0_attn_mix_k")),
        "mixv": fm(gi("L0_attn_mix_v")),
        "mixr": fm(gi("L0_attn_mix_r")),
        "sd": fm(gi("L0_attn_spatial_decay")),
        "sf": fm(gi("L0_attn_spatial_first")),
        "b1f": np.ascontiguousarray(
            gi("L0_ffn_b1").reshape(E, CHK, P).transpose(2, 0, 1).reshape(P, E * CHK)),
        "b3f": np.ascontiguousarray(
            gi("L0_ffn_b3").reshape(E, CHK, P).transpose(2, 0, 1).reshape(P, E * CHK)),
        "bs1f": np.ascontiguousarray(
            gi("L0_ffn_bs1").reshape(2, CHK, P).transpose(2, 0, 1).reshape(P, 2 * CHK)),
        "bs3f": np.ascontiguousarray(
            gi("L0_ffn_bs3").reshape(2, CHK, P).transpose(2, 0, 1).reshape(P, 2 * CHK)),
        "b2": np.ascontiguousarray(gi("L0_ffn_b2")),
        "bs2": np.ascontiguousarray(gi("L0_ffn_bs2").reshape(1, D)),
        "gb": np.ascontiguousarray(gi("L0_ffn_gate_b").reshape(1, E)),
        "ident": np.eye(P, dtype=np.float32),
        "oh": _onehot_blocks(),
        "onesr": np.ones((1, HN), dtype=np.float32),
        "onesc": np.ones((P, 1), dtype=np.float32),
        "misc": np.concatenate([np.ones((P, 1), np.float32),
                                np.full((P, 1), EPS, np.float32)], axis=1),
    }
    for k in com:
        com[k] = np.ascontiguousarray(com[k].astype(np.float32, copy=False))

    in_maps = []
    for c in range(NCORES):
        xc = np.ascontiguousarray(
            np.concatenate([x[2 * c].T, x[2 * c + 1].T], axis=1)).astype(
                np.float32, copy=False)
        m = {"xin": xc}
        m.update(com)
        in_maps.append(m)
    return in_maps


def kernel(**inputs):
    in_maps = _host_prep(inputs)
    nc = build_program()
    res = run_bass_kernel_spmd(nc, in_maps, core_ids=list(range(NCORES)))
    LAST_DEBUG.clear()
    full = np.empty((NB, T, D), dtype=np.float32)
    for c in range(NCORES):
        r = res.results[c]
        for k, v in r.items():
            if k.startswith("dbg_"):
                LAST_DEBUG[(c, k)] = np.asarray(v)
        o = np.asarray(r["out"])
        full[2 * c] = o[:, 0:T].T
        full[2 * c + 1] = o[:, T:2 * T].T
    return full


# version 26
# speedup vs baseline: 1.0847x; 1.0371x over previous
import numpy as np
import concourse.bass as bass
import concourse.mybir as mybir
from concourse.bass_utils import run_bass_kernel_spmd

F32 = mybir.dt.float32
BF16 = mybir.dt.bfloat16
F32R = mybir.dt.float32r
AF = mybir.ActivationFunctionType
ALU = mybir.AluOpType
AX = mybir.AxisListType

P = 128
CHK = 8
D = 1024
T = 2048
NB = 16
NCORES = 8
SEQ = 2
TOK = SEQ * T
TILES = 4
TILE = 1024
SUBS = 4
SUB = 256
HN = 512
QW = 256
E = 8
NEXP = 10
EPS = 1e-5
INVT = 1.0 / T

DEBUG_TAPS = []
LAST_DEBUG = {}

ENGS = ("sp", "gp", "gpc", "pe", "act", "dve")
DMA_SEMS = ("sp", "gp")


class Sched:
    def __init__(self):
        self.ops = {e: [] for e in ENGS}
        self._pending = {}
        self._seq = 0

    UNSAFE_METHODS = ("tensor_tensor_scan", "scalar_tensor_tensor")

    SAME_SKIP = ("pe",)

    def want(self, sem, token, force=False):
        if token is None:
            return
        tsem, tidx = token
        if (tsem == sem and tsem not in DMA_SEMS and not force
                and tsem in self.SAME_SKIP):
            prod = self.ops[tsem][tidx]["m"]
            if prod not in self.UNSAFE_METHODS:
                return
        self._pending.setdefault(sem, []).append((tsem, tidx))

    def emit(self, sem, method, args, kwargs):
        w = self._pending.pop(sem, [])
        idx = len(self.ops[sem])
        self._seq += 1
        self.ops[sem].append(
            {"m": method, "a": args, "k": kwargs, "flag": False, "waits": w,
             "seq": self._seq})
        for (ts, ti) in w:
            self.ops[ts][ti]["flag"] = True
        return (sem, idx)

    def finalize(self):
        self.ms = {}
        for s in ENGS:
            c = 0
            nums = []
            for o in self.ops[s]:
                if o["flag"]:
                    c += 1
                nums.append(c)
            self.ms[s] = nums

    def replay(self, sems, semlist, engine):
        if isinstance(semlist, str):
            semlist = (semlist,)
        stream = []
        for sm in semlist:
            for o in self.ops[sm]:
                stream.append((o["seq"], sm, o))
        stream.sort()
        lastw = {}
        for _, sm, o in stream:
            for (ts, ti) in o["waits"]:
                v = self.ms[ts][ti]
                step = 16 if ts in DMA_SEMS else 1
                if lastw.get(ts, 0) >= v:
                    continue
                lastw[ts] = v
                engine.wait_ge(sems[ts], v * step)
            instr = getattr(engine, o["m"])(*o["a"], **o["k"])
            if o["flag"]:
                step = 16 if sm in DMA_SEMS else 1
                instr.then_inc(sems[sm], step)


class Buf:
    __slots__ = ("name", "last_w", "readers", "frozen")

    def __init__(self, name, frozen=False):
        self.name = name
        self.last_w = None
        self.readers = []
        self.frozen = frozen


def build_program():
    nc = bass.Bass("TRN2", target_bir_lowering=False)
    S = Sched()
    ctx = []

    def alloc(name, shape, dt):
        t = nc.sbuf_tensor(name, shape, dt)
        h = t.__enter__()
        ctx.append(t)
        return h

    def palloc(name, shape, dt):
        t = nc.psum_tensor(name, shape, dt)
        h = t.__enter__()
        ctx.append(t)
        return h

    def op(sem, method, *args, reads=(), writes=(), **kwargs):
        force = method in Sched.UNSAFE_METHODS
        for b in reads:
            S.want(sem, b.last_w, force)
        for b in writes:
            S.want(sem, b.last_w, force)
            for t in b.readers:
                S.want(sem, t, force)
        tok = S.emit(sem, method, args, kwargs)
        for b in reads:
            if not b.frozen:
                b.readers.append(tok)
        for b in writes:
            b.last_w = tok
            b.readers = []
        return tok

    def din(name, shape, dt=F32):
        return nc.dram_tensor(name, shape, dt, kind="ExternalInput").ap()

    xin = din("xin", [D, TOK])
    out = nc.dram_tensor("out", [D, TOK], F32, kind="ExternalOutput").ap()
    wk_h = din("wk", [D, D]); wv_h = din("wv", [D, D])
    wr_h = din("wr", [D, D]); wo_h = din("wo", [D, D])
    w1_h = din("w1", [E, D, D]); w3_h = din("w3", [E, D, D]); w2_h = din("w2", [E, D, D])
    ws1_h = din("ws1", [D, 2 * D]); ws3_h = din("ws3", [D, 2 * D]); ws2_h = din("ws2", [2 * D, D])
    gwt_h = din("gwt", [D, E])
    mixk_h = din("mixk", [P, CHK]); mixv_h = din("mixv", [P, CHK]); mixr_h = din("mixr", [P, CHK])
    sd_h = din("sd", [P, CHK]); sf_h = din("sf", [P, CHK])
    b1_h = din("b1f", [P, E * CHK]); b3_h = din("b3f", [P, E * CHK])
    bs1_h = din("bs1f", [P, 2 * CHK]); bs3_h = din("bs3f", [P, 2 * CHK])
    b2_h = din("b2", [E, D]); bs2_h = din("bs2", [1, D]); gb_h = din("gb", [1, E])
    ident_h = din("ident", [P, P]); oh_h = din("oh", [E, E * P])
    onesr_h = din("onesr", [1, HN]); onesc_h = din("onesc", [P, 1])
    misc_h = din("misc", [P, 2])

    dbg_out = {}
    for tag in DEBUG_TAPS:
        w = CHK * (SUB + 1) if tag.startswith("s_") else CHK * TILE
        dbg_out[tag] = nc.dram_tensor("dbg_" + tag, [P, w], F32,
                                      kind="ExternalOutput").ap()

    xb_t = alloc("xb", [P, CHK, TILE + 1], F32R)
    ly_t = alloc("ly", [P, CHK, TILE + 2], F32R)
    xt_t = alloc("xt", [P, CHK, D], BF16)
    wb_t = [alloc(f"wb{i}", [P, CHK, D], BF16) for i in range(3)]
    AR = 10256
    ar_t = alloc("arena", [P, AR], F32)
    fz1_t = alloc("fz1", [P, 2048], F32R)
    fz2_t = alloc("fz2", [P, 2048], F32R)
    rows_t = alloc("rows", [1, 1024], F32)
    gfm_t = alloc("gfm", [E, D], F32R)
    onesc_t = alloc("onescs", [P, 1], F32R)
    onesr_t = alloc("onesrs", [1, HN], F32)
    ident_t = alloc("idents", [P, P], F32)
    mixk_t = alloc("mixks", [P, CHK], F32)
    mixv_t = alloc("mixvs", [P, CHK], F32)
    mixr_t = alloc("mixrs", [P, CHK], F32)
    sd_t = alloc("sds", [P, CHK], F32)
    sf_t = alloc("sfs", [P, CHK], F32)
    ew_t = alloc("ews", [P, CHK], F32)
    uT_t = alloc("uTs", [P, CHK], F32)
    misc_t = alloc("miscs", [P, 2], F32)
    b1_t = alloc("b1s", [P, E * CHK], F32)
    b3_t = alloc("b3s", [P, E * CHK], F32)
    bs1_t = alloc("bs1s", [P, 2 * CHK], F32)
    bs3_t = alloc("bs3s", [P, 2 * CHK], F32)
    b2_t = alloc("b2s", [E, D], F32R)
    bs2_t = alloc("bs2s", [1, D], F32)
    gbB_t = alloc("gbBs", [P, E], F32)
    gbrow_t = alloc("gbrows", [1, E], F32)
    gwt_t = alloc("gwts", [P, CHK, E], BF16)
    oh_t = alloc("ohs", [E, E * P], F32R)
    acar_t = alloc("acars", [P, CHK], F32)
    bcar_t = alloc("bcars", [P, CHK], F32)
    ltail_t = alloc("ltails", [P, CHK], F32)
    pp = [palloc(f"pp{i}", [P, HN], F32) for i in range(8)]

    xb = Buf("xb"); ly = Buf("ly"); xt = Buf("xt")
    wb = [Buf(f"wb{i}") for i in range(3)]
    ppb = [Buf(f"pp{i}") for i in range(8)]
    rows_b = Buf("rows"); gfm_b = Buf("gfm")
    acar = Buf("acar"); bcar = Buf("bcar"); ltail = Buf("ltail")
    gbB_b = Buf("gbB")
    consts = Buf("consts", frozen=True)
    ewu = Buf("ewu")
    aB1 = Buf("aB1"); aB2 = Buf("aB2")
    aA = Buf("aA"); aBs = Buf("aBs"); aXK = Buf("aXK")
    aFZ1 = Buf("aFZ1"); aFZ2 = Buf("aFZ2")
    aSR = Buf("aSR")
    aGH = Buf("aGH"); aS1 = Buf("aS1"); aGP3 = Buf("aGP3"); aGE = Buf("aGE")
    aGL = Buf("aGL"); aBC2 = Buf("aBC2"); aT1 = Buf("aT1")
    aBC = aB2
    aXK2 = aXK
    PHASE_A = [aB1, aB2, aA, aBs, aXK, aSR]
    PHASE_B = [aGH, aS1, aGP3, aGE, aGL, aBC2, aT1]

    O_B1, O_B2 = 0, 2048
    O_A, O_Bs = 4096, 6152
    O_XK = 8208
    O_SR = 9232
    O_BC = 2048
    O_GH, O_S1, O_GP3, O_GE = 0, 2048, 2560, 3072
    O_GL = 4096
    O_BC2 = 4608
    O_T1 = 0

    def arena_switch(new, old):
        toks = []
        for b in old:
            if b.last_w is not None:
                toks.append(b.last_w)
            toks.extend(b.readers)
            b.last_w = None
            b.readers = []
        for b in new:
            b.last_w = None
            b.readers = list(toks)

    def arf(off, chk, w):
        return ar_t[:, off:off + chk * w].rearrange("p (c s) -> p c s", s=w)

    def arr(off, chk, w):
        return ar_t[:, off:off + chk * w].bitcast(F32R).rearrange(
            "p (c s) -> p c s", s=w)

    def arbf(off, chk, w):
        return ar_t[:, off:off + chk * w // 2].bitcast(BF16).rearrange(
            "p (c s) -> p c s", s=w)

    def arf2(off, w):
        return ar_t[:, off:off + w]

    def dma(q, out_ap, in_ap, reads=(), writes=()):
        return op(q, "dma_start", reads=reads, writes=writes,
                  out=out_ap, in_=in_ap)

    def stap(name, src3_f32, srcbuf, w):
        tag = "s_" + name
        if tag in dbg_out:
            dma("sp", dbg_out[tag][:, 0:CHK * w].rearrange(
                "p (c t) -> p c t", t=w), src3_f32, reads=(srcbuf,))

    def tap(tag, src3_f32, srcbuf):
        if tag in dbg_out:
            dma("sp", dbg_out[tag].rearrange("p (c t) -> p c t", t=TILE),
                src3_f32, reads=(srcbuf,))

    dma("sp", ident_t[:], ident_h, writes=(consts,))
    dma("sp", mixk_t[:], mixk_h, writes=(consts,))
    dma("sp", mixv_t[:], mixv_h, writes=(consts,))
    dma("sp", mixr_t[:], mixr_h, writes=(consts,))
    dma("sp", sd_t[:], sd_h, writes=(consts,))
    dma("sp", sf_t[:], sf_h, writes=(consts,))
    dma("sp", misc_t[:], misc_h, writes=(consts,))
    dma("sp", b1_t[:], b1_h, writes=(consts,))
    dma("sp", b3_t[:], b3_h, writes=(consts,))
    dma("sp", bs1_t[:], bs1_h, writes=(consts,))
    dma("sp", bs3_t[:], bs3_h, writes=(consts,))
    dma("gp", onesc_t[:], onesc_h, writes=(consts,))
    dma("sp", onesr_t[:], onesr_h, writes=(consts,))
    dma("gp", b2_t[:], b2_h, writes=(consts,))
    dma("sp", bs2_t[:], bs2_h, writes=(consts,))
    dma("sp", gbrow_t[:], gb_h, writes=(consts,))
    dma("gp", gwt_t[:], gwt_h.rearrange("(c p) e -> p c e", p=P), writes=(consts,))
    dma("gp", oh_t[:], oh_h, writes=(consts,))

    op("act", "activation", ew_t[:], sd_t[:], AF.Exp, scale=INVT,
       reads=(consts,), writes=(ewu,))
    op("dve", "tensor_scalar", uT_t[:], sf_t[:], INVT, None, ALU.mult,
       reads=(consts,), writes=(ewu,))
    op("pe", "matmul", pp[3][0:P, 0:E], onesr_t[0:1, 0:P], gbrow_t[:],
       start=True, stop=True, reads=(consts,), writes=(ppb[3],))
    op("dve", "tensor_copy", gbB_t[:], pp[3][0:P, 0:E],
       reads=(ppb[3],), writes=(gbB_b,))

    xin3 = xin.rearrange("(c p) t -> p c t", p=P)
    out3 = out.rearrange("(c p) t -> p c t", p=P)

    def chunked_w(h2d):
        return h2d.rearrange("(c p) n -> p c n", p=P)

    r_m = rows_t[0:1, 0:QW]
    r_q = rows_t[0:1, QW:2 * QW]
    r_s = rows_t[0:1, 2 * QW:3 * QW]
    rr_rstd = rows_t[0:1, 3 * QW:4 * QW]
    rr_mrs = r_s

    def ln_stats(quarters, dat_bufs, norm):
        def banks(qi):
            return ((pp[2], 2, pp[3], 3) if qi % 2 == 0
                    else (pp[6], 6, pp[7], 7))

        def stats(qi):
            chunks, f3d, wq, c0 = quarters[qi]
            ppS, ppSi, ppB_, ppBi = banks(qi)
            sq3 = fz1_t[:, 0:CHK * wq].rearrange("p (c s) -> p c s", s=wq)
            op("dve", "tensor_tensor", sq3, f3d, f3d, ALU.mult,
               reads=tuple(dat_bufs), writes=(aFZ1,))
            for ci in range(CHK):
                op("pe", "matmul", ppS[0:1, 0:wq], onesc_t[:], chunks[ci],
                   start=(ci == 0), stop=(ci == CHK - 1),
                   reads=tuple(dat_bufs) + (consts,), writes=(ppb[ppSi],))
            for ci in range(CHK):
                op("pe", "matmul", ppB_[0:1, 0:wq], onesc_t[:],
                   sq3[:, ci, :], start=(ci == 0), stop=(ci == CHK - 1),
                   reads=(aFZ1, consts), writes=(ppb[ppBi],))

        def tail(qi):
            chunks, f3d, wq, c0 = quarters[qi]
            ppS, ppSi, ppB_, ppBi = banks(qi)
            op("dve", "tensor_scalar", r_m[:, 0:wq], ppS[0:1, 0:wq],
               1.0 / D, None, ALU.mult, reads=(ppb[ppSi],), writes=(rows_b,))
            op("dve", "tensor_scalar", r_q[:, 0:wq], ppB_[0:1, 0:wq],
               1.0 / D, None, ALU.mult, reads=(ppb[ppBi],), writes=(rows_b,))
            op("dve", "tensor_tensor", r_s[:, 0:wq], r_m[:, 0:wq], r_m[:, 0:wq],
               ALU.mult, reads=(rows_b,), writes=(rows_b,))
            op("dve", "tensor_tensor", r_s[:, 0:wq], r_q[:, 0:wq], r_s[:, 0:wq],
               ALU.subtract, reads=(rows_b,), writes=(rows_b,))
            op("act", "activation", r_s[:, 0:wq], r_s[:, 0:wq], AF.Ln,
               bias=misc_t[0:1, 1:2], reads=(rows_b, consts), writes=(rows_b,))
            op("act", "activation", rr_rstd[:, 0:wq], r_s[:, 0:wq], AF.Exp,
               scale=-0.5, reads=(rows_b,), writes=(rows_b,))
            op("dve", "tensor_tensor", rr_mrs[:, 0:wq], r_m[:, 0:wq],
               rr_rstd[:, 0:wq], ALU.mult, reads=(rows_b,), writes=(rows_b,))
            op("pe", "matmul", ppS[:, 0:wq], onesr_t[0:1, 0:P],
               rr_rstd[:, 0:wq], start=True, stop=True,
               reads=(rows_b, consts), writes=(ppb[ppSi],))
            op("pe", "matmul", ppB_[:, 0:wq], onesr_t[0:1, 0:P],
               rr_mrs[:, 0:wq], start=True, stop=True,
               reads=(rows_b, consts), writes=(ppb[ppBi],))
            norm(qi, c0, wq, ppS[:, 0:wq], ppB_[:, 0:wq],
                 (ppb[ppSi], ppb[ppBi]))

        for qi in range(len(quarters)):
            stats(qi)
            tail(qi)

    def half_quarters(src_t, base):
        qs = []
        for q in range(2):
            c0 = base + q * QW
            qs.append(([src_t[:, ci, c0:c0 + QW] for ci in range(CHK)],
                       src_t[:, :, c0:c0 + QW].bitcast(F32), QW, q * QW))
        return qs

    out_toks = []
    for tile in range(TILES):
        t0 = tile * TILE
        seq_start = (tile % 2 == 0)
        seq_end = (tile % 2 == 1)

        ncols = TILE + (0 if seq_end else 1)
        dma("gp", xb_t[:, :, 0:ncols], xin3[:, :, t0:t0 + ncols], writes=(xb,))

        arena_switch(PHASE_A, PHASE_B)

        for nh in range(2):
            w0 = nh * HN

            def norm1(qi, c0, wq, rB, mB, rbufs, w0=w0):
                rB3 = rB.unsqueeze(1).broadcast_to([P, CHK, wq])
                mB3 = mB.unsqueeze(1).broadcast_to([P, CHK, wq])
                dst = ly_t[:, :, 1 + w0 + c0:1 + w0 + c0 + wq]
                src = xb_t[:, :, w0 + c0:w0 + c0 + wq].bitcast(F32)
                op("dve", "tensor_tensor", dst, src, rB3, ALU.mult,
                   reads=(xb,) + rbufs[:1], writes=(ly,))
                op("dve", "tensor_tensor", dst, dst.bitcast(F32), mB3,
                   ALU.subtract, reads=rbufs[1:], writes=(ly,))

            ln_stats(half_quarters(xb_t, w0), (xb,), norm1)

        if seq_start:
            op("dve", "tensor_scalar", ly_t[:, :, 0:1], mixk_t[:].unsqueeze(2),
               0.0, None, ALU.mult, reads=(consts,), writes=(ly,))
        else:
            op("dve", "tensor_copy", ly_t[:, :, 0:1], ltail_t[:].unsqueeze(2),
               reads=(ltail,), writes=(ly,))
        op("dve", "tensor_copy", ltail_t[:].unsqueeze(2),
           ly_t[:, :, TILE:TILE + 1].bitcast(F32), reads=(ly,), writes=(ltail,))

        if seq_end:
            op("dve", "tensor_scalar", ly_t[:, :, TILE + 1:TILE + 2],
               mixk_t[:].unsqueeze(2), 0.0, None, ALU.mult,
               reads=(consts,), writes=(ly,))
        else:
            xcol_f = xb_t[:, :, TILE:TILE + 1].bitcast(F32).rearrange(
                "p c o -> p (c o)")
            sqh = fz1_t[:, 0:CHK]
            op("dve", "tensor_tensor", sqh, xcol_f, xcol_f, ALU.mult,
               reads=(xb,), writes=(aFZ1,))
            for ci in range(CHK):
                op("pe", "matmul", pp[2][0:1, 0:1], onesc_t[:].bitcast(F32),
                   xb_t[:, ci, TILE:TILE + 1].bitcast(F32),
                   start=(ci == 0), stop=(ci == CHK - 1),
                   reads=(xb, consts), writes=(ppb[2],))
            for ci in range(CHK):
                op("pe", "matmul", pp[3][0:1, 0:1], onesc_t[:].bitcast(F32),
                   sqh[:, ci:ci + 1].bitcast(F32),
                   start=(ci == 0), stop=(ci == CHK - 1),
                   reads=(aFZ1, consts), writes=(ppb[3],))
            op("dve", "tensor_scalar", r_m[:, 0:1], pp[2][0:1, 0:1], 1.0 / D,
               None, ALU.mult, reads=(ppb[2],), writes=(rows_b,))
            op("dve", "tensor_scalar", r_q[:, 0:1], pp[3][0:1, 0:1], 1.0 / D,
               None, ALU.mult, reads=(ppb[3],), writes=(rows_b,))
            op("dve", "tensor_tensor", r_s[:, 0:1], r_m[:, 0:1], r_m[:, 0:1],
               ALU.mult, reads=(rows_b,), writes=(rows_b,))
            op("dve", "tensor_tensor", r_s[:, 0:1], r_q[:, 0:1], r_s[:, 0:1],
               ALU.subtract, reads=(rows_b,), writes=(rows_b,))
            op("act", "activation", r_s[:, 0:1], r_s[:, 0:1], AF.Ln,
               bias=misc_t[0:1, 1:2], reads=(rows_b, consts), writes=(rows_b,))
            op("act", "activation", rr_rstd[:, 0:1], r_s[:, 0:1], AF.Exp,
               scale=-0.5, reads=(rows_b,), writes=(rows_b,))
            op("dve", "tensor_copy", rr_mrs[:, 0:1], r_m[:, 0:1],
               reads=(rows_b,), writes=(rows_b,))
            op("pe", "matmul", pp[3][:, 0:1], onesr_t[0:1, 0:P],
               rr_mrs[:, 0:1], start=True, stop=True,
               reads=(rows_b, consts), writes=(ppb[3],))
            op("pe", "matmul", pp[3][:, 1:2], onesr_t[0:1, 0:P],
               rr_rstd[:, 0:1], start=True, stop=True,
               reads=(rows_b, consts), writes=(ppb[3],))
            mr_sb = ar_t[:, O_BC:O_BC + 2]
            op("dve", "tensor_copy", mr_sb, pp[3][:, 0:2],
               reads=(ppb[3],), writes=(aBC,))
            op("dve", "tensor_scalar",
               ly_t[:, :, TILE + 1:TILE + 2].rearrange("p c o -> p (c o)"),
               xcol_f, mr_sb[:, 0:1], mr_sb[:, 1:2], ALU.subtract, ALU.mult,
               reads=(xb, aBC), writes=(ly,))

        tap(f"lnx{tile}", ly_t[:, :, 1:TILE + 1].bitcast(F32), ly)

        if seq_start:
            op("dve", "tensor_scalar", acar_t[:], mixk_t[:], 0.0, None,
               ALU.mult, reads=(consts,), writes=(acar,))
            op("dve", "tensor_scalar", bcar_t[:], mixk_t[:], 0.0, None,
               ALU.mult, reads=(consts,), writes=(bcar,))

        dma("gp", wb_t[0][:], chunked_w(wk_h), writes=(wb[0],))
        dma("gp", wb_t[1][:], chunked_w(wv_h), writes=(wb[1],))
        dma("gp", wb_t[2][:], chunked_w(wr_h), writes=(wb[2],))
        dma("gp", xt_t[:], chunked_w(wo_h), writes=(xt,))

        B1f = arf(O_B1, CHK, SUB)
        B2f = arf(O_B2, CHK, SUB)
        Af = arf(O_A, CHK, SUB + 1)
        Bsf = arf(O_Bs, CHK, SUB + 1)
        XKb = arbf(O_XK, CHK, SUB)
        SRb = arbf(O_SR, CHK, SUB)
        EKr = fz1_t[:, 0:CHK * SUB].rearrange("p (c s) -> p c s", s=SUB)
        EKf = EKr.bitcast(F32)
        NUr = fz2_t[:, 0:CHK * SUB].rearrange("p (c s) -> p c s", s=SUB)
        NUf = NUr.bitcast(F32)

        def xx_ap(ci, s0):
            if ci < 2:
                return ly_t[:, ci, s0:s0 + SUB].bitcast(F32)
            if ci < 4:
                return ly_t[:, ci, s0 + 2:s0 + 2 + SUB].bitcast(F32)
            return ly_t[:, ci, 1 + s0:1 + s0 + SUB].bitcast(F32)

        def mix_into(dst3, mix_t, dstbuf, s0):
            for ci in range(CHK):
                op("dve", "scalar_tensor_tensor", dst3[:, ci, :],
                   B2f[:, ci, :], mix_t[:, ci:ci + 1], xx_ap(ci, s0),
                   ALU.mult, ALU.add,
                   reads=(aB2, ly, consts), writes=(dstbuf,))

        def front(s0):
            for (a_, b_), off in zip([(0, 2), (2, 4), (4, 8)], [0, 2, 1]):
                op("dve", "tensor_tensor", B2f[:, a_:b_, :],
                   ly_t[:, a_:b_, 1 + s0:1 + s0 + SUB].bitcast(F32),
                   ly_t[:, a_:b_, off + s0:off + s0 + SUB].bitcast(F32),
                   ALU.subtract, reads=(ly,), writes=(aB2,))
            mix_into(XKb, mixk_t, aXK, s0)

        def proj(mov3, movbuf, wtile, wbuf, evac_m):
            for m in range(CHK):
                bank = m % 2
                for kc in range(CHK):
                    op("pe", "matmul", pp[bank][:, 0:SUB],
                       wtile[:, kc, m * P:(m + 1) * P], mov3[:, kc, :],
                       start=(kc == 0), stop=(kc == CHK - 1),
                       reads=(movbuf, wbuf), writes=(ppb[bank],))
                evac_m(m, bank)

        def ev_copy(dst3, dstbuf, f):
            def e(m, bank):
                op("act", "activation", dst3[:, m, :], pp[bank][:, 0:SUB],
                   f, reads=(ppb[bank],), writes=(dstbuf,))
            return e

        def ev_sig1(m, bank):
            op("act", "activation", SRb[:, m, :], pp[bank][:, 0:SUB],
               AF.Exp, scale=-1.0, reads=(ppb[bank],), writes=(aSR,))

        front(0)
        for sub in range(SUBS):
            s0 = sub * SUB

            proj(XKb, aXK, wb_t[0], wb[0], ev_copy(B1f, aB1, AF.Copy))
            mix_into(XKb, mixv_t, aXK, s0)
            proj(XKb, aXK, wb_t[1], wb[1], ev_copy(NUr, aFZ2, AF.Copy))
            mix_into(XKb, mixr_t, aXK, s0)

            proj(XKb, aXK, wb_t[2], wb[2], ev_sig1)
            op("act", "activation", SRb[:], SRb[:], AF.Ln,
               bias=misc_t[:, 0:1], reads=(consts,), writes=(aSR,))
            op("act", "activation", SRb[:], SRb[:], AF.Exp, scale=-1.0,
               writes=(aSR,))

            op("act", "activation", EKr[:], B1f[:], AF.Exp,
               reads=(aB1,), writes=(aFZ1,))
            for ci in range(CHK):
                op("act", "activation", B1f[:, ci, :], B1f[:, ci, :], AF.Exp,
                   bias=uT_t[:, ci:ci + 1], reads=(ewu,), writes=(aB1,))
            op("dve", "tensor_copy", Bsf[:, :, 0:1], bcar_t[:].unsqueeze(2),
               reads=(bcar,), writes=(aBs,))
            for ci in range(CHK):
                op("dve", "tensor_tensor_scan", Bsf[:, ci, 1:SUB + 1],
                   ew_t[:, ci:ci + 1].broadcast_to([P, SUB]), EKf[:, ci, :],
                   bcar_t[:, ci:ci + 1], ALU.mult, ALU.add,
                   reads=(ewu, aFZ1, bcar), writes=(aBs,))
            op("dve", "tensor_copy", bcar_t[:].unsqueeze(2),
               Bsf[:, :, SUB:SUB + 1], reads=(aBs,), writes=(bcar,))
            op("dve", "tensor_tensor", EKr[:], EKf[:], NUf[:], ALU.mult,
               reads=(aFZ2,), writes=(aFZ1,))
            op("dve", "tensor_tensor", NUr[:], B1f[:], NUf[:], ALU.mult,
               reads=(aB1,), writes=(aFZ2,))
            op("dve", "tensor_tensor", B1f[:], B1f[:], Bsf[:, :, 0:SUB],
               ALU.add, reads=(aBs,), writes=(aB1,))
            op("act", "activation", B1f[:], B1f[:], AF.Ln, writes=(aB1,))
            op("act", "activation", B1f[:], B1f[:], AF.Exp, scale=-1.0,
               writes=(aB1,))
            op("dve", "tensor_copy", Af[:, :, 0:1], acar_t[:].unsqueeze(2),
               reads=(acar,), writes=(aA,))
            for ci in range(CHK):
                op("dve", "tensor_tensor_scan", Af[:, ci, 1:SUB + 1],
                   ew_t[:, ci:ci + 1].broadcast_to([P, SUB]), EKf[:, ci, :],
                   acar_t[:, ci:ci + 1], ALU.mult, ALU.add,
                   reads=(ewu, aFZ1, acar), writes=(aA,))
            op("dve", "tensor_copy", acar_t[:].unsqueeze(2),
               Af[:, :, SUB:SUB + 1], reads=(aA,), writes=(acar,))
            op("dve", "tensor_tensor", NUr[:], NUf[:], Af[:, :, 0:SUB],
               ALU.add, reads=(aA,), writes=(aFZ2,))
            op("dve", "tensor_tensor", NUr[:], NUf[:], B1f[:], ALU.mult,
               reads=(aB1,), writes=(aFZ2,))
            op("dve", "tensor_tensor", XKb[:], NUf[:], SRb[:], ALU.mult,
               reads=(aSR, aFZ2), writes=(aXK,))

            proj(XKb, aXK, xt_t, xt, ev_copy(NUr, aFZ2, AF.Copy))

            if sub + 1 < SUBS:
                front(s0 + SUB)

            def norm2(qi, c0, wq, rB, mB, rbufs, s0=s0):
                rB3 = rB.unsqueeze(1).broadcast_to([P, CHK, wq])
                mB3 = mB.unsqueeze(1).broadcast_to([P, CHK, wq])
                op("dve", "tensor_tensor", B1f[:], NUf[:], rB3, ALU.mult,
                   reads=(aFZ2,) + rbufs[:1], writes=(aB1,))
                op("dve", "tensor_tensor", B1f[:], B1f[:], mB3, ALU.subtract,
                   reads=rbufs[1:], writes=(aB1,))
                xs = xb_t[:, :, s0:s0 + SUB]
                op("dve", "tensor_tensor", xs, xs.bitcast(F32), B1f[:],
                   ALU.add, reads=(aB1,), writes=(xb,))

            ln_stats([([NUr[:, ci, :] for ci in range(CHK)],
                       NUf[:], SUB, 0)], (aFZ2,), norm2)

        arena_switch(PHASE_B, PHASE_A)

        GH = arbf(O_GH, CHK, HN)
        S1a = arf2(O_S1, HN)
        GP3 = arf2(O_GP3, HN)
        GE = arf2(O_GE, D)
        GLl = arf(O_GL, CHK, E)
        GLsc = arf2(O_GL + 64, 256)
        T1 = arf(O_T1, CHK, HN)
        yv = ly_t[:, :, 0:TILE].bitcast(F32)
        yv3 = ly_t[:, :, 0:TILE]

        for nh in range(2):
            w0 = nh * HN

            def norm3(qi, c0, wq, rB, mB, rbufs, w0=w0):
                rB3 = rB.unsqueeze(1).broadcast_to([P, CHK, wq])
                mB3 = mB.unsqueeze(1).broadcast_to([P, CHK, wq])
                t1q = T1[:, :, c0:c0 + wq]
                op("dve", "tensor_tensor", t1q,
                   xb_t[:, :, w0 + c0:w0 + c0 + wq].bitcast(F32), rB3,
                   ALU.mult, reads=(xb,) + rbufs[:1],
                   writes=(aT1, aGH, aS1, aGP3, aGE))
                op("dve", "tensor_tensor", xt_t[:, :, w0 + c0:w0 + c0 + wq],
                   t1q, mB3, ALU.subtract, reads=(aT1,) + rbufs[1:],
                   writes=(xt,))

            ln_stats(half_quarters(xb_t, w0), (xb,), norm3)

        for tc in range(CHK):
            for kc in range(CHK):
                op("pe", "matmul", pp[2][:, tc * E:(tc + 1) * E],
                   xt_t[:, kc, tc * P:(tc + 1) * P], gwt_t[:, kc, :],
                   start=(kc == 0), stop=(kc == CHK - 1),
                   reads=(xt, consts), writes=(ppb[2],))
        op("dve", "tensor_tensor", GLl[:],
           pp[2][:, 0:64].rearrange("p (c e) -> p c e", e=E),
           gbB_t[:].unsqueeze(1).broadcast_to([P, CHK, E]), ALU.add,
           reads=(ppb[2], gbB_b), writes=(aGL,))
        gm1 = GLsc[:, 0:64].rearrange("p (c e) -> p c e", e=E)
        gtmp = GLsc[:, 64:128].rearrange("p (c e) -> p c e", e=E)
        gmax = GLsc[:, 128:136].rearrange("p (c e) -> p c e", e=1)
        gsum = GLsc[:, 136:144].rearrange("p (c e) -> p c e", e=1)
        gsum2 = GLsc[:, 136:144]
        op("dve", "tensor_reduce", gmax, GLl[:], AX.X, ALU.max,
           reads=(aGL,), writes=(aGL,))
        op("dve", "tensor_tensor", GLl[:], GLl[:],
           gmax.broadcast_to([P, CHK, E]), ALU.subtract, writes=(aGL,))
        op("act", "activation", arf2(O_GL, 64), arf2(O_GL, 64), AF.Exp,
           reads=(aGL,), writes=(aGL,))
        op("dve", "tensor_reduce", gsum, GLl[:], AX.X, ALU.add, writes=(aGL,))
        op("dve", "tensor_reduce", gmax, GLl[:], AX.X, ALU.max, writes=(aGL,))
        op("dve", "tensor_tensor", gm1[:], GLl[:],
           gmax.broadcast_to([P, CHK, E]), ALU.is_ge, writes=(aGL,))
        op("dve", "tensor_scalar", gm1[:], gm1[:], 1e30, None, ALU.mult,
           writes=(aGL,))
        op("dve", "tensor_tensor", gtmp[:], GLl[:], gm1[:], ALU.subtract,
           writes=(aGL,))
        op("dve", "tensor_reduce", gmax, gtmp[:], AX.X, ALU.max, writes=(aGL,))
        op("dve", "tensor_tensor", gm1[:], GLl[:],
           gmax.broadcast_to([P, CHK, E]), ALU.is_ge, writes=(aGL,))
        op("dve", "tensor_tensor", gm1[:], gm1[:], GLl[:], ALU.mult,
           writes=(aGL,))
        op("act", "activation", gsum2, gsum2, AF.Ln, writes=(aGL,))
        op("act", "activation", gsum2, gsum2, AF.Exp, scale=-1.0, writes=(aGL,))
        op("dve", "tensor_tensor", gm1[:], gm1[:],
           gsum.broadcast_to([P, CHK, E]), ALU.mult, writes=(aGL,))

        for tc in range(CHK):
            bank = 4 + (tc // 4)
            op("pe", "transpose", pp[bank][0:E, (tc % 4) * P:(tc % 4) * P + P],
               gm1[:, tc, :], ident_t[:],
               reads=(aGL, consts), writes=(ppb[bank],))
        for half in range(2):
            op("dve", "tensor_copy", gfm_t[:, half * HN:(half + 1) * HN],
               pp[4 + half][0:E, :], reads=(ppb[4 + half],), writes=(gfm_b,))

        for ex in range(NEXP):
            shared = ex >= E
            h = ex - E
            if shared:
                w1src = chunked_w(ws1_h[:, h * D:(h + 1) * D])
                w3src = chunked_w(ws3_h[:, h * D:(h + 1) * D])
                w2src = chunked_w(ws2_h[h * D:(h + 1) * D, :])
            else:
                w1src = chunked_w(w1_h[ex])
                w3src = chunked_w(w3_h[ex])
                w2src = chunked_w(w2_h[ex])
            dma("gp", wb_t[0][:], w1src, writes=(wb[0],))
            dma("gp", wb_t[1][:], w3src, writes=(wb[1],))
            dma("gp", wb_t[2][:], w2src, writes=(wb[2],))

            if not shared:
                for nh in range(2):
                    op("pe", "matmul", pp[4 + nh][:, 0:HN],
                       oh_t[:, ex * P:(ex + 1) * P],
                       gfm_t[:, nh * HN:(nh + 1) * HN], start=True, stop=True,
                       reads=(gfm_b, consts), writes=(ppb[4 + nh],))
                    op("dve", "tensor_copy", GE[:, nh * HN:(nh + 1) * HN],
                       pp[4 + nh][:, 0:HN], reads=(ppb[4 + nh],), writes=(aGE,))

            for nh in range(2):
                w0 = nh * HN
                for m in range(CHK):
                    bA = 4 + (m % 2)
                    bB_ = 6 + (m % 2)
                    for kc in range(CHK):
                        op("pe", "matmul", pp[bA][:, 0:HN],
                           wb_t[0][:, kc, m * P:(m + 1) * P],
                           xt_t[:, kc, w0:w0 + HN],
                           start=(kc == 0), stop=(kc == CHK - 1),
                           reads=(wb[0], xt), writes=(ppb[bA],))
                    for kc in range(CHK):
                        op("pe", "matmul", pp[bB_][:, 0:HN],
                           wb_t[1][:, kc, m * P:(m + 1) * P],
                           xt_t[:, kc, w0:w0 + HN],
                           start=(kc == 0), stop=(kc == CHK - 1),
                           reads=(wb[1], xt), writes=(ppb[bB_],))
                    if shared:
                        b1col = bs1_t[:, h * CHK + m:h * CHK + m + 1]
                        b3col = bs3_t[:, h * CHK + m:h * CHK + m + 1]
                    else:
                        b1col = b1_t[:, ex * CHK + m:ex * CHK + m + 1]
                        b3col = b3_t[:, ex * CHK + m:ex * CHK + m + 1]
                    op("act", "activation", S1a[:], pp[bA][:, 0:HN], AF.Silu,
                       bias=b1col, reads=(ppb[bA], consts), writes=(aS1,))
                    if shared:
                        op("dve", "tensor_scalar", GP3[:], pp[bB_][:, 0:HN],
                           b3col, None, ALU.add,
                           reads=(ppb[bB_], consts), writes=(aGP3,))
                    else:
                        op("dve", "scalar_tensor_tensor", GP3[:],
                           pp[bB_][:, 0:HN], b3col, GE[:, w0:w0 + HN],
                           ALU.add, ALU.mult,
                           reads=(ppb[bB_], consts, aGE), writes=(aGP3,))
                    op("dve", "tensor_tensor", GH[:, m, :], S1a[:], GP3[:],
                       ALU.mult, reads=(aS1, aGP3), writes=(aGH,))
                for c in range(CHK):
                    bank = c % 2
                    if ex == 0:
                        op("pe", "matmul", pp[bank][:, 0:HN],
                           b2_t[:, c * P:(c + 1) * P],
                           gfm_t[:, w0:w0 + HN], start=True, stop=False,
                           reads=(gfm_b, consts), writes=(ppb[bank],))
                        op("pe", "matmul", pp[bank][:, 0:HN],
                           bs2_t[0:1, c * P:(c + 1) * P], onesr_t[:],
                           start=False, stop=False,
                           reads=(consts,), writes=(ppb[bank],))
                    for kc in range(CHK):
                        op("pe", "matmul", pp[bank][:, 0:HN],
                           wb_t[2][:, kc, c * P:(c + 1) * P], GH[:, kc, :],
                           start=(kc == 0 and ex != 0),
                           stop=(kc == CHK - 1),
                           reads=(wb[2], aGH), writes=(ppb[bank],))
                    if ex == 0:
                        op("act", "activation", yv3[:, c, w0:w0 + HN],
                           pp[bank][:, 0:HN], AF.Copy,
                           reads=(ppb[bank],), writes=(ly,))
                    else:
                        op("dve", "tensor_tensor", yv3[:, c, w0:w0 + HN],
                           yv[:, c, w0:w0 + HN], pp[bank][:, 0:HN],
                           ALU.add, reads=(ppb[bank],), writes=(ly,))

        tap(f"y{tile}", ly_t[:, :, 0:TILE].bitcast(F32), ly)

        for nh in range(2):
            w0 = nh * HN

            def norm4(qi, c0, wq, rB, mB, rbufs, w0=w0):
                rB3 = rB.unsqueeze(1).broadcast_to([P, CHK, wq])
                mB3 = mB.unsqueeze(1).broadcast_to([P, CHK, wq])
                t1q = T1[:, :, c0:c0 + wq]
                op("dve", "tensor_tensor", t1q, yv[:, :, w0 + c0:w0 + c0 + wq],
                   rB3, ALU.mult, reads=(ly,) + rbufs[:1],
                   writes=(aT1, aGH, aS1, aGP3, aGE))
                op("dve", "tensor_tensor", t1q, t1q, mB3, ALU.subtract,
                   reads=rbufs[1:], writes=(aT1,))
                xs = xb_t[:, :, w0 + c0:w0 + c0 + wq]
                op("dve", "tensor_tensor", yv3[:, :, w0 + c0:w0 + c0 + wq],
                   xs.bitcast(F32), t1q, ALU.add, reads=(aT1, xb), writes=(ly,))

            ln_stats(half_quarters(ly_t, w0), (ly,), norm4)

        tok = dma("sp", out3[:, :, t0:t0 + TILE],
                  ly_t[:, :, 0:TILE].bitcast(F32), reads=(ly,))
        out_toks.append(tok)

        tap(f"x{tile}", xb_t[:, :, 0:TILE].bitcast(F32), xb)

    for t in out_toks:
        S.want("sp", t)
    S.emit("sp", "nop", (), {})

    S.finalize()

    import contextlib
    with contextlib.ExitStack() as stack:
        sems = {}
        for s in ENGS:
            sems[s] = stack.enter_context(nc.semaphore("sem_" + s))
        block = stack.enter_context(nc.Block())

        @block.sync
        def _(En):
            S.replay(sems, "sp", En)

        @block.gpsimd
        def _(En):
            S.replay(sems, ("gp", "gpc"), En)

        @block.tensor
        def _(En):
            S.replay(sems, "pe", En)

        @block.scalar
        def _(En):
            S.replay(sems, "act", En)

        @block.vector
        def _(En):
            S.replay(sems, "dve", En)

    for c in reversed(ctx):
        c.__exit__(None, None, None)
    return nc


def _onehot_blocks():
    oh = np.zeros((E, E * P), dtype=np.float32)
    for e in range(E):
        oh[e, e * P:(e + 1) * P] = 1.0
    return oh


def _host_prep(inputs):
    gi = lambda k: np.asarray(inputs[k])
    x = gi("L0_attn_x")

    def fm(vec):
        return np.ascontiguousarray(vec.reshape(CHK, P).T)

    com = {
        "wk": np.ascontiguousarray(gi("L0_attn_Wk").T),
        "wv": np.ascontiguousarray(gi("L0_attn_Wv").T),
        "wr": np.ascontiguousarray(gi("L0_attn_Wr").T),
        "wo": np.ascontiguousarray(gi("L0_attn_Wo").T),
        "w1": np.ascontiguousarray(np.transpose(gi("L0_ffn_w1"), (0, 2, 1))),
        "w3": np.ascontiguousarray(np.transpose(gi("L0_ffn_w3"), (0, 2, 1))),
        "w2": np.ascontiguousarray(np.transpose(gi("L0_ffn_w2"), (0, 2, 1))),
        "ws1": np.ascontiguousarray(gi("L0_ffn_ws1").T),
        "ws3": np.ascontiguousarray(gi("L0_ffn_ws3").T),
        "ws2": np.ascontiguousarray(gi("L0_ffn_ws2").T),
        "gwt": np.ascontiguousarray(gi("L0_ffn_gate_w").T),
        "mixk": fm(gi("L0_attn_mix_k")),
        "mixv": fm(gi("L0_attn_mix_v")),
        "mixr": fm(gi("L0_attn_mix_r")),
        "sd": fm(gi("L0_attn_spatial_decay")),
        "sf": fm(gi("L0_attn_spatial_first")),
        "b1f": np.ascontiguousarray(
            gi("L0_ffn_b1").reshape(E, CHK, P).transpose(2, 0, 1).reshape(P, E * CHK)),
        "b3f": np.ascontiguousarray(
            gi("L0_ffn_b3").reshape(E, CHK, P).transpose(2, 0, 1).reshape(P, E * CHK)),
        "bs1f": np.ascontiguousarray(
            gi("L0_ffn_bs1").reshape(2, CHK, P).transpose(2, 0, 1).reshape(P, 2 * CHK)),
        "bs3f": np.ascontiguousarray(
            gi("L0_ffn_bs3").reshape(2, CHK, P).transpose(2, 0, 1).reshape(P, 2 * CHK)),
        "b2": np.ascontiguousarray(gi("L0_ffn_b2")),
        "bs2": np.ascontiguousarray(gi("L0_ffn_bs2").reshape(1, D)),
        "gb": np.ascontiguousarray(gi("L0_ffn_gate_b").reshape(1, E)),
        "ident": np.eye(P, dtype=np.float32),
        "oh": _onehot_blocks(),
        "onesr": np.ones((1, HN), dtype=np.float32),
        "onesc": np.ones((P, 1), dtype=np.float32),
        "misc": np.concatenate([np.ones((P, 1), np.float32),
                                np.full((P, 1), EPS, np.float32)], axis=1),
    }
    for k in com:
        com[k] = np.ascontiguousarray(com[k].astype(np.float32, copy=False))

    in_maps = []
    for c in range(NCORES):
        xc = np.ascontiguousarray(
            np.concatenate([x[2 * c].T, x[2 * c + 1].T], axis=1)).astype(
                np.float32, copy=False)
        m = {"xin": xc}
        m.update(com)
        in_maps.append(m)
    return in_maps


def kernel(**inputs):
    in_maps = _host_prep(inputs)
    nc = build_program()
    res = run_bass_kernel_spmd(nc, in_maps, core_ids=list(range(NCORES)))
    LAST_DEBUG.clear()
    full = np.empty((NB, T, D), dtype=np.float32)
    for c in range(NCORES):
        r = res.results[c]
        for k, v in r.items():
            if k.startswith("dbg_"):
                LAST_DEBUG[(c, k)] = np.asarray(v)
        o = np.asarray(r["out"])
        full[2 * c] = o[:, 0:T].T
        full[2 * c + 1] = o[:, T:2 * T].T
    return full


# version 27
# speedup vs baseline: 1.0959x; 1.0104x over previous
import numpy as np
import concourse.bass as bass
import concourse.mybir as mybir
from concourse.bass_utils import run_bass_kernel_spmd

F32 = mybir.dt.float32
BF16 = mybir.dt.bfloat16
F32R = mybir.dt.float32r
AF = mybir.ActivationFunctionType
ALU = mybir.AluOpType
AX = mybir.AxisListType

P = 128
CHK = 8
D = 1024
T = 2048
NB = 16
NCORES = 8
SEQ = 2
TOK = SEQ * T
TILES = 4
TILE = 1024
SUBS = 4
SUB = 256
HN = 512
QW = 256
E = 8
NEXP = 10
EPS = 1e-5
INVT = 1.0 / T

DEBUG_TAPS = []
LAST_DEBUG = {}

ENGS = ("sp", "gp", "gpc", "pe", "act", "dve")
DMA_SEMS = ("sp", "gp")


class Sched:
    def __init__(self):
        self.ops = {e: [] for e in ENGS}
        self._pending = {}
        self._seq = 0

    UNSAFE_METHODS = ("tensor_tensor_scan", "scalar_tensor_tensor")

    SAME_SKIP = ("pe", "act")

    def want(self, sem, token, force=False):
        if token is None:
            return
        tsem, tidx = token
        if (tsem == sem and tsem not in DMA_SEMS and not force
                and tsem in self.SAME_SKIP):
            prod = self.ops[tsem][tidx]["m"]
            if prod not in self.UNSAFE_METHODS:
                return
        self._pending.setdefault(sem, []).append((tsem, tidx))

    def emit(self, sem, method, args, kwargs):
        w = self._pending.pop(sem, [])
        idx = len(self.ops[sem])
        self._seq += 1
        self.ops[sem].append(
            {"m": method, "a": args, "k": kwargs, "flag": False, "waits": w,
             "seq": self._seq})
        for (ts, ti) in w:
            self.ops[ts][ti]["flag"] = True
        return (sem, idx)

    def finalize(self):
        self.ms = {}
        for s in ENGS:
            c = 0
            nums = []
            for o in self.ops[s]:
                if o["flag"]:
                    c += 1
                nums.append(c)
            self.ms[s] = nums

    def replay(self, sems, semlist, engine):
        if isinstance(semlist, str):
            semlist = (semlist,)
        stream = []
        for sm in semlist:
            for o in self.ops[sm]:
                stream.append((o["seq"], sm, o))
        stream.sort()
        lastw = {}
        for _, sm, o in stream:
            for (ts, ti) in o["waits"]:
                v = self.ms[ts][ti]
                step = 16 if ts in DMA_SEMS else 1
                if lastw.get(ts, 0) >= v:
                    continue
                lastw[ts] = v
                engine.wait_ge(sems[ts], v * step)
            instr = getattr(engine, o["m"])(*o["a"], **o["k"])
            if o["flag"]:
                step = 16 if sm in DMA_SEMS else 1
                instr.then_inc(sems[sm], step)


class Buf:
    __slots__ = ("name", "last_w", "readers", "frozen")

    def __init__(self, name, frozen=False):
        self.name = name
        self.last_w = None
        self.readers = []
        self.frozen = frozen


def build_program():
    nc = bass.Bass("TRN2", target_bir_lowering=False)
    S = Sched()
    ctx = []

    def alloc(name, shape, dt):
        t = nc.sbuf_tensor(name, shape, dt)
        h = t.__enter__()
        ctx.append(t)
        return h

    def palloc(name, shape, dt):
        t = nc.psum_tensor(name, shape, dt)
        h = t.__enter__()
        ctx.append(t)
        return h

    def op(sem, method, *args, reads=(), writes=(), **kwargs):
        force = method in Sched.UNSAFE_METHODS
        for b in reads:
            S.want(sem, b.last_w, force)
        for b in writes:
            S.want(sem, b.last_w, force)
            for t in b.readers:
                S.want(sem, t, force)
        tok = S.emit(sem, method, args, kwargs)
        for b in reads:
            if not b.frozen:
                b.readers.append(tok)
        for b in writes:
            b.last_w = tok
            b.readers = []
        return tok

    def din(name, shape, dt=F32):
        return nc.dram_tensor(name, shape, dt, kind="ExternalInput").ap()

    xin = din("xin", [D, TOK])
    out = nc.dram_tensor("out", [D, TOK], F32, kind="ExternalOutput").ap()
    wk_h = din("wk", [D, D]); wv_h = din("wv", [D, D])
    wr_h = din("wr", [D, D]); wo_h = din("wo", [D, D])
    w1_h = din("w1", [E, D, D]); w3_h = din("w3", [E, D, D]); w2_h = din("w2", [E, D, D])
    ws1_h = din("ws1", [D, 2 * D]); ws3_h = din("ws3", [D, 2 * D]); ws2_h = din("ws2", [2 * D, D])
    gwt_h = din("gwt", [D, E])
    mixk_h = din("mixk", [P, CHK]); mixv_h = din("mixv", [P, CHK]); mixr_h = din("mixr", [P, CHK])
    sd_h = din("sd", [P, CHK]); sf_h = din("sf", [P, CHK])
    b1_h = din("b1f", [P, E * CHK]); b3_h = din("b3f", [P, E * CHK])
    bs1_h = din("bs1f", [P, 2 * CHK]); bs3_h = din("bs3f", [P, 2 * CHK])
    b2_h = din("b2", [E, D]); bs2_h = din("bs2", [1, D]); gb_h = din("gb", [1, E])
    ident_h = din("ident", [P, P]); oh_h = din("oh", [E, E * P])
    onesr_h = din("onesr", [1, HN]); onesc_h = din("onesc", [P, 1])
    misc_h = din("misc", [P, 2])

    dbg_out = {}
    for tag in DEBUG_TAPS:
        w = CHK * (SUB + 1) if tag.startswith("s_") else CHK * TILE
        dbg_out[tag] = nc.dram_tensor("dbg_" + tag, [P, w], F32,
                                      kind="ExternalOutput").ap()

    xb_t = alloc("xb", [P, CHK, TILE + 1], F32R)
    ly_t = alloc("ly", [P, CHK, TILE + 2], F32R)
    xt_t = alloc("xt", [P, CHK, D], BF16)
    wb_t = [alloc(f"wb{i}", [P, CHK, D], BF16) for i in range(3)]
    AR = 10256
    ar_t = alloc("arena", [P, AR], F32)
    fz1_t = alloc("fz1", [P, 2048], F32R)
    fz2_t = alloc("fz2", [P, 2048], F32R)
    rows_t = alloc("rows", [1, 1024], F32)
    gfm_t = alloc("gfm", [E, D], F32R)
    onesc_t = alloc("onescs", [P, 1], F32R)
    onesr_t = alloc("onesrs", [1, HN], F32)
    ident_t = alloc("idents", [P, P], F32)
    mixk_t = alloc("mixks", [P, CHK], F32)
    mixv_t = alloc("mixvs", [P, CHK], F32)
    mixr_t = alloc("mixrs", [P, CHK], F32)
    sd_t = alloc("sds", [P, CHK], F32)
    sf_t = alloc("sfs", [P, CHK], F32)
    ew_t = alloc("ews", [P, CHK], F32)
    uT_t = alloc("uTs", [P, CHK], F32)
    misc_t = alloc("miscs", [P, 2], F32)
    b1_t = alloc("b1s", [P, E * CHK], F32)
    b3_t = alloc("b3s", [P, E * CHK], F32)
    bs1_t = alloc("bs1s", [P, 2 * CHK], F32)
    bs3_t = alloc("bs3s", [P, 2 * CHK], F32)
    b2_t = alloc("b2s", [E, D], F32R)
    bs2_t = alloc("bs2s", [1, D], F32)
    gbB_t = alloc("gbBs", [P, E], F32)
    gbrow_t = alloc("gbrows", [1, E], F32)
    gwt_t = alloc("gwts", [P, CHK, E], BF16)
    oh_t = alloc("ohs", [E, E * P], F32R)
    acar_t = alloc("acars", [P, CHK], F32)
    bcar_t = alloc("bcars", [P, CHK], F32)
    ltail_t = alloc("ltails", [P, CHK], F32)
    pp = [palloc(f"pp{i}", [P, HN], F32) for i in range(8)]

    xb = Buf("xb"); ly = Buf("ly"); xt = Buf("xt")
    wb = [Buf(f"wb{i}") for i in range(3)]
    ppb = [Buf(f"pp{i}") for i in range(8)]
    rows_b = Buf("rows"); gfm_b = Buf("gfm")
    acar = Buf("acar"); bcar = Buf("bcar"); ltail = Buf("ltail")
    gbB_b = Buf("gbB")
    consts = Buf("consts", frozen=True)
    ewu = Buf("ewu")
    aB1 = Buf("aB1"); aB2 = Buf("aB2")
    aA = Buf("aA"); aBs = Buf("aBs"); aXK = Buf("aXK")
    aFZ1 = Buf("aFZ1"); aFZ2 = Buf("aFZ2")
    aSR = Buf("aSR")
    aGH = Buf("aGH"); aS1 = Buf("aS1"); aGP3 = Buf("aGP3"); aGE = Buf("aGE")
    aGL = Buf("aGL"); aBC2 = Buf("aBC2"); aT1 = Buf("aT1")
    aBC = aB2
    aXK2 = aXK
    PHASE_A = [aB1, aB2, aA, aBs, aXK, aSR]
    PHASE_B = [aGH, aS1, aGP3, aGE, aGL, aBC2, aT1]

    O_B1, O_B2 = 0, 2048
    O_A, O_Bs = 4096, 6152
    O_XK = 8208
    O_SR = 9232
    O_BC = 2048
    O_GH, O_S1, O_GP3, O_GE = 0, 2048, 2560, 3072
    O_GL = 4096
    O_BC2 = 4608
    O_T1 = 0

    def arena_switch(new, old):
        toks = []
        for b in old:
            if b.last_w is not None:
                toks.append(b.last_w)
            toks.extend(b.readers)
            b.last_w = None
            b.readers = []
        for b in new:
            b.last_w = None
            b.readers = list(toks)

    def arf(off, chk, w):
        return ar_t[:, off:off + chk * w].rearrange("p (c s) -> p c s", s=w)

    def arr(off, chk, w):
        return ar_t[:, off:off + chk * w].bitcast(F32R).rearrange(
            "p (c s) -> p c s", s=w)

    def arbf(off, chk, w):
        return ar_t[:, off:off + chk * w // 2].bitcast(BF16).rearrange(
            "p (c s) -> p c s", s=w)

    def arf2(off, w):
        return ar_t[:, off:off + w]

    def dma(q, out_ap, in_ap, reads=(), writes=()):
        return op(q, "dma_start", reads=reads, writes=writes,
                  out=out_ap, in_=in_ap)

    def stap(name, src3_f32, srcbuf, w):
        tag = "s_" + name
        if tag in dbg_out:
            dma("sp", dbg_out[tag][:, 0:CHK * w].rearrange(
                "p (c t) -> p c t", t=w), src3_f32, reads=(srcbuf,))

    def tap(tag, src3_f32, srcbuf):
        if tag in dbg_out:
            dma("sp", dbg_out[tag].rearrange("p (c t) -> p c t", t=TILE),
                src3_f32, reads=(srcbuf,))

    dma("sp", ident_t[:], ident_h, writes=(consts,))
    dma("sp", mixk_t[:], mixk_h, writes=(consts,))
    dma("sp", mixv_t[:], mixv_h, writes=(consts,))
    dma("sp", mixr_t[:], mixr_h, writes=(consts,))
    dma("sp", sd_t[:], sd_h, writes=(consts,))
    dma("sp", sf_t[:], sf_h, writes=(consts,))
    dma("sp", misc_t[:], misc_h, writes=(consts,))
    dma("sp", b1_t[:], b1_h, writes=(consts,))
    dma("sp", b3_t[:], b3_h, writes=(consts,))
    dma("sp", bs1_t[:], bs1_h, writes=(consts,))
    dma("sp", bs3_t[:], bs3_h, writes=(consts,))
    dma("gp", onesc_t[:], onesc_h, writes=(consts,))
    dma("sp", onesr_t[:], onesr_h, writes=(consts,))
    dma("gp", b2_t[:], b2_h, writes=(consts,))
    dma("sp", bs2_t[:], bs2_h, writes=(consts,))
    dma("sp", gbrow_t[:], gb_h, writes=(consts,))
    dma("gp", gwt_t[:], gwt_h.rearrange("(c p) e -> p c e", p=P), writes=(consts,))
    dma("gp", oh_t[:], oh_h, writes=(consts,))

    op("act", "activation", ew_t[:], sd_t[:], AF.Exp, scale=INVT,
       reads=(consts,), writes=(ewu,))
    op("dve", "tensor_scalar", uT_t[:], sf_t[:], INVT, None, ALU.mult,
       reads=(consts,), writes=(ewu,))
    op("pe", "matmul", pp[3][0:P, 0:E], onesr_t[0:1, 0:P], gbrow_t[:],
       start=True, stop=True, reads=(consts,), writes=(ppb[3],))
    op("dve", "tensor_copy", gbB_t[:], pp[3][0:P, 0:E],
       reads=(ppb[3],), writes=(gbB_b,))

    xin3 = xin.rearrange("(c p) t -> p c t", p=P)
    out3 = out.rearrange("(c p) t -> p c t", p=P)

    def chunked_w(h2d):
        return h2d.rearrange("(c p) n -> p c n", p=P)

    r_m = rows_t[0:1, 0:QW]
    r_q = rows_t[0:1, QW:2 * QW]
    r_s = rows_t[0:1, 2 * QW:3 * QW]
    rr_rstd = rows_t[0:1, 3 * QW:4 * QW]
    rr_mrs = r_s

    def ln_stats(quarters, dat_bufs, norm):
        def banks(qi):
            return ((pp[2], 2, pp[3], 3) if qi % 2 == 0
                    else (pp[6], 6, pp[7], 7))

        def stats(qi):
            chunks, f3d, wq, c0 = quarters[qi]
            ppS, ppSi, ppB_, ppBi = banks(qi)
            sq3 = fz1_t[:, 0:CHK * wq].rearrange("p (c s) -> p c s", s=wq)
            op("dve", "tensor_tensor", sq3, f3d, f3d, ALU.mult,
               reads=tuple(dat_bufs), writes=(aFZ1,))
            for ci in range(CHK):
                op("pe", "matmul", ppS[0:1, 0:wq], onesc_t[:], chunks[ci],
                   start=(ci == 0), stop=(ci == CHK - 1),
                   reads=tuple(dat_bufs) + (consts,), writes=(ppb[ppSi],))
            for ci in range(CHK):
                op("pe", "matmul", ppB_[0:1, 0:wq], onesc_t[:],
                   sq3[:, ci, :], start=(ci == 0), stop=(ci == CHK - 1),
                   reads=(aFZ1, consts), writes=(ppb[ppBi],))

        def tail(qi):
            chunks, f3d, wq, c0 = quarters[qi]
            ppS, ppSi, ppB_, ppBi = banks(qi)
            op("dve", "tensor_scalar", r_m[:, 0:wq], ppS[0:1, 0:wq],
               1.0 / D, None, ALU.mult, reads=(ppb[ppSi],), writes=(rows_b,))
            op("dve", "tensor_scalar", r_q[:, 0:wq], ppB_[0:1, 0:wq],
               1.0 / D, None, ALU.mult, reads=(ppb[ppBi],), writes=(rows_b,))
            op("dve", "tensor_tensor", r_s[:, 0:wq], r_m[:, 0:wq], r_m[:, 0:wq],
               ALU.mult, reads=(rows_b,), writes=(rows_b,))
            op("dve", "tensor_tensor", r_s[:, 0:wq], r_q[:, 0:wq], r_s[:, 0:wq],
               ALU.subtract, reads=(rows_b,), writes=(rows_b,))
            op("act", "activation", r_s[:, 0:wq], r_s[:, 0:wq], AF.Ln,
               bias=misc_t[0:1, 1:2], reads=(rows_b, consts), writes=(rows_b,))
            op("act", "activation", rr_rstd[:, 0:wq], r_s[:, 0:wq], AF.Exp,
               scale=-0.5, reads=(rows_b,), writes=(rows_b,))
            op("dve", "tensor_tensor", rr_mrs[:, 0:wq], r_m[:, 0:wq],
               rr_rstd[:, 0:wq], ALU.mult, reads=(rows_b,), writes=(rows_b,))
            op("pe", "matmul", ppS[:, 0:wq], onesr_t[0:1, 0:P],
               rr_rstd[:, 0:wq], start=True, stop=True,
               reads=(rows_b, consts), writes=(ppb[ppSi],))
            op("pe", "matmul", ppB_[:, 0:wq], onesr_t[0:1, 0:P],
               rr_mrs[:, 0:wq], start=True, stop=True,
               reads=(rows_b, consts), writes=(ppb[ppBi],))
            norm(qi, c0, wq, ppS[:, 0:wq], ppB_[:, 0:wq],
                 (ppb[ppSi], ppb[ppBi]))

        for qi in range(len(quarters)):
            stats(qi)
            tail(qi)

    def half_quarters(src_t, base):
        qs = []
        for q in range(2):
            c0 = base + q * QW
            qs.append(([src_t[:, ci, c0:c0 + QW] for ci in range(CHK)],
                       src_t[:, :, c0:c0 + QW].bitcast(F32), QW, q * QW))
        return qs

    out_toks = []
    for tile in range(TILES):
        t0 = tile * TILE
        seq_start = (tile % 2 == 0)
        seq_end = (tile % 2 == 1)

        ncols = TILE + (0 if seq_end else 1)
        dma("gp", xb_t[:, :, 0:ncols], xin3[:, :, t0:t0 + ncols], writes=(xb,))

        arena_switch(PHASE_A, PHASE_B)

        for nh in range(2):
            w0 = nh * HN

            def norm1(qi, c0, wq, rB, mB, rbufs, w0=w0):
                rB3 = rB.unsqueeze(1).broadcast_to([P, CHK, wq])
                mB3 = mB.unsqueeze(1).broadcast_to([P, CHK, wq])
                dst = ly_t[:, :, 1 + w0 + c0:1 + w0 + c0 + wq]
                src = xb_t[:, :, w0 + c0:w0 + c0 + wq].bitcast(F32)
                op("dve", "tensor_tensor", dst, src, rB3, ALU.mult,
                   reads=(xb,) + rbufs[:1], writes=(ly,))
                op("dve", "tensor_tensor", dst, dst.bitcast(F32), mB3,
                   ALU.subtract, reads=rbufs[1:], writes=(ly,))

            ln_stats(half_quarters(xb_t, w0), (xb,), norm1)

        if seq_start:
            op("dve", "tensor_scalar", ly_t[:, :, 0:1], mixk_t[:].unsqueeze(2),
               0.0, None, ALU.mult, reads=(consts,), writes=(ly,))
        else:
            op("dve", "tensor_copy", ly_t[:, :, 0:1], ltail_t[:].unsqueeze(2),
               reads=(ltail,), writes=(ly,))
        op("dve", "tensor_copy", ltail_t[:].unsqueeze(2),
           ly_t[:, :, TILE:TILE + 1].bitcast(F32), reads=(ly,), writes=(ltail,))

        if seq_end:
            op("dve", "tensor_scalar", ly_t[:, :, TILE + 1:TILE + 2],
               mixk_t[:].unsqueeze(2), 0.0, None, ALU.mult,
               reads=(consts,), writes=(ly,))
        else:
            xcol_f = xb_t[:, :, TILE:TILE + 1].bitcast(F32).rearrange(
                "p c o -> p (c o)")
            sqh = fz1_t[:, 0:CHK]
            op("dve", "tensor_tensor", sqh, xcol_f, xcol_f, ALU.mult,
               reads=(xb,), writes=(aFZ1,))
            for ci in range(CHK):
                op("pe", "matmul", pp[2][0:1, 0:1], onesc_t[:].bitcast(F32),
                   xb_t[:, ci, TILE:TILE + 1].bitcast(F32),
                   start=(ci == 0), stop=(ci == CHK - 1),
                   reads=(xb, consts), writes=(ppb[2],))
            for ci in range(CHK):
                op("pe", "matmul", pp[3][0:1, 0:1], onesc_t[:].bitcast(F32),
                   sqh[:, ci:ci + 1].bitcast(F32),
                   start=(ci == 0), stop=(ci == CHK - 1),
                   reads=(aFZ1, consts), writes=(ppb[3],))
            op("dve", "tensor_scalar", r_m[:, 0:1], pp[2][0:1, 0:1], 1.0 / D,
               None, ALU.mult, reads=(ppb[2],), writes=(rows_b,))
            op("dve", "tensor_scalar", r_q[:, 0:1], pp[3][0:1, 0:1], 1.0 / D,
               None, ALU.mult, reads=(ppb[3],), writes=(rows_b,))
            op("dve", "tensor_tensor", r_s[:, 0:1], r_m[:, 0:1], r_m[:, 0:1],
               ALU.mult, reads=(rows_b,), writes=(rows_b,))
            op("dve", "tensor_tensor", r_s[:, 0:1], r_q[:, 0:1], r_s[:, 0:1],
               ALU.subtract, reads=(rows_b,), writes=(rows_b,))
            op("act", "activation", r_s[:, 0:1], r_s[:, 0:1], AF.Ln,
               bias=misc_t[0:1, 1:2], reads=(rows_b, consts), writes=(rows_b,))
            op("act", "activation", rr_rstd[:, 0:1], r_s[:, 0:1], AF.Exp,
               scale=-0.5, reads=(rows_b,), writes=(rows_b,))
            op("dve", "tensor_copy", rr_mrs[:, 0:1], r_m[:, 0:1],
               reads=(rows_b,), writes=(rows_b,))
            op("pe", "matmul", pp[3][:, 0:1], onesr_t[0:1, 0:P],
               rr_mrs[:, 0:1], start=True, stop=True,
               reads=(rows_b, consts), writes=(ppb[3],))
            op("pe", "matmul", pp[3][:, 1:2], onesr_t[0:1, 0:P],
               rr_rstd[:, 0:1], start=True, stop=True,
               reads=(rows_b, consts), writes=(ppb[3],))
            mr_sb = ar_t[:, O_BC:O_BC + 2]
            op("dve", "tensor_copy", mr_sb, pp[3][:, 0:2],
               reads=(ppb[3],), writes=(aBC,))
            op("dve", "tensor_scalar",
               ly_t[:, :, TILE + 1:TILE + 2].rearrange("p c o -> p (c o)"),
               xcol_f, mr_sb[:, 0:1], mr_sb[:, 1:2], ALU.subtract, ALU.mult,
               reads=(xb, aBC), writes=(ly,))

        tap(f"lnx{tile}", ly_t[:, :, 1:TILE + 1].bitcast(F32), ly)

        if seq_start:
            op("dve", "tensor_scalar", acar_t[:], mixk_t[:], 0.0, None,
               ALU.mult, reads=(consts,), writes=(acar,))
            op("dve", "tensor_scalar", bcar_t[:], mixk_t[:], 0.0, None,
               ALU.mult, reads=(consts,), writes=(bcar,))

        dma("gp", wb_t[0][:], chunked_w(wk_h), writes=(wb[0],))
        dma("gp", wb_t[1][:], chunked_w(wv_h), writes=(wb[1],))
        dma("gp", wb_t[2][:], chunked_w(wr_h), writes=(wb[2],))
        dma("gp", xt_t[:], chunked_w(wo_h), writes=(xt,))

        B1f = arf(O_B1, CHK, SUB)
        B2f = arf(O_B2, CHK, SUB)
        Af = arf(O_A, CHK, SUB + 1)
        Bsf = arf(O_Bs, CHK, SUB + 1)
        XKb = arbf(O_XK, CHK, SUB)
        SRb = arbf(O_SR, CHK, SUB)
        EKr = fz1_t[:, 0:CHK * SUB].rearrange("p (c s) -> p c s", s=SUB)
        EKf = EKr.bitcast(F32)
        NUr = fz2_t[:, 0:CHK * SUB].rearrange("p (c s) -> p c s", s=SUB)
        NUf = NUr.bitcast(F32)

        def xx_ap(ci, s0):
            if ci < 2:
                return ly_t[:, ci, s0:s0 + SUB].bitcast(F32)
            if ci < 4:
                return ly_t[:, ci, s0 + 2:s0 + 2 + SUB].bitcast(F32)
            return ly_t[:, ci, 1 + s0:1 + s0 + SUB].bitcast(F32)

        def mix_into(dst3, mix_t, dstbuf, s0):
            for ci in range(CHK):
                op("dve", "scalar_tensor_tensor", dst3[:, ci, :],
                   B2f[:, ci, :], mix_t[:, ci:ci + 1], xx_ap(ci, s0),
                   ALU.mult, ALU.add,
                   reads=(aB2, ly, consts), writes=(dstbuf,))

        def front(s0):
            for (a_, b_), off in zip([(0, 2), (2, 4), (4, 8)], [0, 2, 1]):
                op("dve", "tensor_tensor", B2f[:, a_:b_, :],
                   ly_t[:, a_:b_, 1 + s0:1 + s0 + SUB].bitcast(F32),
                   ly_t[:, a_:b_, off + s0:off + s0 + SUB].bitcast(F32),
                   ALU.subtract, reads=(ly,), writes=(aB2,))
            mix_into(XKb, mixk_t, aXK, s0)

        def proj(mov3, movbuf, wtile, wbuf, evac_m):
            for m in range(CHK):
                bank = m % 2
                for kc in range(CHK):
                    op("pe", "matmul", pp[bank][:, 0:SUB],
                       wtile[:, kc, m * P:(m + 1) * P], mov3[:, kc, :],
                       start=(kc == 0), stop=(kc == CHK - 1),
                       reads=(movbuf, wbuf), writes=(ppb[bank],))
                evac_m(m, bank)

        def ev_copy(dst3, dstbuf, f):
            def e(m, bank):
                op("act", "activation", dst3[:, m, :], pp[bank][:, 0:SUB],
                   f, reads=(ppb[bank],), writes=(dstbuf,))
            return e

        def ev_sig1(m, bank):
            op("act", "activation", SRb[:, m, :], pp[bank][:, 0:SUB],
               AF.Exp, scale=-1.0, reads=(ppb[bank],), writes=(aSR,))

        front(0)
        for sub in range(SUBS):
            s0 = sub * SUB

            proj(XKb, aXK, wb_t[0], wb[0], ev_copy(B1f, aB1, AF.Copy))
            mix_into(XKb, mixv_t, aXK, s0)
            proj(XKb, aXK, wb_t[1], wb[1], ev_copy(NUr, aFZ2, AF.Copy))
            mix_into(XKb, mixr_t, aXK, s0)

            proj(XKb, aXK, wb_t[2], wb[2], ev_sig1)
            op("act", "activation", SRb[:], SRb[:], AF.Ln,
               bias=misc_t[:, 0:1], reads=(consts,), writes=(aSR,))
            op("act", "activation", SRb[:], SRb[:], AF.Exp, scale=-1.0,
               writes=(aSR,))

            op("act", "activation", EKr[:], B1f[:], AF.Exp,
               reads=(aB1,), writes=(aFZ1,))
            for ci in range(CHK):
                op("act", "activation", B1f[:, ci, :], B1f[:, ci, :], AF.Exp,
                   bias=uT_t[:, ci:ci + 1], reads=(ewu,), writes=(aB1,))
            op("dve", "tensor_copy", Bsf[:, :, 0:1], bcar_t[:].unsqueeze(2),
               reads=(bcar,), writes=(aBs,))
            for ci in range(CHK):
                op("dve", "tensor_tensor_scan", Bsf[:, ci, 1:SUB + 1],
                   ew_t[:, ci:ci + 1].broadcast_to([P, SUB]), EKf[:, ci, :],
                   bcar_t[:, ci:ci + 1], ALU.mult, ALU.add,
                   reads=(ewu, aFZ1, bcar), writes=(aBs,))
            op("dve", "tensor_copy", bcar_t[:].unsqueeze(2),
               Bsf[:, :, SUB:SUB + 1], reads=(aBs,), writes=(bcar,))
            op("dve", "tensor_tensor", EKr[:], EKf[:], NUf[:], ALU.mult,
               reads=(aFZ2,), writes=(aFZ1,))
            op("dve", "tensor_tensor", NUr[:], B1f[:], NUf[:], ALU.mult,
               reads=(aB1,), writes=(aFZ2,))
            op("dve", "tensor_tensor", B1f[:], B1f[:], Bsf[:, :, 0:SUB],
               ALU.add, reads=(aBs,), writes=(aB1,))
            op("act", "activation", B1f[:], B1f[:], AF.Ln, writes=(aB1,))
            op("act", "activation", B1f[:], B1f[:], AF.Exp, scale=-1.0,
               writes=(aB1,))
            op("dve", "tensor_copy", Af[:, :, 0:1], acar_t[:].unsqueeze(2),
               reads=(acar,), writes=(aA,))
            for ci in range(CHK):
                op("dve", "tensor_tensor_scan", Af[:, ci, 1:SUB + 1],
                   ew_t[:, ci:ci + 1].broadcast_to([P, SUB]), EKf[:, ci, :],
                   acar_t[:, ci:ci + 1], ALU.mult, ALU.add,
                   reads=(ewu, aFZ1, acar), writes=(aA,))
            op("dve", "tensor_copy", acar_t[:].unsqueeze(2),
               Af[:, :, SUB:SUB + 1], reads=(aA,), writes=(acar,))
            op("dve", "tensor_tensor", NUr[:], NUf[:], Af[:, :, 0:SUB],
               ALU.add, reads=(aA,), writes=(aFZ2,))
            op("dve", "tensor_tensor", NUr[:], NUf[:], B1f[:], ALU.mult,
               reads=(aB1,), writes=(aFZ2,))
            op("dve", "tensor_tensor", XKb[:], NUf[:], SRb[:], ALU.mult,
               reads=(aSR, aFZ2), writes=(aXK,))

            proj(XKb, aXK, xt_t, xt, ev_copy(NUr, aFZ2, AF.Copy))

            if sub + 1 < SUBS:
                front(s0 + SUB)

            def norm2(qi, c0, wq, rB, mB, rbufs, s0=s0):
                rB3 = rB.unsqueeze(1).broadcast_to([P, CHK, wq])
                mB3 = mB.unsqueeze(1).broadcast_to([P, CHK, wq])
                op("dve", "tensor_tensor", B1f[:], NUf[:], rB3, ALU.mult,
                   reads=(aFZ2,) + rbufs[:1], writes=(aB1,))
                op("dve", "tensor_tensor", B1f[:], B1f[:], mB3, ALU.subtract,
                   reads=rbufs[1:], writes=(aB1,))
                xs = xb_t[:, :, s0:s0 + SUB]
                op("dve", "tensor_tensor", xs, xs.bitcast(F32), B1f[:],
                   ALU.add, reads=(aB1,), writes=(xb,))

            ln_stats([([NUr[:, ci, :] for ci in range(CHK)],
                       NUf[:], SUB, 0)], (aFZ2,), norm2)

        arena_switch(PHASE_B, PHASE_A)

        GH = arbf(O_GH, CHK, HN)
        S1a = arf2(O_S1, HN)
        GP3 = arf2(O_GP3, HN)
        GE = arf2(O_GE, D)
        GLl = arf(O_GL, CHK, E)
        GLsc = arf2(O_GL + 64, 256)
        T1 = arf(O_T1, CHK, HN)
        yv = ly_t[:, :, 0:TILE].bitcast(F32)
        yv3 = ly_t[:, :, 0:TILE]

        for nh in range(2):
            w0 = nh * HN

            def norm3(qi, c0, wq, rB, mB, rbufs, w0=w0):
                rB3 = rB.unsqueeze(1).broadcast_to([P, CHK, wq])
                mB3 = mB.unsqueeze(1).broadcast_to([P, CHK, wq])
                t1q = T1[:, :, c0:c0 + wq]
                op("dve", "tensor_tensor", t1q,
                   xb_t[:, :, w0 + c0:w0 + c0 + wq].bitcast(F32), rB3,
                   ALU.mult, reads=(xb,) + rbufs[:1],
                   writes=(aT1, aGH, aS1, aGP3, aGE))
                op("dve", "tensor_tensor", xt_t[:, :, w0 + c0:w0 + c0 + wq],
                   t1q, mB3, ALU.subtract, reads=(aT1,) + rbufs[1:],
                   writes=(xt,))

            ln_stats(half_quarters(xb_t, w0), (xb,), norm3)

        for tc in range(CHK):
            for kc in range(CHK):
                op("pe", "matmul", pp[2][:, tc * E:(tc + 1) * E],
                   xt_t[:, kc, tc * P:(tc + 1) * P], gwt_t[:, kc, :],
                   start=(kc == 0), stop=(kc == CHK - 1),
                   reads=(xt, consts), writes=(ppb[2],))
        op("dve", "tensor_tensor", GLl[:],
           pp[2][:, 0:64].rearrange("p (c e) -> p c e", e=E),
           gbB_t[:].unsqueeze(1).broadcast_to([P, CHK, E]), ALU.add,
           reads=(ppb[2], gbB_b), writes=(aGL,))
        gm1 = GLsc[:, 0:64].rearrange("p (c e) -> p c e", e=E)
        gtmp = GLsc[:, 64:128].rearrange("p (c e) -> p c e", e=E)
        gmax = GLsc[:, 128:136].rearrange("p (c e) -> p c e", e=1)
        gsum = GLsc[:, 136:144].rearrange("p (c e) -> p c e", e=1)
        gsum2 = GLsc[:, 136:144]
        op("dve", "tensor_reduce", gmax, GLl[:], AX.X, ALU.max,
           reads=(aGL,), writes=(aGL,))
        op("dve", "tensor_tensor", GLl[:], GLl[:],
           gmax.broadcast_to([P, CHK, E]), ALU.subtract, writes=(aGL,))
        op("act", "activation", arf2(O_GL, 64), arf2(O_GL, 64), AF.Exp,
           reads=(aGL,), writes=(aGL,))
        op("dve", "tensor_reduce", gsum, GLl[:], AX.X, ALU.add, writes=(aGL,))
        op("dve", "tensor_reduce", gmax, GLl[:], AX.X, ALU.max, writes=(aGL,))
        op("dve", "tensor_tensor", gm1[:], GLl[:],
           gmax.broadcast_to([P, CHK, E]), ALU.is_ge, writes=(aGL,))
        op("dve", "tensor_scalar", gm1[:], gm1[:], 1e30, None, ALU.mult,
           writes=(aGL,))
        op("dve", "tensor_tensor", gtmp[:], GLl[:], gm1[:], ALU.subtract,
           writes=(aGL,))
        op("dve", "tensor_reduce", gmax, gtmp[:], AX.X, ALU.max, writes=(aGL,))
        op("dve", "tensor_tensor", gm1[:], GLl[:],
           gmax.broadcast_to([P, CHK, E]), ALU.is_ge, writes=(aGL,))
        op("dve", "tensor_tensor", gm1[:], gm1[:], GLl[:], ALU.mult,
           writes=(aGL,))
        op("act", "activation", gsum2, gsum2, AF.Ln, writes=(aGL,))
        op("act", "activation", gsum2, gsum2, AF.Exp, scale=-1.0, writes=(aGL,))
        op("dve", "tensor_tensor", gm1[:], gm1[:],
           gsum.broadcast_to([P, CHK, E]), ALU.mult, writes=(aGL,))

        for tc in range(CHK):
            bank = 4 + (tc // 4)
            op("pe", "transpose", pp[bank][0:E, (tc % 4) * P:(tc % 4) * P + P],
               gm1[:, tc, :], ident_t[:],
               reads=(aGL, consts), writes=(ppb[bank],))
        for half in range(2):
            op("dve", "tensor_copy", gfm_t[:, half * HN:(half + 1) * HN],
               pp[4 + half][0:E, :], reads=(ppb[4 + half],), writes=(gfm_b,))

        for ex in range(NEXP):
            shared = ex >= E
            h = ex - E
            if shared:
                w1src = chunked_w(ws1_h[:, h * D:(h + 1) * D])
                w3src = chunked_w(ws3_h[:, h * D:(h + 1) * D])
                w2src = chunked_w(ws2_h[h * D:(h + 1) * D, :])
            else:
                w1src = chunked_w(w1_h[ex])
                w3src = chunked_w(w3_h[ex])
                w2src = chunked_w(w2_h[ex])
            dma("gp", wb_t[0][:], w1src, writes=(wb[0],))
            dma("gp", wb_t[1][:], w3src, writes=(wb[1],))
            dma("gp", wb_t[2][:], w2src, writes=(wb[2],))

            if not shared:
                for nh in range(2):
                    op("pe", "matmul", pp[4 + nh][:, 0:HN],
                       oh_t[:, ex * P:(ex + 1) * P],
                       gfm_t[:, nh * HN:(nh + 1) * HN], start=True, stop=True,
                       reads=(gfm_b, consts), writes=(ppb[4 + nh],))
                    op("dve", "tensor_copy", GE[:, nh * HN:(nh + 1) * HN],
                       pp[4 + nh][:, 0:HN], reads=(ppb[4 + nh],), writes=(aGE,))

            for nh in range(2):
                w0 = nh * HN
                for m in range(CHK):
                    bA = 4 + (m % 2)
                    bB_ = 6 + (m % 2)
                    for kc in range(CHK):
                        op("pe", "matmul", pp[bA][:, 0:HN],
                           wb_t[0][:, kc, m * P:(m + 1) * P],
                           xt_t[:, kc, w0:w0 + HN],
                           start=(kc == 0), stop=(kc == CHK - 1),
                           reads=(wb[0], xt), writes=(ppb[bA],))
                    for kc in range(CHK):
                        op("pe", "matmul", pp[bB_][:, 0:HN],
                           wb_t[1][:, kc, m * P:(m + 1) * P],
                           xt_t[:, kc, w0:w0 + HN],
                           start=(kc == 0), stop=(kc == CHK - 1),
                           reads=(wb[1], xt), writes=(ppb[bB_],))
                    if shared:
                        b1col = bs1_t[:, h * CHK + m:h * CHK + m + 1]
                        b3col = bs3_t[:, h * CHK + m:h * CHK + m + 1]
                    else:
                        b1col = b1_t[:, ex * CHK + m:ex * CHK + m + 1]
                        b3col = b3_t[:, ex * CHK + m:ex * CHK + m + 1]
                    op("act", "activation", S1a[:], pp[bA][:, 0:HN], AF.Silu,
                       bias=b1col, reads=(ppb[bA], consts), writes=(aS1,))
                    if shared:
                        op("dve", "tensor_scalar", GP3[:], pp[bB_][:, 0:HN],
                           b3col, None, ALU.add,
                           reads=(ppb[bB_], consts), writes=(aGP3,))
                    else:
                        op("dve", "scalar_tensor_tensor", GP3[:],
                           pp[bB_][:, 0:HN], b3col, GE[:, w0:w0 + HN],
                           ALU.add, ALU.mult,
                           reads=(ppb[bB_], consts, aGE), writes=(aGP3,))
                    op("dve", "tensor_tensor", GH[:, m, :], S1a[:], GP3[:],
                       ALU.mult, reads=(aS1, aGP3), writes=(aGH,))
                for c in range(CHK):
                    bank = c % 2
                    if ex == 0:
                        op("pe", "matmul", pp[bank][:, 0:HN],
                           b2_t[:, c * P:(c + 1) * P],
                           gfm_t[:, w0:w0 + HN], start=True, stop=False,
                           reads=(gfm_b, consts), writes=(ppb[bank],))
                        op("pe", "matmul", pp[bank][:, 0:HN],
                           bs2_t[0:1, c * P:(c + 1) * P], onesr_t[:],
                           start=False, stop=False,
                           reads=(consts,), writes=(ppb[bank],))
                    for kc in range(CHK):
                        op("pe", "matmul", pp[bank][:, 0:HN],
                           wb_t[2][:, kc, c * P:(c + 1) * P], GH[:, kc, :],
                           start=(kc == 0 and ex != 0),
                           stop=(kc == CHK - 1),
                           reads=(wb[2], aGH), writes=(ppb[bank],))
                    if ex == 0:
                        op("act", "activation", yv3[:, c, w0:w0 + HN],
                           pp[bank][:, 0:HN], AF.Copy,
                           reads=(ppb[bank],), writes=(ly,))
                    else:
                        op("dve", "tensor_tensor", yv3[:, c, w0:w0 + HN],
                           yv[:, c, w0:w0 + HN], pp[bank][:, 0:HN],
                           ALU.add, reads=(ppb[bank],), writes=(ly,))

        tap(f"y{tile}", ly_t[:, :, 0:TILE].bitcast(F32), ly)

        for nh in range(2):
            w0 = nh * HN

            def norm4(qi, c0, wq, rB, mB, rbufs, w0=w0):
                rB3 = rB.unsqueeze(1).broadcast_to([P, CHK, wq])
                mB3 = mB.unsqueeze(1).broadcast_to([P, CHK, wq])
                t1q = T1[:, :, c0:c0 + wq]
                op("dve", "tensor_tensor", t1q, yv[:, :, w0 + c0:w0 + c0 + wq],
                   rB3, ALU.mult, reads=(ly,) + rbufs[:1],
                   writes=(aT1, aGH, aS1, aGP3, aGE))
                op("dve", "tensor_tensor", t1q, t1q, mB3, ALU.subtract,
                   reads=rbufs[1:], writes=(aT1,))
                xs = xb_t[:, :, w0 + c0:w0 + c0 + wq]
                op("dve", "tensor_tensor", yv3[:, :, w0 + c0:w0 + c0 + wq],
                   xs.bitcast(F32), t1q, ALU.add, reads=(aT1, xb), writes=(ly,))

            ln_stats(half_quarters(ly_t, w0), (ly,), norm4)

        tok = dma("sp", out3[:, :, t0:t0 + TILE],
                  ly_t[:, :, 0:TILE].bitcast(F32), reads=(ly,))
        out_toks.append(tok)

        tap(f"x{tile}", xb_t[:, :, 0:TILE].bitcast(F32), xb)

    for t in out_toks:
        S.want("sp", t)
    S.emit("sp", "nop", (), {})

    S.finalize()

    import contextlib
    with contextlib.ExitStack() as stack:
        sems = {}
        for s in ENGS:
            sems[s] = stack.enter_context(nc.semaphore("sem_" + s))
        block = stack.enter_context(nc.Block())

        @block.sync
        def _(En):
            S.replay(sems, "sp", En)

        @block.gpsimd
        def _(En):
            S.replay(sems, ("gp", "gpc"), En)

        @block.tensor
        def _(En):
            S.replay(sems, "pe", En)

        @block.scalar
        def _(En):
            S.replay(sems, "act", En)

        @block.vector
        def _(En):
            S.replay(sems, "dve", En)

    for c in reversed(ctx):
        c.__exit__(None, None, None)
    return nc


def _onehot_blocks():
    oh = np.zeros((E, E * P), dtype=np.float32)
    for e in range(E):
        oh[e, e * P:(e + 1) * P] = 1.0
    return oh


def _host_prep(inputs):
    gi = lambda k: np.asarray(inputs[k])
    x = gi("L0_attn_x")

    def fm(vec):
        return np.ascontiguousarray(vec.reshape(CHK, P).T)

    com = {
        "wk": np.ascontiguousarray(gi("L0_attn_Wk").T),
        "wv": np.ascontiguousarray(gi("L0_attn_Wv").T),
        "wr": np.ascontiguousarray(gi("L0_attn_Wr").T),
        "wo": np.ascontiguousarray(gi("L0_attn_Wo").T),
        "w1": np.ascontiguousarray(np.transpose(gi("L0_ffn_w1"), (0, 2, 1))),
        "w3": np.ascontiguousarray(np.transpose(gi("L0_ffn_w3"), (0, 2, 1))),
        "w2": np.ascontiguousarray(np.transpose(gi("L0_ffn_w2"), (0, 2, 1))),
        "ws1": np.ascontiguousarray(gi("L0_ffn_ws1").T),
        "ws3": np.ascontiguousarray(gi("L0_ffn_ws3").T),
        "ws2": np.ascontiguousarray(gi("L0_ffn_ws2").T),
        "gwt": np.ascontiguousarray(gi("L0_ffn_gate_w").T),
        "mixk": fm(gi("L0_attn_mix_k")),
        "mixv": fm(gi("L0_attn_mix_v")),
        "mixr": fm(gi("L0_attn_mix_r")),
        "sd": fm(gi("L0_attn_spatial_decay")),
        "sf": fm(gi("L0_attn_spatial_first")),
        "b1f": np.ascontiguousarray(
            gi("L0_ffn_b1").reshape(E, CHK, P).transpose(2, 0, 1).reshape(P, E * CHK)),
        "b3f": np.ascontiguousarray(
            gi("L0_ffn_b3").reshape(E, CHK, P).transpose(2, 0, 1).reshape(P, E * CHK)),
        "bs1f": np.ascontiguousarray(
            gi("L0_ffn_bs1").reshape(2, CHK, P).transpose(2, 0, 1).reshape(P, 2 * CHK)),
        "bs3f": np.ascontiguousarray(
            gi("L0_ffn_bs3").reshape(2, CHK, P).transpose(2, 0, 1).reshape(P, 2 * CHK)),
        "b2": np.ascontiguousarray(gi("L0_ffn_b2")),
        "bs2": np.ascontiguousarray(gi("L0_ffn_bs2").reshape(1, D)),
        "gb": np.ascontiguousarray(gi("L0_ffn_gate_b").reshape(1, E)),
        "ident": np.eye(P, dtype=np.float32),
        "oh": _onehot_blocks(),
        "onesr": np.ones((1, HN), dtype=np.float32),
        "onesc": np.ones((P, 1), dtype=np.float32),
        "misc": np.concatenate([np.ones((P, 1), np.float32),
                                np.full((P, 1), EPS, np.float32)], axis=1),
    }
    for k in com:
        com[k] = np.ascontiguousarray(com[k].astype(np.float32, copy=False))

    in_maps = []
    for c in range(NCORES):
        xc = np.ascontiguousarray(
            np.concatenate([x[2 * c].T, x[2 * c + 1].T], axis=1)).astype(
                np.float32, copy=False)
        m = {"xin": xc}
        m.update(com)
        in_maps.append(m)
    return in_maps


def kernel(**inputs):
    in_maps = _host_prep(inputs)
    nc = build_program()
    res = run_bass_kernel_spmd(nc, in_maps, core_ids=list(range(NCORES)))
    LAST_DEBUG.clear()
    full = np.empty((NB, T, D), dtype=np.float32)
    for c in range(NCORES):
        r = res.results[c]
        for k, v in r.items():
            if k.startswith("dbg_"):
                LAST_DEBUG[(c, k)] = np.asarray(v)
        o = np.asarray(r["out"])
        full[2 * c] = o[:, 0:T].T
        full[2 * c + 1] = o[:, T:2 * T].T
    return full
